# version 07
# baseline (speedup 1.0000x reference)
import numpy as np
import ml_dtypes
from contextlib import ExitStack

import concourse.bass as bass
import concourse.mybir as mybir
from concourse.bass_utils import run_bass_kernel_spmd

F32 = mybir.dt.float32
BF16 = mybir.dt.bfloat16
AF = mybir.ActivationFunctionType
ALU = mybir.AluOpType
AX = mybir.AxisListType

NCORES = 8
T = 8192
D = 2048
NCH = 16
TL = 1024
NBLK = 16
HEADS = 16
HMID = 8192
INV_SQRT_DH = float(1.0 / np.sqrt(128.0))
FWHT_SCALE = float(1.0 / np.sqrt(8192.0))
SLOPES = [2.0 ** (-0.5 * (h + 1)) for h in range(HEADS)]
NEG = -1e30
AGROWS = 2056
QB_UNITS = 15
QA_UNITS = 7
JCOLS = (QB_UNITS + QA_UNITS + 2) * 4

ENG2SEM = {"te": "te", "ve": "ve", "se": "se", "ge": "gc", "sy": "syc"}
NDMASEM = 8
DMASEMS = [f"gd{i}" for i in range(NDMASEM)] + [f"sd{i}" for i in range(NDMASEM)]
SEMNAMES = ["te", "ve", "se", "gc", "syc", "cc"] + DMASEMS

UG0 = 8192
WR0 = 0
MX0 = 8192
SH0 = 0
AK0 = 16384
AV0 = 19456
AP0 = 22528
AO0 = 0


class Sched:
    def __init__(self, strict_dma=False):
        self.strict_dma = strict_dma
        self.fns = {e: [] for e in ENG2SEM}
        self.cnt = {s: 0 for s in SEMNAMES}
        self.rr = {"gd": 0, "sd": 0}
        self.reg = {}
        self.lastw = {e: {} for e in ENG2SEM}
        self.sems = None

    def _need(self, reads, writes):
        need = {}
        for k in reads:
            r = self.reg.get(k)
            if r:
                for s, v in r["w"].items():
                    need[s] = max(need.get(s, 0), v)
        for k in writes:
            r = self.reg.get(k)
            if r:
                for s, v in r["w"].items():
                    need[s] = max(need.get(s, 0), v)
                for s, v in r["r"].items():
                    need[s] = max(need.get(s, 0), v)
        return need

    def op(self, eng, fn, reads=(), writes=(), sem=None, amt=1):
        if sem in ("gd", "sd"):
            q = sem
            sem = f"{q}{self.rr[q]}"
            self.rr[q] = (self.rr[q] + 1) % NDMASEM
        else:
            sem = sem or ENG2SEM[eng]
        need = self._need(reads, writes)
        if amt == 16:
            need[sem] = max(need.get(sem, 0), self.cnt[sem])
        lw = self.lastw[eng]
        waits = []
        for s, v in sorted(need.items()):
            if eng == "te" and s == "te":
                continue
            if lw.get(s, 0) < v:
                waits.append((s, v))
                lw[s] = v
        self.cnt[sem] += amt
        val = self.cnt[sem]
        for k in writes:
            self.reg[k] = {"w": {sem: val}, "r": {}}
        for k in reads:
            r = self.reg.setdefault(k, {"w": {}, "r": {}})
            r["r"][sem] = max(r["r"].get(sem, 0), val)

        def closure(h, _w=tuple(waits), _fn=fn, _sem=sem, _amt=amt):
            for s, v in _w:
                h.wait_ge(self.sems[s], v)
            _fn(h).then_inc(self.sems[_sem], _amt)

        self.fns[eng].append(closure)
        return (sem, val)

    def wait_keys(self, eng, keys):
        need = self._need(keys, ())
        lw = self.lastw[eng]
        waits = []
        for s, v in sorted(need.items()):
            if lw.get(s, 0) < v:
                waits.append((s, v))
                lw[s] = v

        def closure(h, _w=tuple(waits)):
            for s, v in _w:
                h.wait_ge(self.sems[s], v)

        self.fns[eng].append(closure)


def _hadamard(n):
    h = np.array([[1]], dtype=np.int64)
    while h.shape[0] < n:
        h = np.block([[h, h], [h, -h]])
    return h


def _slots(buf, c0, n):
    return [f"{buf}{s}" for s in range(c0 // 512, (c0 + n + 511) // 512)]


def build(tap=(), strict_dma=False):
    tap = set(tap)
    nc = bass.Bass("TRN2", target_bir_lowering=False)
    dt = nc.dram_tensor

    xT = dt("xT", [D, TL], F32, kind="ExternalInput").ap()
    iotag_i = dt("iotag", [1, TL], BF16, kind="ExternalInput").ap()
    selAB_i = dt("selAB", [128, 32], F32, kind="ExternalInput").ap()
    jtab_i = dt("jtab", [128, HEADS * JCOLS], F32, kind="ExternalInput").ap()
    outT = dt("outT", [D, TL], F32, kind="ExternalOutput").ap()

    gwT = dt("gwT", [D, D], F32, kind="ExternalInput").ap()
    w1T = dt("w1T", [D, HMID], F32, kind="ExternalInput").ap()
    w3T = dt("w3T", [D, HMID], F32, kind="ExternalInput").ap()
    w2T = dt("w2T", [HMID, D], F32, kind="ExternalInput").ap()
    wqT = dt("wqT", [D, D], F32, kind="ExternalInput").ap()
    wkT = dt("wkT", [D, D], F32, kind="ExternalInput").ap()
    wvT = dt("wvT", [D, D], F32, kind="ExternalInput").ap()
    woT = dt("woT", [D, D], F32, kind="ExternalInput").ap()
    wrt = dt("wrt", [D, 128], F32, kind="ExternalInput").ap()
    wga = dt("wga", [D, HMID], F32, kind="ExternalInput").ap()
    wua = dt("wua", [D, HMID], F32, kind="ExternalInput").ap()
    wda = dt("wda", [1024, D], F32, kind="ExternalInput").ap()
    swg = dt("swg", [D, D], F32, kind="ExternalInput").ap()
    swu = dt("swu", [D, D], F32, kind="ExternalInput").ap()
    swd = dt("swd", [D, D], F32, kind="ExternalInput").ap()
    dlru_i = dt("dlru", [D, 4], F32, kind="ExternalInput").ap()
    dlrv_i = dt("dlrv", [4, D], F32, kind="ExternalInput").ap()
    vcons_i = dt("vcons", [128, 164], F32, kind="ExternalInput").ap()
    ident_i = dt("ident", [128, 128], BF16, kind="ExternalInput").ap()
    h128_i = dt("h128", [128, 128], BF16, kind="ExternalInput").ap()
    mext_i = dt("mext", [128, 896], BF16, kind="ExternalInput").ap()
    selr_i = dt("selrow", [8, 1024], BF16, kind="ExternalInput").ap()

    ag1_in = dt("ag1_in", [3, D], F32)
    ag1_out = dt("ag1_out", [3 * NCORES, D], F32)
    ag2_in = dt("ag2_in", [AGROWS, D], BF16)
    ag2_out = dt("ag2_out", [AGROWS * NCORES, D], BF16)
    q_scr = dt("q_scr", [D, TL], BF16)

    taps = {}
    for name in ["xn0", "xl0", "xn1", "moe", "attn", "xa0"]:
        if name in tap:
            taps[name] = dt("t_" + name, [D, TL], F32, kind="ExternalOutput").ap()
    if "adbg" in tap:
        taps["adbg"] = dt("t_adbg", [128, 6144], F32, kind="ExternalOutput").ap()

    S = Sched(strict_dma=strict_dma)

    with ExitStack() as ctx:
        en = ctx.enter_context
        sb_x = en(nc.sbuf_tensor("sb_x", [128, NCH * TL], BF16))
        sb_xn = en(nc.sbuf_tensor("sb_xn", [128, NCH * TL], BF16))
        sb_wk = en(nc.sbuf_tensor("sb_wk", [128, 24576], BF16))
        sb_wsl = en(nc.sbuf_tensor("sb_wsl", [128, 16384], BF16))
        sb_f = en(nc.sbuf_tensor("sb_f", [128, 3072], F32))
        sb_b = en(nc.sbuf_tensor("sb_b", [128, 4096], BF16))
        sb_v = en(nc.sbuf_tensor("sb_v", [128, 256], F32))
        sb_s = en(nc.sbuf_tensor("sb_s", [8, 512], F32))
        sb_row = en(nc.sbuf_tensor("sb_row", [1, 1024], F32))
        sb_rowb = en(nc.sbuf_tensor("sb_rowb", [1, 1280], BF16))
        sb_cr = en(nc.sbuf_tensor("sb_cr", [128, 464], F32))
        sb_km = en(nc.sbuf_tensor("sb_km", [1, 144], F32))
        sb_id = en(nc.sbuf_tensor("sb_id", [128, 128], BF16))
        sb_h128 = en(nc.sbuf_tensor("sb_h128", [128, 128], BF16))
        sb_mext = en(nc.sbuf_tensor("sb_mext", [128, 896], BF16))
        sb_jtab = en(nc.sbuf_tensor("sb_jtab", [128, HEADS * JCOLS], F32))
        sb_iotag = en(nc.sbuf_tensor("sb_iotag", [1, TL], BF16))
        sb_vc = en(nc.sbuf_tensor("sb_vc", [128, 164], F32))
        sb_sel = en(nc.sbuf_tensor("sb_sel", [128, 32], F32))
        sb_selr = en(nc.sbuf_tensor("sb_selr", [8, 1024], BF16))
        sb_onesrf = en(nc.sbuf_tensor("sb_onesrf", [33, 128], F32))
        sb_ones = en(nc.sbuf_tensor("sb_ones", [128, 1], BF16))
        sb_onesr = en(nc.sbuf_tensor("sb_onesr", [1, 128], BF16))
        sb_dlru = en(nc.sbuf_tensor("sb_dlru", [128, 64], BF16))
        sb_d4 = en(nc.sbuf_tensor("sb_d4", [4, 1], BF16))
        sb_umb = en(nc.sbuf_tensor("sb_umb", [128, 16], BF16))

        PA = en(nc.psum_tensor("PA", [128, 512], F32))
        PB = en(nc.psum_tensor("PB", [128, 512], F32))
        PC = en(nc.psum_tensor("PC", [128, 512], F32))
        PD = en(nc.psum_tensor("PD", [128, 512], F32))
        PE1 = en(nc.psum_tensor("PE1", [128, 512], F32))
        PE2 = en(nc.psum_tensor("PE2", [128, 512], F32))
        PG = en(nc.psum_tensor("PG", [128, 1024], BF16))
        PH = en(nc.psum_tensor("PH", [8, 512], F32))
        PS = {"PA": PA, "PB": PB, "PC": PC, "PD": PD, "PE1": PE1, "PE2": PE2}

        sems = {}
        for s in SEMNAMES:
            sems[s] = en(nc.semaphore(s))
        S.sems = sems
        block = en(nc.Block())

        VCOFF = {"a": 0, "b": 16, "c": 32, "d": 48, "gb": 64, "bq": 80,
                 "bk": 96, "bv": 112, "bo": 128, "esc": 144, "amps": 146}

        def vc(name, ch):
            o = VCOFF[name]
            return sb_vc[:, o + ch: o + ch + 1]

        VOFF = {"aa": 0, "lnaa": 16, "aa512": 32, "bqs": 48, "dlr": 64,
                "carryA": 80, "carryB": 96, "cA": 112, "cB": 128,
                "umean": 144, "chain": 160, "mixp": 176, "tmp": 192}

        def vsl(name, n=16):
            o = VOFF[name]
            return sb_v[:, o: o + n]

        def vch(name, ch):
            o = VOFF[name]
            return sb_v[:, o + ch: o + ch + 1]

        def xch(ch, c0=0, n=512):
            return sb_x[:, ch * TL + c0: ch * TL + c0 + n]

        def xk(ch):
            return f"x{ch}"

        def xnch(ch, c0=0, n=512):
            return sb_xn[:, ch * TL + c0: ch * TL + c0 + n]

        def xnk(ch):
            return f"xn{ch}"

        def wk(c0, n):
            return sb_wk[:, c0:c0 + n]

        def wkK(c0, n):
            return _slots("wk", c0, n)

        def fbv(c0, n):
            return sb_f[:, c0:c0 + n]

        def fK(c0, n):
            return _slots("f", c0, n)

        def bbv(c0, n):
            return sb_b[:, c0:c0 + n]

        def bK(c0, n):
            return _slots("b", c0, n)

        def wslK(c0, n):
            return _slots("wsl", c0, n)

        def ug(ch, c0=0, n=512):
            return wk(UG0 + ch * TL + c0, n)

        def ugK(ch, c0=0, n=512):
            return wkK(UG0 + ch * TL + c0, n)

        S.op("ge", lambda g: g.dma_start(
            out=sb_x[:, :].rearrange("p (c f) -> p c f", f=TL),
            in_=xT.rearrange("(c p) f -> p c f", p=128)),
            writes=[xk(c) for c in range(NCH)], sem="gd", amt=16)
        for ap_, dst, wr in [(ident_i, sb_id, "ident"), (h128_i, sb_h128, "h128"),
                             (mext_i, sb_mext, "mext"), (jtab_i, sb_jtab, "jtab"),
                             (iotag_i, sb_iotag, "iotag"),
                             (vcons_i, sb_vc, "vcn"), (selAB_i, sb_sel, "sel"),
                             (selr_i, sb_selr, "selr")]:
            S.op("ge", lambda g, a=ap_, d=dst: g.dma_start(out=d[:, :], in_=a),
                 writes=[wr], sem="gd", amt=16)
        S.op("ge", lambda g: g.dma_start(
            out=sb_dlru[:, :].rearrange("p (c f) -> p c f", f=4),
            in_=dlru_i.rearrange("(c p) f -> p c f", p=128)),
            writes=["dlru"], sem="gd", amt=16)
        S.op("ge", lambda g: g.memset(sb_b[0:8, 0:2048], 0.0), writes=bK(0, 2048))
        S.op("sy", lambda s: s.dma_start(out=ag2_in[2048:AGROWS, :], in_=sb_b[0:8, 0:2048]),
             reads=bK(0, 2048), writes=["ag2m"], sem="sd", amt=16)
        S.op("ge", lambda g: g.memset(sb_ones[:, :], 1.0), writes=["ones"])
        S.op("ge", lambda g: g.memset(sb_onesr[:, :], 1.0), writes=["onesr"])
        S.op("ge", lambda g: g.memset(sb_onesrf[:, :], 1.0), writes=["onesrf"])
        S.op("se", lambda s: s.activation(vsl("aa"), sb_vc[:, 0:16], AF.Tanh),
             reads=["vcn"], writes=["aa"])
        S.op("se", lambda s: s.activation(vsl("lnaa"), vsl("aa"), AF.Ln),
             reads=["aa"], writes=["lnaa"])
        S.op("se", lambda s: s.activation(vsl("aa512"), vsl("lnaa"), AF.Exp, scale=512.0),
             reads=["lnaa"], writes=["aa512"])
        S.op("ve", lambda v: v.tensor_scalar_mul(vsl("bqs"), sb_vc[:, 80:96], INV_SQRT_DH),
             reads=["vcn"], writes=["bqs"])
        S.op("ve", lambda v: v.tensor_mul(fbv(0, 16), sb_vc[:, 146:162], sb_vc[:, 146:162]),
             reads=["vcn"], writes=fK(0, 16))
        S.op("ve", lambda v: v.tensor_add(vsl("mixp", 8), sb_f[:, 0:16:2], sb_f[:, 1:17:2]),
             reads=fK(0, 16), writes=["mixp"])
        S.op("ve", lambda v: v.reduce_sum(vch("tmp", 0), vsl("mixp", 8), axis=AX.X),
             reads=["mixp"], writes=["tmp0"])
        S.op("ve", lambda v: v.tensor_scalar_add(vch("tmp", 0), vch("tmp", 0), 1e-8),
             reads=["tmp0"], writes=["tmp0"])
        S.op("ve", lambda v: v.reciprocal(vch("tmp", 1), vch("tmp", 0)),
             reads=["tmp0"], writes=["tmp1"])
        S.op("ve", lambda v: v.tensor_scalar_mul(vsl("mixp", 8), vsl("mixp", 8), vch("tmp", 1)),
             reads=["mixp", "tmp1"], writes=["mixp"])
        S.op("ve", lambda v: v.tensor_scalar_mul(vsl("mixp", 8), vsl("mixp", 8), vc("esc", 0)),
             reads=["mixp", "vcn"], writes=["mixp"])
        S.op("ve", lambda v: v.tensor_copy(sb_b[:, 4088:4096], vsl("mixp", 8)),
             reads=["mixp"], writes=["mixpb"])

        def stream_w(src_ap, col0, ncols, buf, nchunks=NCH, rows=D):
            S.op("ge", lambda g: g.dma_start(
                out=sb_wsl[:, buf:buf + nchunks * ncols].rearrange("p (c f) -> p c f", f=ncols),
                in_=src_ap[0:rows, col0:col0 + ncols].rearrange("(c p) f -> p c f", p=128)),
                writes=wslK(buf, nchunks * ncols), sem="gd", amt=16)

        def accgrp(pk, n, fn_i, reads_i):
            for i in range(n):
                S.op("te", fn_i(i), reads=reads_i(i),
                     writes=[pk] if (i == 0 or i == n - 1) else [])

        def ln_phase(layer):
            for tile in range(2):
                t0 = tile * 512
                accgrp("PH", NCH,
                       lambda i, t0=t0: (lambda t, ch=i: t.matmul(
                           PH[0:1, :], sb_ones[:, :], xch(ch, t0),
                           start=(ch == 0), stop=(ch == NCH - 1))),
                       lambda i: [xk(i), "ones"])
                for ch in range(NCH):
                    dst = bbv(0, 512) if ch % 2 == 0 else bbv(512, 512)
                    dk = bK(0, 512) if ch % 2 == 0 else bK(512, 512)
                    S.op("ve", lambda v, ch=ch, t0=t0, dst=dst: v.tensor_mul(
                        dst, xch(ch, t0), xch(ch, t0)),
                        reads=[xk(ch)], writes=dk)
                    S.op("te", lambda t, ch=ch, dst=dst: t.matmul(
                        PA[0:1, :], sb_ones[:, :], dst,
                        start=(ch == 0), stop=(ch == NCH - 1)),
                        reads=dk + ["ones"],
                        writes=["PA"] if (ch == 0 or ch == NCH - 1) else [])
                S.op("ve", lambda v: v.tensor_scalar_mul(sb_row[0:1, 0:512], PH[0:1, :], 1.0 / D),
                     reads=["PH"], writes=["rw0"])
                S.op("ve", lambda v: v.tensor_scalar_mul(sb_row[0:1, 512:1024], PA[0:1, :], 1.0 / D),
                     reads=["PA"], writes=["rw1"])
                S.op("ve", lambda v: v.tensor_mul(sb_rowb[0:1, 0:512], sb_row[0:1, 0:512],
                                                  sb_row[0:1, 0:512]),
                     reads=["rw0"], writes=["rwb0"])
                S.op("ve", lambda v: v.tensor_sub(sb_row[0:1, 512:1024], sb_row[0:1, 512:1024],
                                                  sb_rowb[0:1, 0:512]),
                     reads=["rw1", "rwb0"], writes=["rw1"])
                S.op("ve", lambda v: v.tensor_scalar_add(sb_row[0:1, 512:1024],
                                                         sb_row[0:1, 512:1024], 1e-5),
                     reads=["rw1"], writes=["rw1"])
                S.op("ve", lambda v: v.reciprocal(sb_row[0:1, 512:1024], sb_row[0:1, 512:1024]),
                     reads=["rw1"], writes=["rw1"])
                S.op("se", lambda s: s.activation(sb_row[0:1, 512:1024], sb_row[0:1, 512:1024],
                                                  AF.Sqrt),
                     reads=["rw1"], writes=["rw1"])
                S.op("ve", lambda v: v.tensor_mul(sb_row[0:1, 0:512], sb_row[0:1, 0:512],
                                                  sb_row[0:1, 512:1024]),
                     reads=["rw0", "rw1"], writes=["rw0"])
                S.op("ve", lambda v: v.tensor_copy(sb_rowb[0:1, 0:512], sb_row[0:1, 512:1024]),
                     reads=["rw1"], writes=["rwb0"])
                S.op("ve", lambda v: v.tensor_copy(sb_rowb[0:1, 512:1024], sb_row[0:1, 0:512]),
                     reads=["rw0"], writes=["rwb1"])
                S.op("te", lambda t: t.matmul(PE1[:, :], sb_onesr[:, :], sb_rowb[0:1, 0:512],
                                              start=True, stop=True),
                     reads=["rwb0", "onesr"], writes=["PE1"])
                S.op("te", lambda t: t.matmul(PE2[:, :], sb_onesr[:, :], sb_rowb[0:1, 512:1024],
                                              start=True, stop=True),
                     reads=["rwb1", "onesr"], writes=["PE2"])
                for ch in range(NCH):
                    dst = fbv(0, 512) if ch % 2 == 0 else fbv(512, 512)
                    dk = fK(0, 512) if ch % 2 == 0 else fK(512, 512)
                    S.op("ve", lambda v, ch=ch, t0=t0, dst=dst: v.tensor_mul(
                        dst, xch(ch, t0), PE1[:, :]),
                        reads=[xk(ch), "PE1"], writes=dk)
                    S.op("ve", lambda v, ch=ch, t0=t0, dst=dst: v.tensor_sub(
                        xnch(ch, t0), dst, PE2[:, :]),
                        reads=dk + ["PE2"], writes=[xnk(ch)])

        def proj2048(wap, evac, rhs=None, rhs_keys=None):
            rhs = rhs or (lambda ch, c0: xnch(ch, c0))
            rhs_keys = rhs_keys or (lambda ch: [xnk(ch)])
            bufs = [0, 8192]
            stream_w(wap, 0, 512, bufs[0])
            for sl in range(4):
                if sl < 3:
                    stream_w(wap, (sl + 1) * 512, 512, bufs[(sl + 1) % 2])
                buf = bufs[sl % 2]
                for esub in range(4):
                    e = sl * 4 + esub
                    for tile in range(2):
                        pk = "PA" if tile == 0 else "PB"
                        ps = PS[pk]
                        accgrp(pk, NCH,
                               lambda i, ps=ps, buf=buf, esub=esub, tile=tile: (
                                   lambda t, ch=i: t.matmul(
                                       ps[:, :],
                                       sb_wsl[:, buf + ch * 512 + esub * 128:
                                              buf + ch * 512 + esub * 128 + 128],
                                       rhs(ch, tile * 512),
                                       start=(ch == 0), stop=(ch == NCH - 1))),
                               lambda i, buf=buf, esub=esub: (
                                   _slots("wsl", buf + i * 512 + esub * 128, 128) + rhs_keys(i)))
                        evac(e, tile, ps, pk)

        ln_phase(0)

        def dump_tap(name, src, src_keys):
            for ch in range(NCH):
                S.op("ve", lambda v, ch=ch: v.tensor_copy(fbv(0, 1024), src(ch)),
                     reads=src_keys(ch), writes=fK(0, 1024))
                S.op("sy", lambda s, ch=ch: s.dma_start(
                    out=taps[name][ch * 128:(ch + 1) * 128, :], in_=fbv(0, 1024)),
                    reads=fK(0, 1024), writes=["dmp"], sem="sd", amt=16)

        if "xn0" in tap:
            dump_tap("xn0", lambda ch: xnch(ch, 0, TL), lambda ch: [xnk(ch)])

        def evac_gate(e, tile, ps, pk):
            t0 = tile * 512
            dst = bbv(0, 512) if tile == 0 else bbv(512, 512)
            dk = bK(0, 512) if tile == 0 else bK(512, 512)
            S.op("se", lambda s, e=e, ps=ps, dst=dst: s.activation(
                dst, ps[:, :], AF.Sigmoid, bias=vc("gb", e)),
                reads=[pk, "vcn"], writes=dk)
            S.op("ve", lambda v, e=e, t0=t0, dst=dst, tile=tile: v.scalar_tensor_tensor(
                ug(e, t0), dst, 1.0, xnch(e, t0), ALU.mult, ALU.mult,
                accum_out=sb_cr[:, 32 + e * 2 + tile: 33 + e * 2 + tile]),
                reads=dk + [xnk(e)], writes=ugK(e, t0) + [f"ugs{e}_{tile}"])

        proj2048(gwT, evac_gate)

        for ch in range(NCH):
            for tile in range(2):
                t0 = tile * 512
                buf_ = fbv(1024, 512) if tile == 0 else fbv(1536, 512)
                buK = fK(1024, 512) if tile == 0 else fK(1536, 512)
                ssf = fbv(2048, 512) if tile == 0 else fbv(2560, 512)
                ssK = fK(2048, 512) if tile == 0 else fK(2560, 512)
                S.op("ve", lambda v, ch=ch, t0=t0, buf_=buf_: v.tensor_scalar_mul(
                    buf_, ug(ch, t0), vc("b", ch)),
                    reads=ugK(ch, t0) + ["vcn"], writes=buK)
                S.op("ve", lambda v, ch=ch, buf_=buf_, ssf=ssf: v.tensor_tensor_scan(
                    ssf, vch("aa", ch).broadcast_to((128, 512)), buf_, 0.0, ALU.mult, ALU.add),
                    reads=buK + ["aa"], writes=ssK)
                S.op("ve", lambda v, ch=ch, tile=tile, ssf=ssf: v.tensor_copy(
                    sb_cr[:, tile * 16 + ch: tile * 16 + ch + 1], ssf[:, 511:512]),
                    reads=ssK, writes=[f"carr{tile}_{ch}"])
                S.op("ve", lambda v, ch=ch, t0=t0, ssf=ssf: v.scalar_tensor_tensor(
                    xch(ch, t0), ssf, vc("c", ch), xch(ch, t0), ALU.mult, ALU.add),
                    reads=ssK + ["vcn", xk(ch)], writes=[xk(ch)])
                S.op("ve", lambda v, ch=ch, t0=t0: v.scalar_tensor_tensor(
                    xch(ch, t0), ug(ch, t0), vc("d", ch), xch(ch, t0), ALU.mult, ALU.add),
                    reads=ugK(ch, t0) + ["vcn", xk(ch)], writes=[xk(ch)])
        for ch in range(NCH):
            S.op("ve", lambda v, ch=ch: v.tensor_add(
                sb_cr[:, 64 + ch: 65 + ch], sb_cr[:, 32 + ch * 2: 33 + ch * 2],
                sb_cr[:, 33 + ch * 2: 34 + ch * 2]),
                reads=[f"ugs{ch}_0", f"ugs{ch}_1"], writes=[f"ups{ch}"])

        S.op("sy", lambda s: s.dma_start(
            out=ag1_in[0, :].rearrange("(p c) -> p c", c=16), in_=sb_cr[:, 0:16]),
            reads=[f"carr0_{ch}" for ch in range(NCH)], writes=["ag1a"], sem="sd", amt=16)
        S.op("sy", lambda s: s.dma_start(
            out=ag1_in[1, :].rearrange("(p c) -> p c", c=16), in_=sb_cr[:, 16:32]),
            reads=[f"carr1_{ch}" for ch in range(NCH)], writes=["ag1b"], sem="sd", amt=16)
        S.op("sy", lambda s: s.dma_start(
            out=ag1_in[2, :].rearrange("(p c) -> p c", c=16), in_=sb_cr[:, 64:80]),
            reads=[f"ups{ch}" for ch in range(NCH)], writes=["ag1c"], sem="sd", amt=16)
        S.op("ge", lambda g: g.collective_compute(
            "AllGather", ALU.bypass, replica_groups=[list(range(NCORES))],
            ins=[ag1_in.ap()], outs=[ag1_out.ap()]),
            reads=["ag1a", "ag1b", "ag1c"], writes=["ag1o"], sem="cc", amt=1)
        S.op("ge", lambda g: g.dma_start(
            out=sb_cr[:, 80:464].rearrange("p (r c) -> p r c", c=16),
            in_=ag1_out.ap().rearrange("r (p c) -> p r c", c=16)),
            reads=["ag1o"], writes=["agm"], sem="gd", amt=16)

        def crow(j):
            r = j if j < 8 else 15 - j
            return 80 + (3 * r + (0 if j < 8 else 1)) * 16

        S.op("ge", lambda g: g.memset(vsl("chain"), 0.0), writes=["chain"])
        S.op("ge", lambda g: g.memset(vsl("carryA"), 0.0), writes=["carryA"])
        S.op("ge", lambda g: g.memset(vsl("carryB"), 0.0), writes=["carryB"])
        for j in range(NBLK):
            S.op("ve", lambda v, j=j: v.scalar_tensor_tensor(
                vsl("carryA"), vsl("chain"), sb_sel[:, j:j + 1], vsl("carryA"),
                ALU.mult, ALU.add),
                reads=["chain", "sel", "carryA"], writes=["carryA"])
            S.op("ve", lambda v, j=j: v.scalar_tensor_tensor(
                vsl("carryB"), vsl("chain"), sb_sel[:, 16 + j:17 + j], vsl("carryB"),
                ALU.mult, ALU.add),
                reads=["chain", "sel", "carryB"], writes=["carryB"])
            if j < NBLK - 1:
                S.op("ve", lambda v: v.tensor_mul(vsl("chain"), vsl("aa512"), vsl("chain")),
                     reads=["chain", "aa512"], writes=["chain"])
                S.op("ve", lambda v, j=j: v.tensor_add(
                    vsl("chain"), vsl("chain"), sb_cr[:, crow(j):crow(j) + 16]),
                    reads=["chain", "agm"], writes=["chain"])
        S.op("ve", lambda v: v.tensor_mul(vsl("cA"), vsl("carryA"), sb_vc[:, 32:48]),
             reads=["carryA", "vcn"], writes=["cA"])
        S.op("ve", lambda v: v.tensor_mul(vsl("cB"), vsl("carryB"), sb_vc[:, 32:48]),
             reads=["carryB", "vcn"], writes=["cB"])
        S.op("ve", lambda v: v.tensor_add(vsl("umean"), sb_cr[:, 80 + 2 * 16:80 + 3 * 16],
                                          sb_cr[:, 80 + 5 * 16:80 + 6 * 16]),
             reads=["agm"], writes=["umean"])
        for r in range(2, 8):
            S.op("ve", lambda v, r=r: v.tensor_add(
                vsl("umean"), vsl("umean"),
                sb_cr[:, 80 + (3 * r + 2) * 16:80 + (3 * r + 2) * 16 + 16]),
                reads=["agm", "umean"], writes=["umean"])
        S.op("ve", lambda v: v.tensor_scalar_mul(vsl("umean"), vsl("umean"), 1.0 / T),
             reads=["umean"], writes=["umean"])
        S.op("ve", lambda v: v.tensor_copy(sb_umb[:, :], vsl("umean")),
             reads=["umean"], writes=["umb"])
        accgrp("PH", NCH,
               lambda i: (lambda t, ch=i: t.matmul(
                   PH[0:4, 0:1], sb_dlru[:, ch * 4: ch * 4 + 4],
                   sb_umb[:, ch:ch + 1], start=(ch == 0), stop=(ch == NCH - 1))),
               lambda i: ["dlru", "umb"])
        S.op("se", lambda s: s.activation(sb_d4[:, :], PH[0:4, 0:1], AF.Identity, scale=0.25),
             reads=["PH"], writes=["d4"])
        S.op("ge", lambda g: g.dma_start(out=sb_b[0:4, 0:2048], in_=dlrv_i),
             writes=bK(0, 2048), sem="gd", amt=16)
        for e in range(NCH):
            S.op("te", lambda t, e=e: t.matmul(
                PB[:, e:e + 1], sb_b[0:4, e * 128:(e + 1) * 128],
                sb_d4[:, :], start=True, stop=True),
                reads=bK(e * 128, 128) + ["d4"], writes=["PB"] if (e == 0 or e == NCH - 1) else [])
        S.op("ve", lambda v: v.tensor_copy(vsl("dlr"), PB[:, 0:16]),
             reads=["PB"], writes=["dlr"])
        S.op("ge", lambda g: g.iota(fbv(1024, 512), [[1, 512]], channel_multiplier=0, allow_small_or_imprecise_dtypes=True),
             writes=fK(1024, 512))
        for ch in range(NCH):
            apd = fbv(0, 512) if ch % 2 == 0 else fbv(512, 512)
            apK = fK(0, 512) if ch % 2 == 0 else fK(512, 512)
            S.op("se", lambda s, ch=ch, apd=apd: s.activation(
                apd, fbv(1024, 512), AF.Exp, scale=vch("lnaa", ch), bias=vch("lnaa", ch)),
                reads=["lnaa"] + fK(1024, 512), writes=apK)
            for tile in range(2):
                t0 = tile * 512
                cXn = "cA" if tile == 0 else "cB"
                S.op("ve", lambda v, ch=ch, t0=t0, cXn=cXn, apd=apd: v.scalar_tensor_tensor(
                    xch(ch, t0), apd, vch(cXn, ch), xch(ch, t0), ALU.mult, ALU.add),
                    reads=apK + [cXn, xk(ch)], writes=[xk(ch)])
                S.op("ve", lambda v, ch=ch, t0=t0: v.scalar_tensor_tensor(
                    xch(ch, t0), ug(ch, t0), vch("dlr", ch), xch(ch, t0), ALU.mult, ALU.add),
                    reads=ugK(ch, t0) + ["dlr", xk(ch)], writes=[xk(ch)])

        if "xa0" in tap:
            dump_tap("xa0", lambda ch: xch(ch, 0, TL), lambda ch: [xk(ch)])

        def hid(alpha, n=512):
            if alpha < 48:
                return wk(alpha * 512, n)
            return sb_wsl[:, 8192 + (alpha - 48) * 512: 8192 + (alpha - 48) * 512 + n]

        def hidK(alpha):
            if alpha < 48:
                return wkK(alpha * 512, 512)
            return wslK(8192 + (alpha - 48) * 512, 512)

        for tile in range(2):
            t0 = tile * 512
            stream_w(w1T, 0, 128, 0)
            stream_w(w3T, 0, 128, 4096)
            for alpha in range(64):
                if alpha < 63:
                    stream_w(w1T, (alpha + 1) * 128, 128, ((alpha + 1) % 2) * 2048)
                    stream_w(w3T, (alpha + 1) * 128, 128, 4096 + ((alpha + 1) % 2) * 2048)
                buf = (alpha % 2) * 2048
                pga = "PA" if alpha % 2 == 0 else "PC"
                pgu = "PB" if alpha % 2 == 0 else "PD"
                psa, psu = PS[pga], PS[pgu]
                accgrp(pga, NCH,
                       lambda i, psa=psa, buf=buf, t0=t0: (
                           lambda t, ch=i: t.matmul(
                               psa[:, :], sb_wsl[:, buf + ch * 128: buf + ch * 128 + 128],
                               xnch(ch, t0), start=(ch == 0), stop=(ch == NCH - 1))),
                       lambda i, buf=buf: _slots("wsl", buf + i * 128, 128) + [xnk(i)])
                accgrp(pgu, NCH,
                       lambda i, psu=psu, buf=buf, t0=t0: (
                           lambda t, ch=i: t.matmul(
                               psu[:, :], sb_wsl[:, 4096 + buf + ch * 128: 4096 + buf + ch * 128 + 128],
                               xnch(ch, t0), start=(ch == 0), stop=(ch == NCH - 1))),
                       lambda i, buf=buf: _slots("wsl", 4096 + buf + i * 128, 128) + [xnk(i)])
                dst = bbv(0, 512) if alpha % 2 == 0 else bbv(512, 512)
                dk = bK(0, 512) if alpha % 2 == 0 else bK(512, 512)
                S.op("se", lambda s, psa=psa, dst=dst: s.activation(dst, psa[:, :], AF.Silu),
                     reads=[pga], writes=dk)
                S.op("ve", lambda v, alpha=alpha, psu=psu, dst=dst: v.tensor_mul(
                    hid(alpha), dst, psu[:, :]),
                    reads=dk + [pgu], writes=hidK(alpha))
            for alpha in range(64):
                pk = "PE1" if alpha % 2 == 0 else "PE2"
                ps = PS[pk]
                S.op("te", lambda t, alpha=alpha, ps=ps: t.matmul(
                    ps[:, :], sb_h128[:, :], hid(alpha), start=True, stop=True),
                    reads=["h128"] + hidK(alpha), writes=[pk])
                S.op("ve", lambda v, alpha=alpha, ps=ps: v.tensor_copy(hid(alpha), ps[:, :]),
                    reads=[pk], writes=hidK(alpha))
            for k in range(6):
                step = 1 << k
                for base in range(0, 64, step * 2):
                    for off in range(step):
                        a, b2 = base + off, base + off + step
                        S.op("ve", lambda v, a=a, b2=b2: v.tensor_add(
                            hid(a), hid(a), hid(b2)),
                            reads=hidK(a) + hidK(b2), writes=hidK(a))
                        S.op("ve", lambda v, a=a, b2=b2: v.scalar_tensor_tensor(
                            hid(b2), hid(b2), -2.0, hid(a), ALU.mult, ALU.add),
                            reads=hidK(a) + hidK(b2), writes=hidK(b2))
            def w2slab(do, hq, buf):
                S.op("ge", lambda g, do=do, hq=hq, buf=buf: g.dma_start(
                    out=sb_wsl[:, buf:buf + 2048].rearrange("p (c f) -> p c f", f=128),
                    in_=w2T[hq * 2048:(hq + 1) * 2048, do * 128:(do + 1) * 128]
                    .rearrange("(c p) f -> p c f", p=128)),
                    writes=wslK(buf, 2048), sem="gd", amt=16)
            w2slab(0, 0, 0)
            for qi in range(64):
                do, hq = qi // 4, qi % 4
                if qi < 63:
                    do2, hq2 = (qi + 1) // 4, (qi + 1) % 4
                    w2slab(do2, hq2, ((qi + 1) % 2) * 2048)
                buf = (qi % 2) * 2048
                pk = "PA" if do % 2 == 0 else "PB"
                ps = PS[pk]
                accgrp(pk, 16,
                       lambda i, buf=buf, ps=ps, hq=hq: (
                           lambda t, hc=i: t.matmul(
                               ps[:, :], sb_wsl[:, buf + hc * 128: buf + hc * 128 + 128],
                               hid(hq * 16 + hc), start=(hq == 0 and hc == 0),
                               stop=(hq == 3 and hc == 15))),
                       lambda i, buf=buf, hq=hq: _slots("wsl", buf + i * 128, 128) + hidK(hq * 16 + i))
                if hq == 3:
                    S.op("ve", lambda v, do=do, t0=t0, ps=ps: v.scalar_tensor_tensor(
                        xch(do, t0), ps[:, :], FWHT_SCALE, xch(do, t0), ALU.mult, ALU.add),
                        reads=[pk, xk(do)], writes=[xk(do)])

        if "xl0" in tap:
            dump_tap("xl0", lambda ch: xch(ch, 0, TL), lambda ch: [xk(ch)])

        ln_phase(1)
        if "xn1" in tap:
            dump_tap("xn1", lambda ch: xnch(ch, 0, TL), lambda ch: [xnk(ch)])

        def evac_q(e, tile, ps, pk):
            t0 = tile * 512
            dst = bbv(0, 512) if tile == 0 else bbv(512, 512)
            dk = bK(0, 512) if tile == 0 else bK(512, 512)
            S.op("se", lambda s, e=e, ps=ps, dst=dst: s.activation(
                dst, ps[:, :], AF.Identity, bias=vch("bqs", e), scale=INV_SQRT_DH),
                reads=[pk, "bqs"], writes=dk)
            S.op("sy", lambda s, e=e, t0=t0, dst=dst: s.dma_start(
                out=bass.AP(q_scr, e * 128 * TL + t0, [[TL, 128], [1, 512]]), in_=dst),
                reads=dk, writes=["qscr"], sem="sd", amt=16)

        proj2048(wqT, evac_q)

        def evac_k(e, tile, ps, pk):
            t0 = tile * 512
            dst = bbv(0, 512) if tile == 0 else bbv(512, 512)
            dk = bK(0, 512) if tile == 0 else bK(512, 512)
            S.op("se", lambda s, e=e, ps=ps, dst=dst: s.activation(
                dst, ps[:, :], AF.Identity, bias=vc("bk", e)),
                reads=[pk, "vcn"], writes=dk)
            S.op("sy", lambda s, e=e, t0=t0, dst=dst: s.dma_start(
                out=bass.AP(ag2_in, e * 128 * TL + t0, [[TL, 128], [1, 512]]), in_=dst),
                reads=dk, writes=["ag2k"], sem="sd", amt=16)
            S.op("ve", lambda v, dst=dst: v.tensor_mul(bbv(1024, 512), dst, dst),
                 reads=dk, writes=bK(1024, 512))
            S.op("te", lambda t: t.matmul(PH[0:1, :], sb_ones[:, :], bbv(1024, 512),
                                          start=True, stop=True),
                 reads=bK(1024, 512) + ["ones"], writes=["PH"])
            S.op("ve", lambda v, e=e, tile=tile: v.reduce_max(
                sb_km[0:1, 16 + e * 2 + tile:17 + e * 2 + tile], PH[0:1, :], axis=AX.X),
                reads=["PH"], writes=[f"kn{e}_{tile}"])

        proj2048(wkT, evac_k)
        for e in range(NCH):
            S.op("ve", lambda v, e=e: v.tensor_max(
                sb_km[0:1, 48 + e:49 + e], sb_km[0:1, 16 + e * 2:17 + e * 2],
                sb_km[0:1, 17 + e * 2:18 + e * 2]),
                reads=[f"kn{e}_0", f"kn{e}_1"], writes=["knm"])
        S.op("ve", lambda v: v.tensor_scalar_mul(sb_km[0:1, 48:64], sb_km[0:1, 48:64], 1.02),
             reads=["knm"], writes=["knm"])
        S.op("ve", lambda v: v.tensor_copy(sb_rowb[0:1, 1024:1040], sb_km[0:1, 48:64]),
             reads=["knm"], writes=["knmb"])
        S.op("sy", lambda s: s.dma_start(
            out=bass.AP(ag2_in, 2048 * D, [[16, 1], [1, 16]]), in_=sb_rowb[0:1, 1024:1040]),
            reads=["knmb"], writes=["ag2m"], sem="sd", amt=16)

        def evac_v(e, tile, ps, pk):
            t0 = tile * 512
            dst = bbv(0, 512) if tile == 0 else bbv(512, 512)
            dk = bK(0, 512) if tile == 0 else bK(512, 512)
            S.op("se", lambda s, e=e, ps=ps, dst=dst: s.activation(
                dst, ps[:, :], AF.Identity, bias=vc("bv", e)),
                reads=[pk, "vcn"], writes=dk)
            for tc in range(4):
                S.op("te", lambda t, tc=tc, dst=dst: t.transpose(
                    PG[:, tc * 128:(tc + 1) * 128], dst[:, tc * 128:(tc + 1) * 128],
                    sb_id[:, :]),
                    reads=dk + ["ident"], writes=["PG"])
                vcol = tile * 8192 + tc * 2048 + e * 128
                S.op("ve", lambda v, tc=tc, vcol=vcol: v.tensor_copy(
                    wk(vcol, 128), PG[:, tc * 128:(tc + 1) * 128]),
                    reads=["PG"], writes=wkK(vcol, 128))
            if e == NCH - 1:
                for tc in range(4):
                    vcol = tile * 8192 + tc * 2048
                    S.op("sy", lambda s, tc=tc, t0=t0, vcol=vcol: s.dma_start(
                        out=bass.AP(ag2_in, (1024 + t0 + tc * 128) * D, [[D, 128], [1, D]]),
                        in_=wk(vcol, 2048)),
                        reads=wkK(vcol, 2048), writes=["ag2v"], sem="sd", amt=16)

        proj2048(wvT, evac_v)

        S.op("ge", lambda g: g.collective_compute(
            "AllGather", ALU.bypass, replica_groups=[list(range(NCORES))],
            ins=[ag2_in.ap()], outs=[ag2_out.ap()]),
            reads=["ag2k", "ag2v", "ag2m"], writes=["ag2o"], sem="cc", amt=1)

        S.op("ge", lambda g: g.dma_start(
            out=sb_wsl[:, 0:2048].rearrange("p (c f) -> p c f", f=128),
            in_=wrt.rearrange("(c p) f -> p c f", p=128)),
            writes=wslK(0, 2048), sem="gd", amt=16)
        for tile in range(2):
            t0 = tile * 512
            accgrp("PA", NCH,
                   lambda i, t0=t0: (lambda t, ch=i: t.matmul(
                       PA[:, :], sb_wsl[:, ch * 128:(ch + 1) * 128], xnch(ch, t0),
                       start=(ch == 0), stop=(ch == NCH - 1))),
                   lambda i: _slots("wsl", i * 128, 128) + [xnk(i)])
            S.op("ve", lambda v: v.tensor_copy(bbv(0, 512), PA[:, :]),
                 reads=["PA"], writes=bK(0, 512))
            for g4 in range(4):
                S.op("te", lambda t, g4=g4: t.transpose(
                    PG[:, g4 * 128:(g4 + 1) * 128], sb_b[:, g4 * 128:(g4 + 1) * 128],
                    sb_id[:, :]),
                    reads=bK(0, 512) + ["ident"], writes=["PG"])
            S.op("ve", lambda v: v.tensor_copy(bbv(512, 512), PG[:, 0:512]),
                 reads=["PG"], writes=bK(512, 512))
            for g4 in range(4):
                rt = sb_b[:, 512 + g4 * 128: 512 + (g4 + 1) * 128]
                S.op("ve", lambda v, rt=rt: v.reduce_max(fbv(1024, 1), rt, axis=AX.X, negate=True),
                     reads=bK(512, 512), writes=["mr0"])
                S.op("se", lambda s, rt=rt: s.activation(
                    fbv(1536, 128), rt, AF.Exp, bias=sb_f[:, 1024:1025],
                    accum_out=sb_f[:, 1025:1026]),
                    reads=["mr0"] + bK(512, 512), writes=fK(1536, 128) + ["den"])
                S.op("ve", lambda v: v.reciprocal(sb_f[:, 1026:1027], sb_f[:, 1025:1026]),
                     reads=["den"], writes=["deni"])
                S.op("ve", lambda v: v.tensor_scalar_mul(
                    fbv(1536, 128), fbv(1536, 128), sb_f[:, 1026:1027]),
                    reads=fK(1536, 128) + ["deni"], writes=fK(1536, 128))
                S.op("ve", lambda v: v.tensor_copy(fbv(1664, 128), fbv(1536, 128)),
                     reads=fK(1536, 128), writes=fK(1664, 128))
                for rnd in range(3):
                    S.op("ve", lambda v: v.reduce_max(sb_f[:, 1027:1028], fbv(1664, 128), axis=AX.X),
                         reads=fK(1664, 128), writes=["mr"])
                    S.op("ve", lambda v: v.tensor_scalar(
                        fbv(1792, 128), fbv(1664, 128), sb_f[:, 1027:1028], None, ALU.is_equal),
                        reads=fK(1664, 128) + ["mr"], writes=fK(1792, 128))
                    S.op("ve", lambda v: v.scalar_tensor_tensor(
                        fbv(1664, 128), fbv(1792, 128), NEG, fbv(1664, 128), ALU.mult, ALU.add),
                        reads=fK(1792, 128) + fK(1664, 128), writes=fK(1664, 128))
                S.op("ve", lambda v: v.reduce_max(sb_f[:, 1028:1029], fbv(1664, 128), axis=AX.X),
                     reads=fK(1664, 128), writes=["m4"])
                S.op("ve", lambda v: v.scalar_tensor_tensor(
                    fbv(1792, 128), fbv(1536, 128), sb_f[:, 1028:1029], fbv(1536, 128),
                    ALU.is_ge, ALU.mult, accum_out=sb_f[:, 1029:1030]),
                    reads=fK(1536, 128) + ["m4"], writes=fK(1792, 128) + ["ps4"])
                S.op("ve", lambda v: v.tensor_scalar_add(sb_f[:, 1029:1030], sb_f[:, 1029:1030], 1e-8),
                     reads=["ps4"], writes=["ps4"])
                S.op("ve", lambda v: v.reciprocal(sb_f[:, 1030:1031], sb_f[:, 1029:1030]),
                     reads=["ps4"], writes=["ps4i"])
                S.op("ve", lambda v, g4=g4: v.tensor_scalar_mul(
                    sb_b[:, 1024 + g4 * 128: 1024 + (g4 + 1) * 128],
                    fbv(1792, 128), sb_f[:, 1030:1031]),
                    reads=fK(1792, 128) + ["ps4i"], writes=bK(1024 + g4 * 128, 128))
            for g4 in range(4):
                S.op("te", lambda t, g4=g4: t.transpose(
                    PG[:, 512 + g4 * 128: 512 + (g4 + 1) * 128],
                    sb_b[:, 1024 + g4 * 128: 1024 + (g4 + 1) * 128], sb_id[:, :]),
                    reads=bK(1024 + g4 * 128, 128) + ["ident"], writes=["PG"])
            S.op("ve", lambda v: v.tensor_copy(bbv(1536, 512), PG[:, 512:1024]),
                 reads=["PG"], writes=bK(1536, 512))
            S.op("te", lambda t: t.matmul(PH[0:8, :], sb_b[:, 4088:4096], bbv(1536, 512),
                                          start=True, stop=True),
                 reads=["mixpb"] + bK(1536, 512), writes=["PH"])
            S.op("ve", lambda v: v.tensor_copy(sb_b[0:8, 3072:3584], PH[0:8, :]),
                 reads=["PH"], writes=bK(3072, 512))
            for kk in range(8):
                S.op("te", lambda t, kk=kk: t.matmul(
                    PE1[:, :], sb_selr[0:8, kk * 128:(kk + 1) * 128], sb_b[0:8, 3072:3584],
                    start=True, stop=True),
                    reads=["selr"] + bK(3072, 512), writes=["PE1"])
                wcol = WR0 + (kk * 2 + tile) * 512
                S.op("ve", lambda v, wcol=wcol: v.tensor_copy(wk(wcol, 512), PE1[:, :]),
                     reads=["PE1"], writes=wkK(wcol, 512))
        S.op("ge", lambda g: g.memset(wk(MX0, 8192), 0.0), writes=wkK(MX0, 8192))
        bufs = [0, 8192]
        stream_w(wga, 0, 256, 0)
        stream_w(wua, 0, 256, 4096)
        for hb in range(32):
            if hb < 31:
                nb = bufs[(hb + 1) % 2]
                stream_w(wga, (hb + 1) * 256, 256, nb)
                stream_w(wua, (hb + 1) * 256, 256, nb + 4096)
            buf = bufs[hb % 2]
            for hc in range(2):
                alpha = hb * 2 + hc
                kk, ic = alpha >> 3, alpha & 7
                for tile in range(2):
                    t0 = tile * 512
                    pga = "PA" if tile == 0 else "PC"
                    pgu = "PB" if tile == 0 else "PD"
                    psa, psu = PS[pga], PS[pgu]
                    accgrp(pga, NCH,
                           lambda i, psa=psa, buf=buf, hc=hc, t0=t0: (
                               lambda t, ch=i: t.matmul(
                                   psa[:, :],
                                   sb_wsl[:, buf + ch * 256 + hc * 128: buf + ch * 256 + hc * 128 + 128],
                                   xnch(ch, t0), start=(ch == 0), stop=(ch == NCH - 1))),
                           lambda i, buf=buf, hc=hc: _slots("wsl", buf + i * 256 + hc * 128, 128) + [xnk(i)])
                    accgrp(pgu, NCH,
                           lambda i, psu=psu, buf=buf, hc=hc, t0=t0: (
                               lambda t, ch=i: t.matmul(
                                   psu[:, :],
                                   sb_wsl[:, buf + 4096 + ch * 256 + hc * 128: buf + 4096 + ch * 256 + hc * 128 + 128],
                                   xnch(ch, t0), start=(ch == 0), stop=(ch == NCH - 1))),
                           lambda i, buf=buf, hc=hc: _slots("wsl", buf + 4096 + i * 256 + hc * 128, 128) + [xnk(i)])
                    dst = bbv(0, 512) if tile == 0 else bbv(512, 512)
                    dk = bK(0, 512) if tile == 0 else bK(512, 512)
                    S.op("se", lambda s, psa=psa, dst=dst: s.activation(dst, psa[:, :], AF.Silu),
                         reads=[pga], writes=dk)
                    bdst = bbv(1024, 512) if tile == 0 else bbv(2048, 512)
                    bkK = bK(1024, 512) if tile == 0 else bK(2048, 512)
                    S.op("ve", lambda v, psu=psu, dst=dst, bdst=bdst: v.tensor_mul(
                        bdst, dst, psu[:, :]),
                        reads=dk + [pgu], writes=bkK)
                    fdst = fbv(0, 512) if tile == 0 else fbv(512, 512)
                    fkK = fK(0, 512) if tile == 0 else fK(512, 512)
                    wcol = WR0 + (kk * 2 + tile) * 512
                    S.op("ve", lambda v, bdst=bdst, fdst=fdst, wcol=wcol: v.tensor_mul(
                        fdst, bdst, wk(wcol, 512)),
                        reads=bkK + wkK(wcol, 512), writes=fkK)
                    mcol = MX0 + (ic * 2 + tile) * 512
                    S.op("ve", lambda v, mcol=mcol, fdst=fdst: v.tensor_add(
                        wk(mcol, 512), wk(mcol, 512), fdst),
                        reads=fkK + wkK(mcol, 512), writes=wkK(mcol, 512))
        S.op("ge", lambda g: g.dma_start(
            out=sb_wsl[:, :].rearrange("p (c f) -> p c f", f=D),
            in_=wda.rearrange("(c p) f -> p c f", p=128)),
            writes=wslK(0, 16384), sem="gd", amt=16)
        for tile in range(2):
            t0 = tile * 512
            for do in range(16):
                pk = "PA" if do % 2 == 0 else "PB"
                ps = PS[pk]
                accgrp(pk, 8,
                       lambda i, do=do, ps=ps, tile=tile: (
                           lambda t, ic=i: t.matmul(
                               ps[:, :], sb_wsl[:, ic * D + do * 128: ic * D + do * 128 + 128],
                               wk(MX0 + (ic * 2 + tile) * 512, 512),
                               start=(ic == 0), stop=(ic == 7))),
                       lambda i, do=do, tile=tile: (_slots("wsl", i * D + do * 128, 128) +
                                                    wkK(MX0 + (i * 2 + tile) * 512, 512)))
                S.op("ve", lambda v, do=do, t0=t0, ps=ps: v.scalar_tensor_tensor(
                    xch(do, t0), ps[:, :], 1.0, xch(do, t0), ALU.mult, ALU.add),
                    reads=[pk, xk(do)], writes=[xk(do)])
        bufs = [0, 8192]
        stream_w(swg, 0, 256, 0)
        stream_w(swu, 0, 256, 4096)
        for i2b in range(8):
            if i2b < 7:
                nb = bufs[(i2b + 1) % 2]
                stream_w(swg, (i2b + 1) * 256, 256, nb)
                stream_w(swu, (i2b + 1) * 256, 256, nb + 4096)
            buf = bufs[i2b % 2]
            for hc in range(2):
                i2 = i2b * 2 + hc
                for tile in range(2):
                    t0 = tile * 512
                    pga = "PA" if tile == 0 else "PC"
                    pgu = "PB" if tile == 0 else "PD"
                    psa, psu = PS[pga], PS[pgu]
                    accgrp(pga, NCH,
                           lambda i, psa=psa, buf=buf, hc=hc, t0=t0: (
                               lambda t, ch=i: t.matmul(
                                   psa[:, :],
                                   sb_wsl[:, buf + ch * 256 + hc * 128: buf + ch * 256 + hc * 128 + 128],
                                   xnch(ch, t0), start=(ch == 0), stop=(ch == NCH - 1))),
                           lambda i, buf=buf, hc=hc: _slots("wsl", buf + i * 256 + hc * 128, 128) + [xnk(i)])
                    accgrp(pgu, NCH,
                           lambda i, psu=psu, buf=buf, hc=hc, t0=t0: (
                               lambda t, ch=i: t.matmul(
                                   psu[:, :],
                                   sb_wsl[:, buf + 4096 + ch * 256 + hc * 128: buf + 4096 + ch * 256 + hc * 128 + 128],
                                   xnch(ch, t0), start=(ch == 0), stop=(ch == NCH - 1))),
                           lambda i, buf=buf, hc=hc: _slots("wsl", buf + 4096 + i * 256 + hc * 128, 128) + [xnk(i)])
                    dst = bbv(0, 512) if tile == 0 else bbv(512, 512)
                    dk = bK(0, 512) if tile == 0 else bK(512, 512)
                    S.op("se", lambda s, psa=psa, dst=dst: s.activation(dst, psa[:, :], AF.Silu),
                         reads=[pga], writes=dk)
                    scol = SH0 + (i2 * 2 + tile) * 512
                    S.op("ve", lambda v, scol=scol, psu=psu, dst=dst: v.tensor_mul(
                        wk(scol, 512), dst, psu[:, :]),
                        reads=dk + [pgu], writes=wkK(scol, 512))
        bufs = [0, 8192]
        stream_w(swd, 0, 512, 0)
        for sl4 in range(4):
            if sl4 < 3:
                stream_w(swd, (sl4 + 1) * 512, 512, bufs[(sl4 + 1) % 2])
            buf = bufs[sl4 % 2]
            for esub in range(4):
                do = sl4 * 4 + esub
                for tile in range(2):
                    t0 = tile * 512
                    pk = "PA" if tile == 0 else "PB"
                    ps = PS[pk]
                    accgrp(pk, NCH,
                           lambda i, buf=buf, esub=esub, ps=ps, tile=tile: (
                               lambda t, i2=i: t.matmul(
                                   ps[:, :],
                                   sb_wsl[:, buf + i2 * 512 + esub * 128: buf + i2 * 512 + esub * 128 + 128],
                                   wk(SH0 + (i2 * 2 + tile) * 512, 512),
                                   start=(i2 == 0), stop=(i2 == NCH - 1))),
                           lambda i, buf=buf, esub=esub, tile=tile: (
                               _slots("wsl", buf + i * 512 + esub * 128, 128) +
                               wkK(SH0 + (i * 2 + tile) * 512, 512)))
                    S.op("ve", lambda v, do=do, t0=t0, ps=ps: v.scalar_tensor_tensor(
                        xch(do, t0), ps[:, :], 1.0, xch(do, t0), ALU.mult, ALU.add),
                        reads=[pk, xk(do)], writes=[xk(do)])

        if "moe" in tap:
            dump_tap("moe", lambda ch: xch(ch, 0, TL), lambda ch: [xk(ch)])

        S.op("ge", lambda g: g.dma_start(
            out=sb_rowb[0:1, 1040:1168],
            in_=bass.AP(ag2_out, 2048 * D, [[AGROWS * D, 8], [1, 16]])),
            reads=["ag2o"], writes=["kmg"], sem="gd", amt=16)
        S.op("ve", lambda v: v.tensor_copy(sb_km[0:1, 64:80], sb_rowb[0:1, 1040:1056]),
             reads=["kmg"], writes=["km2"])
        for r in range(1, 8):
            S.op("ve", lambda v, r=r: v.tensor_max(
                sb_km[0:1, 64:80], sb_km[0:1, 64:80], sb_rowb[0:1, 1040 + r * 16:1056 + r * 16]),
                reads=["kmg", "km2"], writes=["km2"])
        S.op("se", lambda s: s.activation(sb_km[0:1, 80:96], sb_km[0:1, 64:80], AF.Sqrt),
             reads=["km2"], writes=["km"])

        for hg in range(8):
            h0 = hg * 2
            S.op("sy", lambda s, h0=h0: s.dma_start(
                out=sb_b[:, 0:2048].rearrange("p (h f) -> p h f", f=TL),
                in_=bass.AP(q_scr, h0 * 128 * TL, [[TL, 128], [128 * TL, 2], [1, TL]])),
                reads=["qscr"], writes=bK(0, 2048), sem="sd", amt=16)
            for h in range(2):
                S.op("ve", lambda v, h=h: v.tensor_mul(
                    bbv(2048, 1024), sb_b[:, h * TL:(h + 1) * TL], sb_b[:, h * TL:(h + 1) * TL]),
                    reads=bK(h * TL, TL), writes=bK(2048, 1024))
                for half in range(2):
                    S.op("te", lambda t, half=half: t.matmul(
                        PH[0:1, :], sb_ones[:, :], bbv(2048 + half * 512, 512),
                        start=True, stop=True),
                        reads=bK(2048 + half * 512, 512) + ["ones"], writes=["PH"])
                    S.op("ve", lambda v, h=h, half=half: v.reduce_max(
                        sb_km[0:1, 96 + h * 2 + half:97 + h * 2 + half], PH[0:1, :], axis=AX.X),
                        reads=["PH"], writes=[f"qn{h}_{half}"])
                S.op("ve", lambda v, h=h: v.tensor_max(
                    sb_km[0:1, 104 + h:105 + h], sb_km[0:1, 96 + h * 2:97 + h * 2],
                    sb_km[0:1, 97 + h * 2:98 + h * 2]),
                    reads=[f"qn{h}_0", f"qn{h}_1"], writes=[f"qnm{h}"])
            S.op("se", lambda s: s.activation(sb_km[0:1, 108:110], sb_km[0:1, 104:106], AF.Sqrt),
                 reads=["qnm0", "qnm1"], writes=["qm"])
            S.op("ve", lambda v, h0=h0: v.tensor_mul(
                sb_km[0:1, 112:114], sb_km[0:1, 108:110], sb_km[0:1, 80 + h0:82 + h0]),
                reads=["qm", "km"], writes=["Bh"])
            S.op("ve", lambda v: v.tensor_scalar_mul(sb_km[0:1, 116:118], sb_km[0:1, 112:114], -1.0),
                 reads=["Bh"], writes=["nB"])

            for qt in ("B", "A"):
                nun = QB_UNITS if qt == "B" else QA_UNITS
                qoff = 0 if qt == "A" else 512
                for h in range(2):
                    S.op("se", lambda s, h=h, h0=h0, qoff=qoff: s.activation(
                        sb_row[0:1, 0:512], sb_iotag[0:1, qoff:qoff + 512], AF.Identity,
                        scale=-SLOPES[h0 + h], bias=sb_km[0:1, 116 + h:117 + h]),
                        reads=["iotag", "nB"], writes=["rw0"])
                    S.op("ve", lambda v, h=h: v.tensor_copy(
                        sb_rowb[0:1, h * 512:(h + 1) * 512], sb_row[0:1, 0:512]),
                        reads=["rw0"], writes=[f"rb{h}"])
                units = [(kvb, False) for kvb in range(nun)] + [(None, True)]
                NU = len(units)
                its = []
                for ui, (kvb, diag) in enumerate(units):
                    kcol = AK0 + (ui % 3) * 1024
                    vcol = AV0 + (ui % 3) * 1024
                    for h in range(2):
                        for j in range(4):
                            if diag:
                                jc = (h0 + h) * JCOLS + (QB_UNITS + QA_UNITS + (0 if qt == "B" else 1)) * 4 + j
                            else:
                                u = kvb if qt == "B" else QB_UNITS + kvb
                                jc = (h0 + h) * JCOLS + u * 4 + j
                            its.append(dict(ui=ui, kvb=kvb, diag=diag, h=h, j=j,
                                            kcol=kcol, vcol=vcol, jc=jc,
                                            st=(ui == 0 and j == 0),
                                            sp=(diag and j == 3)))

                def unit_dma(ui):
                    kvb, diag = units[ui]
                    kcol = AK0 + (ui % 3) * 1024
                    vcol = AV0 + (ui % 3) * 1024
                    if diag:
                        tloc = 0 if qt == "A" else 512
                        src_, skey, base = ag2_in, ["ag2k", "ag2v"], 0
                    else:
                        rr = kvb if kvb < 8 else 15 - kvb
                        tloc = 0 if kvb < 8 else 512
                        src_, skey, base = ag2_out, ["ag2o"], rr * AGROWS * D
                    S.op("sy", lambda s, kcol=kcol, base=base, tloc=tloc, h0=h0, src_=src_: s.dma_start(
                        out=wk(kcol, 1024).rearrange("p (h f) -> p h f", f=512),
                        in_=bass.AP(src_, base + h0 * 128 * TL + tloc,
                                    [[TL, 128], [128 * TL, 2], [1, 512]])),
                        reads=skey, writes=wkK(kcol, 1024), sem="sd", amt=16)
                    for hh in range(2):
                        S.op("sy", lambda s, vcol=vcol, base=base, tloc=tloc, h0=h0, src_=src_, hh=hh: s.dma_start(
                            out=wk(vcol + hh * 512, 512).rearrange("p (j f) -> p j f", f=128),
                            in_=bass.AP(src_, base + (1024 + tloc) * D + (h0 + hh) * 128,
                                        [[D, 128], [128 * D, 4], [1, 128]])),
                            reads=skey, writes=wkK(vcol + hh * 512, 512), sem="sd", amt=16)

                def emit_RS(idx):
                    it = its[idx]
                    spk = "PA" if idx % 2 == 0 else "PB"
                    sps = PS[spk]
                    h, j, kcol = it["h"], it["j"], it["kcol"]
                    S.op("te", lambda t, h=h, sps=sps: t.matmul(
                        sps[:, :], sb_onesr[:, :], sb_rowb[0:1, h * 512:(h + 1) * 512],
                        start=True, stop=False),
                        reads=[f"rb{h}", "onesr"], writes=[spk])
                    S.op("te", lambda t, h=h, j=j, sps=sps, kcol=kcol, qoff=qoff: t.matmul(
                        sps[:, :], sb_wk[:, kcol + h * 512 + j * 128: kcol + h * 512 + j * 128 + 128],
                        sb_b[:, h * TL + qoff: h * TL + qoff + 512],
                        start=False, stop=True),
                        reads=wkK(kcol + h * 512 + j * 128, 128) + bK(h * TL + qoff, 512),
                        writes=[spk])

                SKEW = 1
                unit_dma(0)
                unit_dma(1)
                if NU > 2:
                    unit_dma(2)
                for pre in range(SKEW):
                    emit_RS(pre)
                for idx, it in enumerate(its):
                    h, j = it["h"], it["j"]
                    spk = "PA" if idx % 2 == 0 else "PB"
                    sps = PS[spk]
                    pcol = AP0 + (idx % 4) * 512
                    if it["diag"]:
                        tcol = 1024 + (idx % 4) * 512
                        S.op("ve", lambda v, sps=sps, tcol=tcol, j=j: v.tensor_add(
                            fbv(tcol, 512), sps[:, :], sb_mext[:, 384 - j * 128:896 - j * 128]),
                            reads=[spk, "mext"], writes=fK(tcol, 512))
                        S.op("se", lambda s, tcol=tcol, pcol=pcol, jc=it["jc"]: s.activation(
                            wk(pcol, 512), fbv(tcol, 512), AF.Exp, bias=sb_jtab[:, jc:jc + 1]),
                            reads=fK(tcol, 512) + ["jtab"], writes=wkK(pcol, 512))
                    else:
                        S.op("se", lambda s, sps=sps, pcol=pcol, jc=it["jc"]: s.activation(
                            wk(pcol, 512), sps[:, :], AF.Exp, bias=sb_jtab[:, jc:jc + 1]),
                            reads=[spk, "jtab"], writes=wkK(pcol, 512))
                    lbank = "PE2" if h == 0 else "PE1"
                    lps = PS[lbank]
                    obank = "PC" if h == 0 else "PD"
                    ops_ = PS[obank]
                    st, sp = it["st"], it["sp"]
                    S.op("te", lambda t, pcol=pcol, lps=lps, st=st, sp=sp: t.matmul(
                        lps[0:1, :], sb_ones[:, :], wk(pcol, 512), start=st, stop=sp),
                        reads=wkK(pcol, 512) + ["ones"],
                        writes=[lbank] if (st or sp) else [])
                    vcol = it["vcol"]
                    S.op("te", lambda t, pcol=pcol, vcol=vcol, h=h, j=j, ops_=ops_, st=st, sp=sp: t.matmul(
                        ops_[:, :], sb_wk[:, vcol + h * 512 + j * 128: vcol + h * 512 + j * 128 + 128],
                        wk(pcol, 512), start=st, stop=sp),
                        reads=wkK(vcol + h * 512 + j * 128, 128) + wkK(pcol, 512),
                        writes=[obank] if (st or sp) else [])
                    nxt = idx + SKEW
                    if nxt < len(its):
                        if its[nxt]["ui"] != its[nxt - 1]["ui"] and its[nxt]["ui"] + 2 < NU:
                            unit_dma(its[nxt]["ui"] + 2)
                        emit_RS(nxt)
                for h in range(2):
                    obank = "PC" if h == 0 else "PD"
                    ops_ = PS[obank]
                    lbank = "PE2" if h == 0 else "PE1"
                    lps = PS[lbank]
                    S.op("ve", lambda v, lps=lps: v.reciprocal(
                        sb_f[0:1, 1024:1536], lps[0:1, :]),
                        reads=[lbank], writes=["linv"])
                    S.op("te", lambda t: t.matmul(
                        PA[:, :], sb_onesrf[0:1, :], sb_f[0:1, 1024:1536],
                        start=True, stop=True),
                        reads=["onesrf", "linv"], writes=["PA"])
                    S.op("ve", lambda v: v.tensor_copy(fbv(1536, 512), PA[:, :]),
                         reads=["PA"], writes=fK(1536, 512))
                    S.op("ve", lambda v, h=h, h0=h0, qoff=qoff, ops_=ops_: v.tensor_mul(
                        wk(AO0 + (h0 + h) * TL + qoff, 512), fbv(1536, 512), ops_[:, :]),
                        reads=fK(1536, 512) + [obank],
                        writes=wkK(AO0 + (h0 + h) * TL + qoff, 512))

        def evac_wo(e, tile, ps, pk):
            t0 = tile * 512
            S.op("ve", lambda v, e=e, t0=t0, ps=ps: v.scalar_tensor_tensor(
                xch(e, t0), ps[:, :], vc("bo", e), xch(e, t0), ALU.add, ALU.add),
                reads=[pk, "vcn", xk(e)], writes=[xk(e)])

        proj2048(woT, evac_wo,
                 rhs=lambda ch, c0: wk(AO0 + ch * TL + c0, 512),
                 rhs_keys=lambda ch: wkK(AO0 + ch * TL, TL))

        if "attn" in tap:
            dump_tap("attn", lambda ch: wk(AO0 + ch * TL, TL),
                     lambda ch: wkK(AO0 + ch * TL, TL))

        outks = []
        for ch in range(NCH):
            c0 = 0 if ch % 2 == 0 else 1024
            S.op("ve", lambda v, ch=ch, c0=c0: v.tensor_copy(fbv(c0, 1024), xch(ch, 0, TL)),
                 reads=[xk(ch)], writes=fK(c0, 1024))
            S.op("sy", lambda s, ch=ch, c0=c0: s.dma_start(
                out=outT[ch * 128:(ch + 1) * 128, :], in_=fbv(c0, 1024)),
                reads=fK(c0, 1024), writes=[f"out{ch}"], sem="sd", amt=16)
            outks.append(f"out{ch}")
        S.wait_keys("sy", outks)
        S.wait_keys("ge", outks)

        @block.tensor
        def _(h):
            for f in S.fns["te"]:
                f(h)

        @block.vector
        def _(h):
            for f in S.fns["ve"]:
                f(h)

        @block.scalar
        def _(h):
            for f in S.fns["se"]:
                f(h)

        @block.gpsimd
        def _(h):
            for f in S.fns["ge"]:
                f(h)

        @block.sync
        def _(h):
            for f in S.fns["sy"]:
                f(h)

    return nc


def _host_prep(inputs):
    f32 = np.float32
    bf16 = ml_dtypes.bfloat16
    A = {k: np.asarray(v) for k, v in inputs.items()}
    x = np.ascontiguousarray(A["L0_attn_u"][0].T.astype(f32))

    def tr(a):
        return np.ascontiguousarray(np.asarray(a).T.astype(f32))

    shared = dict(
        gwT=tr(A["L0_attn_gate_w"]), w1T=tr(A["L0_ffn_W1"]), w3T=tr(A["L0_ffn_W3"]),
        w2T=tr(A["L0_ffn_W2"]), wqT=tr(A["L1_attn_Wq"]), wkT=tr(A["L1_attn_Wk"]),
        wvT=tr(A["L1_attn_Wv"]), woT=tr(A["L1_attn_Wo"]),
        wrt=np.ascontiguousarray(A["L1_ffn_w_router"].astype(f32)),
        wga=np.ascontiguousarray(A["L1_ffn_w_gate"].astype(f32)),
        wua=np.ascontiguousarray(A["L1_ffn_w_up"].astype(f32)),
        wda=np.ascontiguousarray(A["L1_ffn_w_down"].astype(f32)),
        swg=np.ascontiguousarray(A["L1_ffn_shared_w_gate"].astype(f32)),
        swu=np.ascontiguousarray(A["L1_ffn_shared_w_up"].astype(f32)),
        swd=np.ascontiguousarray(A["L1_ffn_shared_w_down"].astype(f32)),
        dlru=np.ascontiguousarray(A["L0_attn_d_lr_u"].astype(f32)),
        dlrv=np.ascontiguousarray(A["L0_attn_d_lr_v"].astype(f32)),
    )
    vcons = np.zeros((128, 164), f32)

    def pack(col, vec):
        vcons[:, col:col + 16] = np.asarray(vec, f32).reshape(16, 128).T

    pack(0, A["L0_attn_a"]); pack(16, A["L0_attn_b"]); pack(32, A["L0_attn_c"])
    pack(48, A["L0_attn_d"]); pack(64, A["L0_attn_gate_b"])
    pack(80, A["L1_attn_bq"]); pack(96, A["L1_attn_bk"]); pack(112, A["L1_attn_bv"])
    pack(128, A["L1_attn_bo"])
    vcons[:, 144] = np.asarray(A["L1_ffn_expert_scale"], f32)
    vcons[:, 146:162] = np.asarray(A["L1_ffn_amplitudes"], f32).reshape(128, 16)
    shared["vcons"] = vcons

    shared["ident"] = np.eye(128, dtype=bf16)
    shared["h128"] = _hadamard(128).astype(bf16)
    mext = np.full((128, 896), NEG, f32)
    for p in range(128):
        mext[p, 384 + p:] = 0.0
    shared["mext"] = mext.astype(bf16)
    selrow = np.zeros((8, 1024), f32)
    for k in range(8):
        selrow[k, k * 128:(k + 1) * 128] = 1.0
    shared["selrow"] = selrow.astype(bf16)

    in_maps = []
    for c in range(NCORES):
        blkA, blkB = c, 15 - c
        cols = np.r_[blkA * 512:(blkA + 1) * 512, blkB * 512:(blkB + 1) * 512]
        m = dict(shared)
        m["xT"] = np.ascontiguousarray(x[:, cols])
        m["iotag"] = cols.astype(bf16).reshape(1, TL)
        sel = np.zeros((128, 32), f32)
        sel[:, blkA] = 1.0
        sel[:, 16 + blkB] = 1.0
        m["selAB"] = sel
        jt = np.full((128, HEADS * JCOLS), NEG, f32)
        p = np.arange(128, dtype=f32)
        for h in range(HEADS):
            s = SLOPES[h]
            base = h * JCOLS
            for kvb in range(QB_UNITS):
                if kvb <= 14 - c:
                    for j in range(4):
                        jt[:, base + kvb * 4 + j] = s * (512 * kvb + 128 * j + p)
            for kvb in range(QA_UNITS):
                if kvb <= c - 1:
                    for j in range(4):
                        jt[:, base + (QB_UNITS + kvb) * 4 + j] = s * (512 * kvb + 128 * j + p)
            for j in range(4):
                jt[:, base + (QB_UNITS + QA_UNITS) * 4 + j] = s * (512 * blkB + 128 * j + p)
                jt[:, base + (QB_UNITS + QA_UNITS + 1) * 4 + j] = s * (512 * blkA + 128 * j + p)
        m["jtab"] = jt
        in_maps.append(m)
    return in_maps


def kernel(_tap=(), _strict_dma=True, **inputs):
    in_maps = _host_prep(inputs)
    nc = build(tap=_tap, strict_dma=_strict_dma)
    res = run_bass_kernel_spmd(nc, in_maps, core_ids=list(range(NCORES)))
    out = np.zeros((D, T), np.float32)
    for c in range(NCORES):
        r = res.results[c]
        blkA, blkB = c, 15 - c
        out[:, blkA * 512:(blkA + 1) * 512] = r["outT"][:, 0:512]
        out[:, blkB * 512:(blkB + 1) * 512] = r["outT"][:, 512:1024]
    final = np.ascontiguousarray(out.T)[None, :, :]
    if _tap:
        return final, [res.results[c] for c in range(NCORES)]
    return final


# version 08
# speedup vs baseline: 1.2407x; 1.2407x over previous
import numpy as np
import ml_dtypes
from contextlib import ExitStack

import concourse.bass as bass
import concourse.mybir as mybir
from concourse.bass_utils import run_bass_kernel_spmd

F32 = mybir.dt.float32
BF16 = mybir.dt.bfloat16
AF = mybir.ActivationFunctionType
ALU = mybir.AluOpType
AX = mybir.AxisListType

NCORES = 8
T = 8192
D = 2048
NCH = 16
TL = 1024
NBLK = 16
HEADS = 16
HMID = 8192
INV_SQRT_DH = float(1.0 / np.sqrt(128.0))
FWHT_SCALE = float(1.0 / np.sqrt(8192.0))
SLOPES = [2.0 ** (-0.5 * (h + 1)) for h in range(HEADS)]
NEG = -1e30
AGROWS = 2056
QB_UNITS = 15
QA_UNITS = 7
JCOLS = (QB_UNITS + QA_UNITS + 2) * 4

ENG2SEM = {"te": "te", "ve": "ve", "se": "se", "ge": "gc", "sy": "syc"}
NDMASEM = 8
DMASEMS = [f"gd{i}" for i in range(NDMASEM)] + [f"sd{i}" for i in range(NDMASEM)]
SEMNAMES = ["te", "ve", "se", "gc", "syc", "cc"] + DMASEMS

UG0 = 8192
WR0 = 0
MX0 = 8192
SH0 = 0
AK0 = 16384
AV0 = 18432
AP0 = 20480
AO0 = 0


class Sched:
    def __init__(self, strict_dma=False):
        self.strict_dma = strict_dma
        self.fns = {e: [] for e in ENG2SEM}
        self.cnt = {s: 0 for s in SEMNAMES}
        self.rr = {"gd": 0, "sd": 0}
        self.reg = {}
        self.lastw = {e: {} for e in ENG2SEM}
        self.sems = None

    def _need(self, reads, writes):
        need = {}
        for k in reads:
            r = self.reg.get(k)
            if r:
                for s, v in r["w"].items():
                    need[s] = max(need.get(s, 0), v)
        for k in writes:
            r = self.reg.get(k)
            if r:
                for s, v in r["w"].items():
                    need[s] = max(need.get(s, 0), v)
                for s, v in r["r"].items():
                    need[s] = max(need.get(s, 0), v)
        return need

    def op(self, eng, fn, reads=(), writes=(), sem=None, amt=1):
        if sem in ("gd", "sd"):
            q = sem
            sem = f"{q}{self.rr[q]}"
            self.rr[q] = (self.rr[q] + 1) % NDMASEM
        else:
            sem = sem or ENG2SEM[eng]
        need = self._need(reads, writes)
        if amt == 16:
            need[sem] = max(need.get(sem, 0), self.cnt[sem])
        lw = self.lastw[eng]
        waits = []
        for s, v in sorted(need.items()):
            if eng == "te" and s == "te":
                continue
            if lw.get(s, 0) < v:
                waits.append((s, v))
                lw[s] = v
        self.cnt[sem] += amt
        val = self.cnt[sem]
        for k in writes:
            self.reg[k] = {"w": {sem: val}, "r": {}}
        for k in reads:
            r = self.reg.setdefault(k, {"w": {}, "r": {}})
            r["r"][sem] = max(r["r"].get(sem, 0), val)

        def closure(h, _w=tuple(waits), _fn=fn, _sem=sem, _amt=amt):
            for s, v in _w:
                h.wait_ge(self.sems[s], v)
            _fn(h).then_inc(self.sems[_sem], _amt)

        self.fns[eng].append(closure)
        return (sem, val)

    def wait_keys(self, eng, keys):
        need = self._need(keys, ())
        lw = self.lastw[eng]
        waits = []
        for s, v in sorted(need.items()):
            if lw.get(s, 0) < v:
                waits.append((s, v))
                lw[s] = v

        def closure(h, _w=tuple(waits)):
            for s, v in _w:
                h.wait_ge(self.sems[s], v)

        self.fns[eng].append(closure)


def _hadamard(n):
    h = np.array([[1]], dtype=np.int64)
    while h.shape[0] < n:
        h = np.block([[h, h], [h, -h]])
    return h


def _slots(buf, c0, n):
    return [f"{buf}{s}" for s in range(c0 // 512, (c0 + n + 511) // 512)]


def build(tap=(), strict_dma=False):
    tap = set(tap)
    nc = bass.Bass("TRN2", target_bir_lowering=False)
    dt = nc.dram_tensor

    xT = dt("xT", [D, TL], F32, kind="ExternalInput").ap()
    iotag_i = dt("iotag", [1, TL], BF16, kind="ExternalInput").ap()
    selAB_i = dt("selAB", [128, 32], F32, kind="ExternalInput").ap()
    jtab_i = dt("jtab", [128, HEADS * JCOLS], F32, kind="ExternalInput").ap()
    outT = dt("outT", [D, TL], F32, kind="ExternalOutput").ap()

    gwT = dt("gwT", [D, D], F32, kind="ExternalInput").ap()
    w1T = dt("w1T", [D, HMID], F32, kind="ExternalInput").ap()
    w3T = dt("w3T", [D, HMID], F32, kind="ExternalInput").ap()
    w2T = dt("w2T", [HMID, D], F32, kind="ExternalInput").ap()
    wqT = dt("wqT", [D, D], F32, kind="ExternalInput").ap()
    wkT = dt("wkT", [D, D], F32, kind="ExternalInput").ap()
    wvT = dt("wvT", [D, D], F32, kind="ExternalInput").ap()
    woT = dt("woT", [D, D], F32, kind="ExternalInput").ap()
    wrt = dt("wrt", [D, 128], F32, kind="ExternalInput").ap()
    wga = dt("wga", [D, HMID], F32, kind="ExternalInput").ap()
    wua = dt("wua", [D, HMID], F32, kind="ExternalInput").ap()
    wda = dt("wda", [1024, D], F32, kind="ExternalInput").ap()
    swg = dt("swg", [D, D], F32, kind="ExternalInput").ap()
    swu = dt("swu", [D, D], F32, kind="ExternalInput").ap()
    swd = dt("swd", [D, D], F32, kind="ExternalInput").ap()
    dlru_i = dt("dlru", [D, 4], F32, kind="ExternalInput").ap()
    dlrv_i = dt("dlrv", [4, D], F32, kind="ExternalInput").ap()
    vcons_i = dt("vcons", [128, 164], F32, kind="ExternalInput").ap()
    ident_i = dt("ident", [128, 128], BF16, kind="ExternalInput").ap()
    h128_i = dt("h128", [128, 128], BF16, kind="ExternalInput").ap()
    mext_i = dt("mext", [128, 896], BF16, kind="ExternalInput").ap()
    selr_i = dt("selrow", [8, 1024], BF16, kind="ExternalInput").ap()

    ag1_in = dt("ag1_in", [3, D], F32)
    ag1_out = dt("ag1_out", [3 * NCORES, D], F32)
    ag2_in = dt("ag2_in", [AGROWS, D], BF16)
    ag2_out = dt("ag2_out", [AGROWS * NCORES, D], BF16)
    q_scr = dt("q_scr", [D, TL], BF16)

    taps = {}
    for name in ["xn0", "xl0", "xn1", "moe", "attn", "xa0"]:
        if name in tap:
            taps[name] = dt("t_" + name, [D, TL], F32, kind="ExternalOutput").ap()
    if "adbg" in tap:
        taps["adbg"] = dt("t_adbg", [128, 6144], F32, kind="ExternalOutput").ap()

    S = Sched(strict_dma=strict_dma)

    with ExitStack() as ctx:
        en = ctx.enter_context
        sb_x = en(nc.sbuf_tensor("sb_x", [128, NCH * TL], BF16))
        sb_xn = en(nc.sbuf_tensor("sb_xn", [128, NCH * TL], BF16))
        sb_wk = en(nc.sbuf_tensor("sb_wk", [128, 24576], BF16))
        sb_wsl = en(nc.sbuf_tensor("sb_wsl", [128, 16384], BF16))
        sb_f = en(nc.sbuf_tensor("sb_f", [128, 3072], F32))
        sb_b = en(nc.sbuf_tensor("sb_b", [128, 4096], BF16))
        sb_v = en(nc.sbuf_tensor("sb_v", [128, 256], F32))
        sb_s = en(nc.sbuf_tensor("sb_s", [8, 512], F32))
        sb_row = en(nc.sbuf_tensor("sb_row", [1, 1024], F32))
        sb_rowb = en(nc.sbuf_tensor("sb_rowb", [1, 1280], BF16))
        sb_cr = en(nc.sbuf_tensor("sb_cr", [128, 464], F32))
        sb_km = en(nc.sbuf_tensor("sb_km", [1, 144], F32))
        sb_id = en(nc.sbuf_tensor("sb_id", [128, 128], BF16))
        sb_h128 = en(nc.sbuf_tensor("sb_h128", [128, 128], BF16))
        sb_mext = en(nc.sbuf_tensor("sb_mext", [128, 896], BF16))
        sb_jtab = en(nc.sbuf_tensor("sb_jtab", [128, HEADS * JCOLS], F32))
        sb_iotag = en(nc.sbuf_tensor("sb_iotag", [1, TL], BF16))
        sb_vc = en(nc.sbuf_tensor("sb_vc", [128, 164], F32))
        sb_sel = en(nc.sbuf_tensor("sb_sel", [128, 32], F32))
        sb_selr = en(nc.sbuf_tensor("sb_selr", [8, 1024], BF16))
        sb_onesrf = en(nc.sbuf_tensor("sb_onesrf", [33, 128], F32))
        sb_ones = en(nc.sbuf_tensor("sb_ones", [128, 1], BF16))
        sb_onesr = en(nc.sbuf_tensor("sb_onesr", [1, 128], BF16))
        sb_dlru = en(nc.sbuf_tensor("sb_dlru", [128, 64], BF16))
        sb_d4 = en(nc.sbuf_tensor("sb_d4", [4, 1], BF16))
        sb_umb = en(nc.sbuf_tensor("sb_umb", [128, 16], BF16))

        PA = en(nc.psum_tensor("PA", [128, 512], F32))
        PB = en(nc.psum_tensor("PB", [128, 512], F32))
        PC = en(nc.psum_tensor("PC", [128, 512], F32))
        PD = en(nc.psum_tensor("PD", [128, 512], F32))
        PE1 = en(nc.psum_tensor("PE1", [128, 512], F32))
        PE2 = en(nc.psum_tensor("PE2", [128, 512], F32))
        PG = en(nc.psum_tensor("PG", [128, 1024], BF16))
        PH = en(nc.psum_tensor("PH", [8, 512], F32))
        PS = {"PA": PA, "PB": PB, "PC": PC, "PD": PD, "PE1": PE1, "PE2": PE2}

        sems = {}
        for s in SEMNAMES:
            sems[s] = en(nc.semaphore(s))
        S.sems = sems
        block = en(nc.Block())

        VCOFF = {"a": 0, "b": 16, "c": 32, "d": 48, "gb": 64, "bq": 80,
                 "bk": 96, "bv": 112, "bo": 128, "esc": 144, "amps": 146}

        def vc(name, ch):
            o = VCOFF[name]
            return sb_vc[:, o + ch: o + ch + 1]

        VOFF = {"aa": 0, "lnaa": 16, "aa512": 32, "bqs": 48, "dlr": 64,
                "carryA": 80, "carryB": 96, "cA": 112, "cB": 128,
                "umean": 144, "chain": 160, "mixp": 176, "tmp": 192}

        def vsl(name, n=16):
            o = VOFF[name]
            return sb_v[:, o: o + n]

        def vch(name, ch):
            o = VOFF[name]
            return sb_v[:, o + ch: o + ch + 1]

        def xch(ch, c0=0, n=512):
            return sb_x[:, ch * TL + c0: ch * TL + c0 + n]

        def xk(ch):
            return f"x{ch}"

        def xnch(ch, c0=0, n=512):
            return sb_xn[:, ch * TL + c0: ch * TL + c0 + n]

        def xnk(ch):
            return f"xn{ch}"

        def wk(c0, n):
            return sb_wk[:, c0:c0 + n]

        def wkK(c0, n):
            return _slots("wk", c0, n)

        def fbv(c0, n):
            return sb_f[:, c0:c0 + n]

        def fK(c0, n):
            return _slots("f", c0, n)

        def bbv(c0, n):
            return sb_b[:, c0:c0 + n]

        def bK(c0, n):
            return _slots("b", c0, n)

        def wslK(c0, n):
            return _slots("wsl", c0, n)

        def ug(ch, c0=0, n=512):
            return wk(UG0 + ch * TL + c0, n)

        def ugK(ch, c0=0, n=512):
            return wkK(UG0 + ch * TL + c0, n)

        S.op("ge", lambda g: g.dma_start(
            out=sb_x[:, :].rearrange("p (c f) -> p c f", f=TL),
            in_=xT.rearrange("(c p) f -> p c f", p=128)),
            writes=[xk(c) for c in range(NCH)], sem="gd", amt=16)
        for ap_, dst, wr in [(ident_i, sb_id, "ident"), (h128_i, sb_h128, "h128"),
                             (mext_i, sb_mext, "mext"), (jtab_i, sb_jtab, "jtab"),
                             (iotag_i, sb_iotag, "iotag"),
                             (vcons_i, sb_vc, "vcn"), (selAB_i, sb_sel, "sel"),
                             (selr_i, sb_selr, "selr")]:
            S.op("ge", lambda g, a=ap_, d=dst: g.dma_start(out=d[:, :], in_=a),
                 writes=[wr], sem="gd", amt=16)
        S.op("ge", lambda g: g.dma_start(
            out=sb_dlru[:, :].rearrange("p (c f) -> p c f", f=4),
            in_=dlru_i.rearrange("(c p) f -> p c f", p=128)),
            writes=["dlru"], sem="gd", amt=16)
        S.op("ge", lambda g: g.memset(sb_b[0:8, 0:2048], 0.0), writes=bK(0, 2048))
        S.op("sy", lambda s: s.dma_start(out=ag2_in[2048:AGROWS, :], in_=sb_b[0:8, 0:2048]),
             reads=bK(0, 2048), writes=["ag2m"], sem="sd", amt=16)
        S.op("ge", lambda g: g.memset(sb_ones[:, :], 1.0), writes=["ones"])
        S.op("ge", lambda g: g.memset(sb_onesr[:, :], 1.0), writes=["onesr"])
        S.op("ge", lambda g: g.memset(sb_onesrf[:, :], 1.0), writes=["onesrf"])
        S.op("se", lambda s: s.activation(vsl("aa"), sb_vc[:, 0:16], AF.Tanh),
             reads=["vcn"], writes=["aa"])
        S.op("se", lambda s: s.activation(vsl("lnaa"), vsl("aa"), AF.Ln),
             reads=["aa"], writes=["lnaa"])
        S.op("se", lambda s: s.activation(vsl("aa512"), vsl("lnaa"), AF.Exp, scale=512.0),
             reads=["lnaa"], writes=["aa512"])
        S.op("ve", lambda v: v.tensor_scalar_mul(vsl("bqs"), sb_vc[:, 80:96], INV_SQRT_DH),
             reads=["vcn"], writes=["bqs"])
        S.op("ve", lambda v: v.tensor_mul(fbv(0, 16), sb_vc[:, 146:162], sb_vc[:, 146:162]),
             reads=["vcn"], writes=fK(0, 16))
        S.op("ve", lambda v: v.tensor_add(vsl("mixp", 8), sb_f[:, 0:16:2], sb_f[:, 1:17:2]),
             reads=fK(0, 16), writes=["mixp"])
        S.op("ve", lambda v: v.reduce_sum(vch("tmp", 0), vsl("mixp", 8), axis=AX.X),
             reads=["mixp"], writes=["tmp0"])
        S.op("ve", lambda v: v.tensor_scalar_add(vch("tmp", 0), vch("tmp", 0), 1e-8),
             reads=["tmp0"], writes=["tmp0"])
        S.op("ve", lambda v: v.reciprocal(vch("tmp", 1), vch("tmp", 0)),
             reads=["tmp0"], writes=["tmp1"])
        S.op("ve", lambda v: v.tensor_scalar_mul(vsl("mixp", 8), vsl("mixp", 8), vch("tmp", 1)),
             reads=["mixp", "tmp1"], writes=["mixp"])
        S.op("ve", lambda v: v.tensor_scalar_mul(vsl("mixp", 8), vsl("mixp", 8), vc("esc", 0)),
             reads=["mixp", "vcn"], writes=["mixp"])
        S.op("ve", lambda v: v.tensor_copy(sb_b[:, 4088:4096], vsl("mixp", 8)),
             reads=["mixp"], writes=["mixpb"])

        def stream_w(src_ap, col0, ncols, buf, nchunks=NCH, rows=D):
            S.op("ge", lambda g: g.dma_start(
                out=sb_wsl[:, buf:buf + nchunks * ncols].rearrange("p (c f) -> p c f", f=ncols),
                in_=src_ap[0:rows, col0:col0 + ncols].rearrange("(c p) f -> p c f", p=128)),
                writes=wslK(buf, nchunks * ncols), sem="gd", amt=16)

        def accgrp(pk, n, fn_i, reads_i):
            for i in range(n):
                S.op("te", fn_i(i), reads=reads_i(i),
                     writes=[pk] if (i == 0 or i == n - 1) else [])

        def ln_phase(layer):
            for tile in range(2):
                t0 = tile * 512
                accgrp("PH", NCH,
                       lambda i, t0=t0: (lambda t, ch=i: t.matmul(
                           PH[0:1, :], sb_ones[:, :], xch(ch, t0),
                           start=(ch == 0), stop=(ch == NCH - 1))),
                       lambda i: [xk(i), "ones"])
                for ch in range(NCH):
                    dst = bbv(0, 512) if ch % 2 == 0 else bbv(512, 512)
                    dk = bK(0, 512) if ch % 2 == 0 else bK(512, 512)
                    S.op("ve", lambda v, ch=ch, t0=t0, dst=dst: v.tensor_mul(
                        dst, xch(ch, t0), xch(ch, t0)),
                        reads=[xk(ch)], writes=dk)
                    S.op("te", lambda t, ch=ch, dst=dst: t.matmul(
                        PA[0:1, :], sb_ones[:, :], dst,
                        start=(ch == 0), stop=(ch == NCH - 1)),
                        reads=dk + ["ones"],
                        writes=["PA"] if (ch == 0 or ch == NCH - 1) else [])
                S.op("ve", lambda v: v.tensor_scalar_mul(sb_row[0:1, 0:512], PH[0:1, :], 1.0 / D),
                     reads=["PH"], writes=["rw0"])
                S.op("ve", lambda v: v.tensor_scalar_mul(sb_row[0:1, 512:1024], PA[0:1, :], 1.0 / D),
                     reads=["PA"], writes=["rw1"])
                S.op("ve", lambda v: v.tensor_mul(sb_rowb[0:1, 0:512], sb_row[0:1, 0:512],
                                                  sb_row[0:1, 0:512]),
                     reads=["rw0"], writes=["rwb0"])
                S.op("ve", lambda v: v.tensor_sub(sb_row[0:1, 512:1024], sb_row[0:1, 512:1024],
                                                  sb_rowb[0:1, 0:512]),
                     reads=["rw1", "rwb0"], writes=["rw1"])
                S.op("ve", lambda v: v.tensor_scalar_add(sb_row[0:1, 512:1024],
                                                         sb_row[0:1, 512:1024], 1e-5),
                     reads=["rw1"], writes=["rw1"])
                S.op("ve", lambda v: v.reciprocal(sb_row[0:1, 512:1024], sb_row[0:1, 512:1024]),
                     reads=["rw1"], writes=["rw1"])
                S.op("se", lambda s: s.activation(sb_row[0:1, 512:1024], sb_row[0:1, 512:1024],
                                                  AF.Sqrt),
                     reads=["rw1"], writes=["rw1"])
                S.op("ve", lambda v: v.tensor_mul(sb_row[0:1, 0:512], sb_row[0:1, 0:512],
                                                  sb_row[0:1, 512:1024]),
                     reads=["rw0", "rw1"], writes=["rw0"])
                S.op("ve", lambda v: v.tensor_copy(sb_rowb[0:1, 0:512], sb_row[0:1, 512:1024]),
                     reads=["rw1"], writes=["rwb0"])
                S.op("ve", lambda v: v.tensor_copy(sb_rowb[0:1, 512:1024], sb_row[0:1, 0:512]),
                     reads=["rw0"], writes=["rwb1"])
                S.op("te", lambda t: t.matmul(PE1[:, :], sb_onesr[:, :], sb_rowb[0:1, 0:512],
                                              start=True, stop=True),
                     reads=["rwb0", "onesr"], writes=["PE1"])
                S.op("te", lambda t: t.matmul(PE2[:, :], sb_onesr[:, :], sb_rowb[0:1, 512:1024],
                                              start=True, stop=True),
                     reads=["rwb1", "onesr"], writes=["PE2"])
                for ch in range(NCH):
                    dst = fbv(0, 512) if ch % 2 == 0 else fbv(512, 512)
                    dk = fK(0, 512) if ch % 2 == 0 else fK(512, 512)
                    S.op("ve", lambda v, ch=ch, t0=t0, dst=dst: v.tensor_mul(
                        dst, xch(ch, t0), PE1[:, :]),
                        reads=[xk(ch), "PE1"], writes=dk)
                    S.op("ve", lambda v, ch=ch, t0=t0, dst=dst: v.tensor_sub(
                        xnch(ch, t0), dst, PE2[:, :]),
                        reads=dk + ["PE2"], writes=[xnk(ch)])

        def proj2048(wap, evac, rhs=None, rhs_keys=None):
            rhs = rhs or (lambda ch, c0: xnch(ch, c0))
            rhs_keys = rhs_keys or (lambda ch: [xnk(ch)])
            bufs = [0, 8192]
            stream_w(wap, 0, 512, bufs[0])
            for sl in range(4):
                if sl < 3:
                    stream_w(wap, (sl + 1) * 512, 512, bufs[(sl + 1) % 2])
                buf = bufs[sl % 2]
                for esub in range(4):
                    e = sl * 4 + esub
                    for tile in range(2):
                        pk = "PA" if tile == 0 else "PB"
                        ps = PS[pk]
                        accgrp(pk, NCH,
                               lambda i, ps=ps, buf=buf, esub=esub, tile=tile: (
                                   lambda t, ch=i: t.matmul(
                                       ps[:, :],
                                       sb_wsl[:, buf + ch * 512 + esub * 128:
                                              buf + ch * 512 + esub * 128 + 128],
                                       rhs(ch, tile * 512),
                                       start=(ch == 0), stop=(ch == NCH - 1))),
                               lambda i, buf=buf, esub=esub: (
                                   _slots("wsl", buf + i * 512 + esub * 128, 128) + rhs_keys(i)))
                        evac(e, tile, ps, pk)

        ln_phase(0)

        def dump_tap(name, src, src_keys):
            for ch in range(NCH):
                S.op("ve", lambda v, ch=ch: v.tensor_copy(fbv(0, 1024), src(ch)),
                     reads=src_keys(ch), writes=fK(0, 1024))
                S.op("sy", lambda s, ch=ch: s.dma_start(
                    out=taps[name][ch * 128:(ch + 1) * 128, :], in_=fbv(0, 1024)),
                    reads=fK(0, 1024), writes=["dmp"], sem="sd", amt=16)

        if "xn0" in tap:
            dump_tap("xn0", lambda ch: xnch(ch, 0, TL), lambda ch: [xnk(ch)])

        def evac_gate(e, tile, ps, pk):
            t0 = tile * 512
            dst = bbv(0, 512) if tile == 0 else bbv(512, 512)
            dk = bK(0, 512) if tile == 0 else bK(512, 512)
            S.op("se", lambda s, e=e, ps=ps, dst=dst: s.activation(
                dst, ps[:, :], AF.Sigmoid, bias=vc("gb", e)),
                reads=[pk, "vcn"], writes=dk)
            S.op("ve", lambda v, e=e, t0=t0, dst=dst, tile=tile: v.scalar_tensor_tensor(
                ug(e, t0), dst, 1.0, xnch(e, t0), ALU.mult, ALU.mult,
                accum_out=sb_cr[:, 32 + e * 2 + tile: 33 + e * 2 + tile]),
                reads=dk + [xnk(e)], writes=ugK(e, t0) + [f"ugs{e}_{tile}"])

        proj2048(gwT, evac_gate)

        for ch in range(NCH):
            for tile in range(2):
                t0 = tile * 512
                buf_ = fbv(1024, 512) if tile == 0 else fbv(1536, 512)
                buK = fK(1024, 512) if tile == 0 else fK(1536, 512)
                ssf = fbv(2048, 512) if tile == 0 else fbv(2560, 512)
                ssK = fK(2048, 512) if tile == 0 else fK(2560, 512)
                S.op("ve", lambda v, ch=ch, t0=t0, buf_=buf_: v.tensor_scalar_mul(
                    buf_, ug(ch, t0), vc("b", ch)),
                    reads=ugK(ch, t0) + ["vcn"], writes=buK)
                S.op("ve", lambda v, ch=ch, buf_=buf_, ssf=ssf: v.tensor_tensor_scan(
                    ssf, vch("aa", ch).broadcast_to((128, 512)), buf_, 0.0, ALU.mult, ALU.add),
                    reads=buK + ["aa"], writes=ssK)
                S.op("ve", lambda v, ch=ch, tile=tile, ssf=ssf: v.tensor_copy(
                    sb_cr[:, tile * 16 + ch: tile * 16 + ch + 1], ssf[:, 511:512]),
                    reads=ssK, writes=[f"carr{tile}_{ch}"])
                S.op("ve", lambda v, ch=ch, t0=t0, ssf=ssf: v.scalar_tensor_tensor(
                    xch(ch, t0), ssf, vc("c", ch), xch(ch, t0), ALU.mult, ALU.add),
                    reads=ssK + ["vcn", xk(ch)], writes=[xk(ch)])
                S.op("ve", lambda v, ch=ch, t0=t0: v.scalar_tensor_tensor(
                    xch(ch, t0), ug(ch, t0), vc("d", ch), xch(ch, t0), ALU.mult, ALU.add),
                    reads=ugK(ch, t0) + ["vcn", xk(ch)], writes=[xk(ch)])
        for ch in range(NCH):
            S.op("ve", lambda v, ch=ch: v.tensor_add(
                sb_cr[:, 64 + ch: 65 + ch], sb_cr[:, 32 + ch * 2: 33 + ch * 2],
                sb_cr[:, 33 + ch * 2: 34 + ch * 2]),
                reads=[f"ugs{ch}_0", f"ugs{ch}_1"], writes=[f"ups{ch}"])

        S.op("sy", lambda s: s.dma_start(
            out=ag1_in[0, :].rearrange("(p c) -> p c", c=16), in_=sb_cr[:, 0:16]),
            reads=[f"carr0_{ch}" for ch in range(NCH)], writes=["ag1a"], sem="sd", amt=16)
        S.op("sy", lambda s: s.dma_start(
            out=ag1_in[1, :].rearrange("(p c) -> p c", c=16), in_=sb_cr[:, 16:32]),
            reads=[f"carr1_{ch}" for ch in range(NCH)], writes=["ag1b"], sem="sd", amt=16)
        S.op("sy", lambda s: s.dma_start(
            out=ag1_in[2, :].rearrange("(p c) -> p c", c=16), in_=sb_cr[:, 64:80]),
            reads=[f"ups{ch}" for ch in range(NCH)], writes=["ag1c"], sem="sd", amt=16)
        S.op("ge", lambda g: g.collective_compute(
            "AllGather", ALU.bypass, replica_groups=[list(range(NCORES))],
            ins=[ag1_in.ap()], outs=[ag1_out.ap()]),
            reads=["ag1a", "ag1b", "ag1c"], writes=["ag1o"], sem="cc", amt=1)
        S.op("ge", lambda g: g.dma_start(
            out=sb_cr[:, 80:464].rearrange("p (r c) -> p r c", c=16),
            in_=ag1_out.ap().rearrange("r (p c) -> p r c", c=16)),
            reads=["ag1o"], writes=["agm"], sem="gd", amt=16)

        def crow(j):
            r = j if j < 8 else 15 - j
            return 80 + (3 * r + (0 if j < 8 else 1)) * 16

        S.op("ge", lambda g: g.memset(vsl("chain"), 0.0), writes=["chain"])
        S.op("ge", lambda g: g.memset(vsl("carryA"), 0.0), writes=["carryA"])
        S.op("ge", lambda g: g.memset(vsl("carryB"), 0.0), writes=["carryB"])
        for j in range(NBLK):
            S.op("ve", lambda v, j=j: v.scalar_tensor_tensor(
                vsl("carryA"), vsl("chain"), sb_sel[:, j:j + 1], vsl("carryA"),
                ALU.mult, ALU.add),
                reads=["chain", "sel", "carryA"], writes=["carryA"])
            S.op("ve", lambda v, j=j: v.scalar_tensor_tensor(
                vsl("carryB"), vsl("chain"), sb_sel[:, 16 + j:17 + j], vsl("carryB"),
                ALU.mult, ALU.add),
                reads=["chain", "sel", "carryB"], writes=["carryB"])
            if j < NBLK - 1:
                S.op("ve", lambda v: v.tensor_mul(vsl("chain"), vsl("aa512"), vsl("chain")),
                     reads=["chain", "aa512"], writes=["chain"])
                S.op("ve", lambda v, j=j: v.tensor_add(
                    vsl("chain"), vsl("chain"), sb_cr[:, crow(j):crow(j) + 16]),
                    reads=["chain", "agm"], writes=["chain"])
        S.op("ve", lambda v: v.tensor_mul(vsl("cA"), vsl("carryA"), sb_vc[:, 32:48]),
             reads=["carryA", "vcn"], writes=["cA"])
        S.op("ve", lambda v: v.tensor_mul(vsl("cB"), vsl("carryB"), sb_vc[:, 32:48]),
             reads=["carryB", "vcn"], writes=["cB"])
        S.op("ve", lambda v: v.tensor_add(vsl("umean"), sb_cr[:, 80 + 2 * 16:80 + 3 * 16],
                                          sb_cr[:, 80 + 5 * 16:80 + 6 * 16]),
             reads=["agm"], writes=["umean"])
        for r in range(2, 8):
            S.op("ve", lambda v, r=r: v.tensor_add(
                vsl("umean"), vsl("umean"),
                sb_cr[:, 80 + (3 * r + 2) * 16:80 + (3 * r + 2) * 16 + 16]),
                reads=["agm", "umean"], writes=["umean"])
        S.op("ve", lambda v: v.tensor_scalar_mul(vsl("umean"), vsl("umean"), 1.0 / T),
             reads=["umean"], writes=["umean"])
        S.op("ve", lambda v: v.tensor_copy(sb_umb[:, :], vsl("umean")),
             reads=["umean"], writes=["umb"])
        accgrp("PH", NCH,
               lambda i: (lambda t, ch=i: t.matmul(
                   PH[0:4, 0:1], sb_dlru[:, ch * 4: ch * 4 + 4],
                   sb_umb[:, ch:ch + 1], start=(ch == 0), stop=(ch == NCH - 1))),
               lambda i: ["dlru", "umb"])
        S.op("se", lambda s: s.activation(sb_d4[:, :], PH[0:4, 0:1], AF.Identity, scale=0.25),
             reads=["PH"], writes=["d4"])
        S.op("ge", lambda g: g.dma_start(out=sb_b[0:4, 0:2048], in_=dlrv_i),
             writes=bK(0, 2048), sem="gd", amt=16)
        for e in range(NCH):
            S.op("te", lambda t, e=e: t.matmul(
                PB[:, e:e + 1], sb_b[0:4, e * 128:(e + 1) * 128],
                sb_d4[:, :], start=True, stop=True),
                reads=bK(e * 128, 128) + ["d4"], writes=["PB"] if (e == 0 or e == NCH - 1) else [])
        S.op("ve", lambda v: v.tensor_copy(vsl("dlr"), PB[:, 0:16]),
             reads=["PB"], writes=["dlr"])
        S.op("ge", lambda g: g.iota(fbv(1024, 512), [[1, 512]], channel_multiplier=0, allow_small_or_imprecise_dtypes=True),
             writes=fK(1024, 512))
        for ch in range(NCH):
            apd = fbv(0, 512) if ch % 2 == 0 else fbv(512, 512)
            apK = fK(0, 512) if ch % 2 == 0 else fK(512, 512)
            S.op("se", lambda s, ch=ch, apd=apd: s.activation(
                apd, fbv(1024, 512), AF.Exp, scale=vch("lnaa", ch), bias=vch("lnaa", ch)),
                reads=["lnaa"] + fK(1024, 512), writes=apK)
            for tile in range(2):
                t0 = tile * 512
                cXn = "cA" if tile == 0 else "cB"
                S.op("ve", lambda v, ch=ch, t0=t0, cXn=cXn, apd=apd: v.scalar_tensor_tensor(
                    xch(ch, t0), apd, vch(cXn, ch), xch(ch, t0), ALU.mult, ALU.add),
                    reads=apK + [cXn, xk(ch)], writes=[xk(ch)])
                S.op("ve", lambda v, ch=ch, t0=t0: v.scalar_tensor_tensor(
                    xch(ch, t0), ug(ch, t0), vch("dlr", ch), xch(ch, t0), ALU.mult, ALU.add),
                    reads=ugK(ch, t0) + ["dlr", xk(ch)], writes=[xk(ch)])

        if "xa0" in tap:
            dump_tap("xa0", lambda ch: xch(ch, 0, TL), lambda ch: [xk(ch)])

        def hid(alpha, n=512):
            if alpha < 48:
                return wk(alpha * 512, n)
            return sb_wsl[:, 8192 + (alpha - 48) * 512: 8192 + (alpha - 48) * 512 + n]

        def hidK(alpha):
            if alpha < 48:
                return wkK(alpha * 512, 512)
            return wslK(8192 + (alpha - 48) * 512, 512)

        for tile in range(2):
            t0 = tile * 512
            stream_w(w1T, 0, 128, 0)
            stream_w(w3T, 0, 128, 4096)
            for alpha in range(64):
                if alpha < 63:
                    stream_w(w1T, (alpha + 1) * 128, 128, ((alpha + 1) % 2) * 2048)
                    stream_w(w3T, (alpha + 1) * 128, 128, 4096 + ((alpha + 1) % 2) * 2048)
                buf = (alpha % 2) * 2048
                pga = "PA" if alpha % 2 == 0 else "PC"
                pgu = "PB" if alpha % 2 == 0 else "PD"
                psa, psu = PS[pga], PS[pgu]
                accgrp(pga, NCH,
                       lambda i, psa=psa, buf=buf, t0=t0: (
                           lambda t, ch=i: t.matmul(
                               psa[:, :], sb_wsl[:, buf + ch * 128: buf + ch * 128 + 128],
                               xnch(ch, t0), start=(ch == 0), stop=(ch == NCH - 1))),
                       lambda i, buf=buf: _slots("wsl", buf + i * 128, 128) + [xnk(i)])
                accgrp(pgu, NCH,
                       lambda i, psu=psu, buf=buf, t0=t0: (
                           lambda t, ch=i: t.matmul(
                               psu[:, :], sb_wsl[:, 4096 + buf + ch * 128: 4096 + buf + ch * 128 + 128],
                               xnch(ch, t0), start=(ch == 0), stop=(ch == NCH - 1))),
                       lambda i, buf=buf: _slots("wsl", 4096 + buf + i * 128, 128) + [xnk(i)])
                dst = bbv(0, 512) if alpha % 2 == 0 else bbv(512, 512)
                dk = bK(0, 512) if alpha % 2 == 0 else bK(512, 512)
                S.op("se", lambda s, psa=psa, dst=dst: s.activation(dst, psa[:, :], AF.Silu),
                     reads=[pga], writes=dk)
                S.op("ve", lambda v, alpha=alpha, psu=psu, dst=dst: v.tensor_mul(
                    hid(alpha), dst, psu[:, :]),
                    reads=dk + [pgu], writes=hidK(alpha))
            for alpha in range(64):
                pk = "PE1" if alpha % 2 == 0 else "PE2"
                ps = PS[pk]
                S.op("te", lambda t, alpha=alpha, ps=ps: t.matmul(
                    ps[:, :], sb_h128[:, :], hid(alpha), start=True, stop=True),
                    reads=["h128"] + hidK(alpha), writes=[pk])
                S.op("ve", lambda v, alpha=alpha, ps=ps: v.tensor_copy(hid(alpha), ps[:, :]),
                    reads=[pk], writes=hidK(alpha))
            for k in range(6):
                step = 1 << k
                for base in range(0, 64, step * 2):
                    for off in range(step):
                        a, b2 = base + off, base + off + step
                        S.op("ve", lambda v, a=a, b2=b2: v.tensor_add(
                            hid(a), hid(a), hid(b2)),
                            reads=hidK(a) + hidK(b2), writes=hidK(a))
                        S.op("ve", lambda v, a=a, b2=b2: v.scalar_tensor_tensor(
                            hid(b2), hid(b2), -2.0, hid(a), ALU.mult, ALU.add),
                            reads=hidK(a) + hidK(b2), writes=hidK(b2))
            def w2slab(do, hq, buf):
                S.op("ge", lambda g, do=do, hq=hq, buf=buf: g.dma_start(
                    out=sb_wsl[:, buf:buf + 2048].rearrange("p (c f) -> p c f", f=128),
                    in_=w2T[hq * 2048:(hq + 1) * 2048, do * 128:(do + 1) * 128]
                    .rearrange("(c p) f -> p c f", p=128)),
                    writes=wslK(buf, 2048), sem="gd", amt=16)
            w2slab(0, 0, 0)
            for qi in range(64):
                do, hq = qi // 4, qi % 4
                if qi < 63:
                    do2, hq2 = (qi + 1) // 4, (qi + 1) % 4
                    w2slab(do2, hq2, ((qi + 1) % 2) * 2048)
                buf = (qi % 2) * 2048
                pk = "PA" if do % 2 == 0 else "PB"
                ps = PS[pk]
                accgrp(pk, 16,
                       lambda i, buf=buf, ps=ps, hq=hq: (
                           lambda t, hc=i: t.matmul(
                               ps[:, :], sb_wsl[:, buf + hc * 128: buf + hc * 128 + 128],
                               hid(hq * 16 + hc), start=(hq == 0 and hc == 0),
                               stop=(hq == 3 and hc == 15))),
                       lambda i, buf=buf, hq=hq: _slots("wsl", buf + i * 128, 128) + hidK(hq * 16 + i))
                if hq == 3:
                    S.op("ve", lambda v, do=do, t0=t0, ps=ps: v.scalar_tensor_tensor(
                        xch(do, t0), ps[:, :], FWHT_SCALE, xch(do, t0), ALU.mult, ALU.add),
                        reads=[pk, xk(do)], writes=[xk(do)])

        if "xl0" in tap:
            dump_tap("xl0", lambda ch: xch(ch, 0, TL), lambda ch: [xk(ch)])

        ln_phase(1)
        if "xn1" in tap:
            dump_tap("xn1", lambda ch: xnch(ch, 0, TL), lambda ch: [xnk(ch)])

        def evac_q(e, tile, ps, pk):
            t0 = tile * 512
            dst = bbv(0, 512) if tile == 0 else bbv(512, 512)
            dk = bK(0, 512) if tile == 0 else bK(512, 512)
            S.op("se", lambda s, e=e, ps=ps, dst=dst: s.activation(
                dst, ps[:, :], AF.Identity, bias=vch("bqs", e), scale=INV_SQRT_DH),
                reads=[pk, "bqs"], writes=dk)
            S.op("sy", lambda s, e=e, t0=t0, dst=dst: s.dma_start(
                out=bass.AP(q_scr, e * 128 * TL + t0, [[TL, 128], [1, 512]]), in_=dst),
                reads=dk, writes=["qscr"], sem="sd", amt=16)

        proj2048(wqT, evac_q)

        def evac_k(e, tile, ps, pk):
            t0 = tile * 512
            dst = bbv(0, 512) if tile == 0 else bbv(512, 512)
            dk = bK(0, 512) if tile == 0 else bK(512, 512)
            S.op("se", lambda s, e=e, ps=ps, dst=dst: s.activation(
                dst, ps[:, :], AF.Identity, bias=vc("bk", e)),
                reads=[pk, "vcn"], writes=dk)
            S.op("sy", lambda s, e=e, t0=t0, dst=dst: s.dma_start(
                out=bass.AP(ag2_in, e * 128 * TL + t0, [[TL, 128], [1, 512]]), in_=dst),
                reads=dk, writes=["ag2k"], sem="sd", amt=16)
            S.op("ve", lambda v, dst=dst: v.tensor_mul(bbv(1024, 512), dst, dst),
                 reads=dk, writes=bK(1024, 512))
            S.op("te", lambda t: t.matmul(PH[0:1, :], sb_ones[:, :], bbv(1024, 512),
                                          start=True, stop=True),
                 reads=bK(1024, 512) + ["ones"], writes=["PH"])
            S.op("ve", lambda v, e=e, tile=tile: v.reduce_max(
                sb_km[0:1, 16 + e * 2 + tile:17 + e * 2 + tile], PH[0:1, :], axis=AX.X),
                reads=["PH"], writes=[f"kn{e}_{tile}"])

        proj2048(wkT, evac_k)
        for e in range(NCH):
            S.op("ve", lambda v, e=e: v.tensor_max(
                sb_km[0:1, 48 + e:49 + e], sb_km[0:1, 16 + e * 2:17 + e * 2],
                sb_km[0:1, 17 + e * 2:18 + e * 2]),
                reads=[f"kn{e}_0", f"kn{e}_1"], writes=["knm"])
        S.op("ve", lambda v: v.tensor_scalar_mul(sb_km[0:1, 48:64], sb_km[0:1, 48:64], 1.02),
             reads=["knm"], writes=["knm"])
        S.op("ve", lambda v: v.tensor_copy(sb_rowb[0:1, 1024:1040], sb_km[0:1, 48:64]),
             reads=["knm"], writes=["knmb"])
        S.op("sy", lambda s: s.dma_start(
            out=bass.AP(ag2_in, 2048 * D, [[16, 1], [1, 16]]), in_=sb_rowb[0:1, 1024:1040]),
            reads=["knmb"], writes=["ag2m"], sem="sd", amt=16)

        def evac_v(e, tile, ps, pk):
            t0 = tile * 512
            dst = bbv(0, 512) if tile == 0 else bbv(512, 512)
            dk = bK(0, 512) if tile == 0 else bK(512, 512)
            S.op("se", lambda s, e=e, ps=ps, dst=dst: s.activation(
                dst, ps[:, :], AF.Identity, bias=vc("bv", e)),
                reads=[pk, "vcn"], writes=dk)
            for tc in range(4):
                S.op("te", lambda t, tc=tc, dst=dst: t.transpose(
                    PG[:, tc * 128:(tc + 1) * 128], dst[:, tc * 128:(tc + 1) * 128],
                    sb_id[:, :]),
                    reads=dk + ["ident"], writes=["PG"])
                vcol = tile * 8192 + tc * 2048 + e * 128
                S.op("ve", lambda v, tc=tc, vcol=vcol: v.tensor_copy(
                    wk(vcol, 128), PG[:, tc * 128:(tc + 1) * 128]),
                    reads=["PG"], writes=wkK(vcol, 128))
            if e == NCH - 1:
                for tc in range(4):
                    vcol = tile * 8192 + tc * 2048
                    S.op("sy", lambda s, tc=tc, t0=t0, vcol=vcol: s.dma_start(
                        out=bass.AP(ag2_in, (1024 + t0 + tc * 128) * D, [[D, 128], [1, D]]),
                        in_=wk(vcol, 2048)),
                        reads=wkK(vcol, 2048), writes=["ag2v"], sem="sd", amt=16)

        proj2048(wvT, evac_v)

        S.op("ge", lambda g: g.collective_compute(
            "AllGather", ALU.bypass, replica_groups=[list(range(NCORES))],
            ins=[ag2_in.ap()], outs=[ag2_out.ap()]),
            reads=["ag2k", "ag2v", "ag2m"], writes=["ag2o"], sem="cc", amt=1)

        S.op("ge", lambda g: g.dma_start(
            out=sb_wsl[:, 0:2048].rearrange("p (c f) -> p c f", f=128),
            in_=wrt.rearrange("(c p) f -> p c f", p=128)),
            writes=wslK(0, 2048), sem="gd", amt=16)
        for tile in range(2):
            t0 = tile * 512
            accgrp("PA", NCH,
                   lambda i, t0=t0: (lambda t, ch=i: t.matmul(
                       PA[:, :], sb_wsl[:, ch * 128:(ch + 1) * 128], xnch(ch, t0),
                       start=(ch == 0), stop=(ch == NCH - 1))),
                   lambda i: _slots("wsl", i * 128, 128) + [xnk(i)])
            S.op("ve", lambda v: v.tensor_copy(bbv(0, 512), PA[:, :]),
                 reads=["PA"], writes=bK(0, 512))
            for g4 in range(4):
                S.op("te", lambda t, g4=g4: t.transpose(
                    PG[:, g4 * 128:(g4 + 1) * 128], sb_b[:, g4 * 128:(g4 + 1) * 128],
                    sb_id[:, :]),
                    reads=bK(0, 512) + ["ident"], writes=["PG"])
            S.op("ve", lambda v: v.tensor_copy(bbv(512, 512), PG[:, 0:512]),
                 reads=["PG"], writes=bK(512, 512))
            for g4 in range(4):
                rt = sb_b[:, 512 + g4 * 128: 512 + (g4 + 1) * 128]
                S.op("ve", lambda v, rt=rt: v.reduce_max(fbv(1024, 1), rt, axis=AX.X, negate=True),
                     reads=bK(512, 512), writes=["mr0"])
                S.op("se", lambda s, rt=rt: s.activation(
                    fbv(1536, 128), rt, AF.Exp, bias=sb_f[:, 1024:1025],
                    accum_out=sb_f[:, 1025:1026]),
                    reads=["mr0"] + bK(512, 512), writes=fK(1536, 128) + ["den"])
                S.op("ve", lambda v: v.reciprocal(sb_f[:, 1026:1027], sb_f[:, 1025:1026]),
                     reads=["den"], writes=["deni"])
                S.op("ve", lambda v: v.tensor_scalar_mul(
                    fbv(1536, 128), fbv(1536, 128), sb_f[:, 1026:1027]),
                    reads=fK(1536, 128) + ["deni"], writes=fK(1536, 128))
                S.op("ve", lambda v: v.tensor_copy(fbv(1664, 128), fbv(1536, 128)),
                     reads=fK(1536, 128), writes=fK(1664, 128))
                for rnd in range(3):
                    S.op("ve", lambda v: v.reduce_max(sb_f[:, 1027:1028], fbv(1664, 128), axis=AX.X),
                         reads=fK(1664, 128), writes=["mr"])
                    S.op("ve", lambda v: v.tensor_scalar(
                        fbv(1792, 128), fbv(1664, 128), sb_f[:, 1027:1028], None, ALU.is_equal),
                        reads=fK(1664, 128) + ["mr"], writes=fK(1792, 128))
                    S.op("ve", lambda v: v.scalar_tensor_tensor(
                        fbv(1664, 128), fbv(1792, 128), NEG, fbv(1664, 128), ALU.mult, ALU.add),
                        reads=fK(1792, 128) + fK(1664, 128), writes=fK(1664, 128))
                S.op("ve", lambda v: v.reduce_max(sb_f[:, 1028:1029], fbv(1664, 128), axis=AX.X),
                     reads=fK(1664, 128), writes=["m4"])
                S.op("ve", lambda v: v.scalar_tensor_tensor(
                    fbv(1792, 128), fbv(1536, 128), sb_f[:, 1028:1029], fbv(1536, 128),
                    ALU.is_ge, ALU.mult, accum_out=sb_f[:, 1029:1030]),
                    reads=fK(1536, 128) + ["m4"], writes=fK(1792, 128) + ["ps4"])
                S.op("ve", lambda v: v.tensor_scalar_add(sb_f[:, 1029:1030], sb_f[:, 1029:1030], 1e-8),
                     reads=["ps4"], writes=["ps4"])
                S.op("ve", lambda v: v.reciprocal(sb_f[:, 1030:1031], sb_f[:, 1029:1030]),
                     reads=["ps4"], writes=["ps4i"])
                S.op("ve", lambda v, g4=g4: v.tensor_scalar_mul(
                    sb_b[:, 1024 + g4 * 128: 1024 + (g4 + 1) * 128],
                    fbv(1792, 128), sb_f[:, 1030:1031]),
                    reads=fK(1792, 128) + ["ps4i"], writes=bK(1024 + g4 * 128, 128))
            for g4 in range(4):
                S.op("te", lambda t, g4=g4: t.transpose(
                    PG[:, 512 + g4 * 128: 512 + (g4 + 1) * 128],
                    sb_b[:, 1024 + g4 * 128: 1024 + (g4 + 1) * 128], sb_id[:, :]),
                    reads=bK(1024 + g4 * 128, 128) + ["ident"], writes=["PG"])
            S.op("ve", lambda v: v.tensor_copy(bbv(1536, 512), PG[:, 512:1024]),
                 reads=["PG"], writes=bK(1536, 512))
            S.op("te", lambda t: t.matmul(PH[0:8, :], sb_b[:, 4088:4096], bbv(1536, 512),
                                          start=True, stop=True),
                 reads=["mixpb"] + bK(1536, 512), writes=["PH"])
            S.op("ve", lambda v: v.tensor_copy(sb_b[0:8, 3072:3584], PH[0:8, :]),
                 reads=["PH"], writes=bK(3072, 512))
            for kk in range(8):
                S.op("te", lambda t, kk=kk: t.matmul(
                    PE1[:, :], sb_selr[0:8, kk * 128:(kk + 1) * 128], sb_b[0:8, 3072:3584],
                    start=True, stop=True),
                    reads=["selr"] + bK(3072, 512), writes=["PE1"])
                wcol = WR0 + (kk * 2 + tile) * 512
                S.op("ve", lambda v, wcol=wcol: v.tensor_copy(wk(wcol, 512), PE1[:, :]),
                     reads=["PE1"], writes=wkK(wcol, 512))
        S.op("ge", lambda g: g.memset(wk(MX0, 8192), 0.0), writes=wkK(MX0, 8192))
        bufs = [0, 8192]
        stream_w(wga, 0, 256, 0)
        stream_w(wua, 0, 256, 4096)
        for hb in range(32):
            if hb < 31:
                nb = bufs[(hb + 1) % 2]
                stream_w(wga, (hb + 1) * 256, 256, nb)
                stream_w(wua, (hb + 1) * 256, 256, nb + 4096)
            buf = bufs[hb % 2]
            for hc in range(2):
                alpha = hb * 2 + hc
                kk, ic = alpha >> 3, alpha & 7
                for tile in range(2):
                    t0 = tile * 512
                    pga = "PA" if tile == 0 else "PC"
                    pgu = "PB" if tile == 0 else "PD"
                    psa, psu = PS[pga], PS[pgu]
                    accgrp(pga, NCH,
                           lambda i, psa=psa, buf=buf, hc=hc, t0=t0: (
                               lambda t, ch=i: t.matmul(
                                   psa[:, :],
                                   sb_wsl[:, buf + ch * 256 + hc * 128: buf + ch * 256 + hc * 128 + 128],
                                   xnch(ch, t0), start=(ch == 0), stop=(ch == NCH - 1))),
                           lambda i, buf=buf, hc=hc: _slots("wsl", buf + i * 256 + hc * 128, 128) + [xnk(i)])
                    accgrp(pgu, NCH,
                           lambda i, psu=psu, buf=buf, hc=hc, t0=t0: (
                               lambda t, ch=i: t.matmul(
                                   psu[:, :],
                                   sb_wsl[:, buf + 4096 + ch * 256 + hc * 128: buf + 4096 + ch * 256 + hc * 128 + 128],
                                   xnch(ch, t0), start=(ch == 0), stop=(ch == NCH - 1))),
                           lambda i, buf=buf, hc=hc: _slots("wsl", buf + 4096 + i * 256 + hc * 128, 128) + [xnk(i)])
                    dst = bbv(0, 512) if tile == 0 else bbv(512, 512)
                    dk = bK(0, 512) if tile == 0 else bK(512, 512)
                    S.op("se", lambda s, psa=psa, dst=dst: s.activation(dst, psa[:, :], AF.Silu),
                         reads=[pga], writes=dk)
                    bdst = bbv(1024, 512) if tile == 0 else bbv(2048, 512)
                    bkK = bK(1024, 512) if tile == 0 else bK(2048, 512)
                    S.op("ve", lambda v, psu=psu, dst=dst, bdst=bdst: v.tensor_mul(
                        bdst, dst, psu[:, :]),
                        reads=dk + [pgu], writes=bkK)
                    fdst = fbv(0, 512) if tile == 0 else fbv(512, 512)
                    fkK = fK(0, 512) if tile == 0 else fK(512, 512)
                    wcol = WR0 + (kk * 2 + tile) * 512
                    S.op("ve", lambda v, bdst=bdst, fdst=fdst, wcol=wcol: v.tensor_mul(
                        fdst, bdst, wk(wcol, 512)),
                        reads=bkK + wkK(wcol, 512), writes=fkK)
                    mcol = MX0 + (ic * 2 + tile) * 512
                    S.op("ve", lambda v, mcol=mcol, fdst=fdst: v.tensor_add(
                        wk(mcol, 512), wk(mcol, 512), fdst),
                        reads=fkK + wkK(mcol, 512), writes=wkK(mcol, 512))
        S.op("ge", lambda g: g.dma_start(
            out=sb_wsl[:, :].rearrange("p (c f) -> p c f", f=D),
            in_=wda.rearrange("(c p) f -> p c f", p=128)),
            writes=wslK(0, 16384), sem="gd", amt=16)
        for tile in range(2):
            t0 = tile * 512
            for do in range(16):
                pk = "PA" if do % 2 == 0 else "PB"
                ps = PS[pk]
                accgrp(pk, 8,
                       lambda i, do=do, ps=ps, tile=tile: (
                           lambda t, ic=i: t.matmul(
                               ps[:, :], sb_wsl[:, ic * D + do * 128: ic * D + do * 128 + 128],
                               wk(MX0 + (ic * 2 + tile) * 512, 512),
                               start=(ic == 0), stop=(ic == 7))),
                       lambda i, do=do, tile=tile: (_slots("wsl", i * D + do * 128, 128) +
                                                    wkK(MX0 + (i * 2 + tile) * 512, 512)))
                S.op("ve", lambda v, do=do, t0=t0, ps=ps: v.scalar_tensor_tensor(
                    xch(do, t0), ps[:, :], 1.0, xch(do, t0), ALU.mult, ALU.add),
                    reads=[pk, xk(do)], writes=[xk(do)])
        bufs = [0, 8192]
        stream_w(swg, 0, 256, 0)
        stream_w(swu, 0, 256, 4096)
        for i2b in range(8):
            if i2b < 7:
                nb = bufs[(i2b + 1) % 2]
                stream_w(swg, (i2b + 1) * 256, 256, nb)
                stream_w(swu, (i2b + 1) * 256, 256, nb + 4096)
            buf = bufs[i2b % 2]
            for hc in range(2):
                i2 = i2b * 2 + hc
                for tile in range(2):
                    t0 = tile * 512
                    pga = "PA" if tile == 0 else "PC"
                    pgu = "PB" if tile == 0 else "PD"
                    psa, psu = PS[pga], PS[pgu]
                    accgrp(pga, NCH,
                           lambda i, psa=psa, buf=buf, hc=hc, t0=t0: (
                               lambda t, ch=i: t.matmul(
                                   psa[:, :],
                                   sb_wsl[:, buf + ch * 256 + hc * 128: buf + ch * 256 + hc * 128 + 128],
                                   xnch(ch, t0), start=(ch == 0), stop=(ch == NCH - 1))),
                           lambda i, buf=buf, hc=hc: _slots("wsl", buf + i * 256 + hc * 128, 128) + [xnk(i)])
                    accgrp(pgu, NCH,
                           lambda i, psu=psu, buf=buf, hc=hc, t0=t0: (
                               lambda t, ch=i: t.matmul(
                                   psu[:, :],
                                   sb_wsl[:, buf + 4096 + ch * 256 + hc * 128: buf + 4096 + ch * 256 + hc * 128 + 128],
                                   xnch(ch, t0), start=(ch == 0), stop=(ch == NCH - 1))),
                           lambda i, buf=buf, hc=hc: _slots("wsl", buf + 4096 + i * 256 + hc * 128, 128) + [xnk(i)])
                    dst = bbv(0, 512) if tile == 0 else bbv(512, 512)
                    dk = bK(0, 512) if tile == 0 else bK(512, 512)
                    S.op("se", lambda s, psa=psa, dst=dst: s.activation(dst, psa[:, :], AF.Silu),
                         reads=[pga], writes=dk)
                    scol = SH0 + (i2 * 2 + tile) * 512
                    S.op("ve", lambda v, scol=scol, psu=psu, dst=dst: v.tensor_mul(
                        wk(scol, 512), dst, psu[:, :]),
                        reads=dk + [pgu], writes=wkK(scol, 512))
        bufs = [0, 8192]
        stream_w(swd, 0, 512, 0)
        for sl4 in range(4):
            if sl4 < 3:
                stream_w(swd, (sl4 + 1) * 512, 512, bufs[(sl4 + 1) % 2])
            buf = bufs[sl4 % 2]
            for esub in range(4):
                do = sl4 * 4 + esub
                for tile in range(2):
                    t0 = tile * 512
                    pk = "PA" if tile == 0 else "PB"
                    ps = PS[pk]
                    accgrp(pk, NCH,
                           lambda i, buf=buf, esub=esub, ps=ps, tile=tile: (
                               lambda t, i2=i: t.matmul(
                                   ps[:, :],
                                   sb_wsl[:, buf + i2 * 512 + esub * 128: buf + i2 * 512 + esub * 128 + 128],
                                   wk(SH0 + (i2 * 2 + tile) * 512, 512),
                                   start=(i2 == 0), stop=(i2 == NCH - 1))),
                           lambda i, buf=buf, esub=esub, tile=tile: (
                               _slots("wsl", buf + i * 512 + esub * 128, 128) +
                               wkK(SH0 + (i * 2 + tile) * 512, 512)))
                    S.op("ve", lambda v, do=do, t0=t0, ps=ps: v.scalar_tensor_tensor(
                        xch(do, t0), ps[:, :], 1.0, xch(do, t0), ALU.mult, ALU.add),
                        reads=[pk, xk(do)], writes=[xk(do)])

        if "moe" in tap:
            dump_tap("moe", lambda ch: xch(ch, 0, TL), lambda ch: [xk(ch)])

        S.op("ge", lambda g: g.dma_start(
            out=sb_rowb[0:1, 1040:1168],
            in_=bass.AP(ag2_out, 2048 * D, [[AGROWS * D, 8], [1, 16]])),
            reads=["ag2o"], writes=["kmg"], sem="gd", amt=16)
        S.op("ve", lambda v: v.tensor_copy(sb_km[0:1, 64:80], sb_rowb[0:1, 1040:1056]),
             reads=["kmg"], writes=["km2"])
        for r in range(1, 8):
            S.op("ve", lambda v, r=r: v.tensor_max(
                sb_km[0:1, 64:80], sb_km[0:1, 64:80], sb_rowb[0:1, 1040 + r * 16:1056 + r * 16]),
                reads=["kmg", "km2"], writes=["km2"])
        S.op("se", lambda s: s.activation(sb_km[0:1, 80:96], sb_km[0:1, 64:80], AF.Sqrt),
             reads=["km2"], writes=["km"])

        for hg in range(8):
            h0 = hg * 2
            S.op("sy", lambda s, h0=h0: s.dma_start(
                out=sb_b[:, 0:2048].rearrange("p (h f) -> p h f", f=TL),
                in_=bass.AP(q_scr, h0 * 128 * TL, [[TL, 128], [128 * TL, 2], [1, TL]])),
                reads=["qscr"], writes=bK(0, 2048), sem="sd", amt=16)
            for h in range(2):
                S.op("ve", lambda v, h=h: v.tensor_mul(
                    bbv(2048, 1024), sb_b[:, h * TL:(h + 1) * TL], sb_b[:, h * TL:(h + 1) * TL]),
                    reads=bK(h * TL, TL), writes=bK(2048, 1024))
                for half in range(2):
                    S.op("te", lambda t, half=half: t.matmul(
                        PH[0:1, :], sb_ones[:, :], bbv(2048 + half * 512, 512),
                        start=True, stop=True),
                        reads=bK(2048 + half * 512, 512) + ["ones"], writes=["PH"])
                    S.op("ve", lambda v, h=h, half=half: v.reduce_max(
                        sb_km[0:1, 96 + h * 2 + half:97 + h * 2 + half], PH[0:1, :], axis=AX.X),
                        reads=["PH"], writes=[f"qn{h}_{half}"])
                S.op("ve", lambda v, h=h: v.tensor_max(
                    sb_km[0:1, 104 + h:105 + h], sb_km[0:1, 96 + h * 2:97 + h * 2],
                    sb_km[0:1, 97 + h * 2:98 + h * 2]),
                    reads=[f"qn{h}_0", f"qn{h}_1"], writes=[f"qnm{h}"])
            S.op("se", lambda s: s.activation(sb_km[0:1, 108:110], sb_km[0:1, 104:106], AF.Sqrt),
                 reads=["qnm0", "qnm1"], writes=["qm"])
            S.op("ve", lambda v, h0=h0: v.tensor_mul(
                sb_km[0:1, 112:114], sb_km[0:1, 108:110], sb_km[0:1, 80 + h0:82 + h0]),
                reads=["qm", "km"], writes=["Bh"])
            S.op("ve", lambda v: v.tensor_scalar_mul(sb_km[0:1, 116:118], sb_km[0:1, 112:114], -1.0),
                 reads=["Bh"], writes=["nB"])

            for qt in ("B", "A"):
                nun = QB_UNITS if qt == "B" else QA_UNITS
                qoff = 0 if qt == "A" else 512
                for h in range(2):
                    S.op("se", lambda s, h=h, h0=h0, qoff=qoff: s.activation(
                        sb_row[0:1, 0:512], sb_iotag[0:1, qoff:qoff + 512], AF.Identity,
                        scale=-SLOPES[h0 + h], bias=sb_km[0:1, 116 + h:117 + h]),
                        reads=["iotag", "nB"], writes=["rw0"])
                    S.op("ve", lambda v, h=h: v.tensor_copy(
                        sb_rowb[0:1, h * 512:(h + 1) * 512], sb_row[0:1, 0:512]),
                        reads=["rw0"], writes=[f"rb{h}"])
                units = [(kvb, False) for kvb in range(nun)] + [(None, True)]
                NU = len(units)
                its = []
                for ui, (kvb, diag) in enumerate(units):
                    kcol = AK0 + (ui % 2) * 1024
                    vcol = AV0 + (ui % 2) * 1024
                    for h in range(2):
                        for j in range(4):
                            if diag:
                                jc = (h0 + h) * JCOLS + (QB_UNITS + QA_UNITS + (0 if qt == "B" else 1)) * 4 + j
                            else:
                                u = kvb if qt == "B" else QB_UNITS + kvb
                                jc = (h0 + h) * JCOLS + u * 4 + j
                            its.append(dict(ui=ui, kvb=kvb, diag=diag, h=h, j=j,
                                            kcol=kcol, vcol=vcol, jc=jc,
                                            st=(ui == 0 and j == 0),
                                            sp=(diag and j == 3)))

                def unit_dma(ui):
                    kvb, diag = units[ui]
                    kcol = AK0 + (ui % 2) * 1024
                    vcol = AV0 + (ui % 2) * 1024
                    if diag:
                        tloc = 0 if qt == "A" else 512
                        src_, skey, base = ag2_in, ["ag2k", "ag2v"], 0
                    else:
                        rr = kvb if kvb < 8 else 15 - kvb
                        tloc = 0 if kvb < 8 else 512
                        src_, skey, base = ag2_out, ["ag2o"], rr * AGROWS * D
                    S.op("sy", lambda s, kcol=kcol, base=base, tloc=tloc, h0=h0, src_=src_: s.dma_start(
                        out=wk(kcol, 1024).rearrange("p (h f) -> p h f", f=512),
                        in_=bass.AP(src_, base + h0 * 128 * TL + tloc,
                                    [[TL, 128], [128 * TL, 2], [1, 512]])),
                        reads=skey, writes=wkK(kcol, 1024), sem="sd", amt=16)
                    for hh in range(2):
                        S.op("sy", lambda s, vcol=vcol, base=base, tloc=tloc, h0=h0, src_=src_, hh=hh: s.dma_start(
                            out=wk(vcol + hh * 512, 512).rearrange("p (j f) -> p j f", f=128),
                            in_=bass.AP(src_, base + (1024 + tloc) * D + (h0 + hh) * 128,
                                        [[D, 128], [128 * D, 4], [1, 128]])),
                            reads=skey, writes=wkK(vcol + hh * 512, 512), sem="sd", amt=16)

                def emit_RS(idx):
                    it = its[idx]
                    spk = "PA" if idx % 2 == 0 else "PB"
                    sps = PS[spk]
                    h, j, kcol = it["h"], it["j"], it["kcol"]
                    S.op("te", lambda t, h=h, sps=sps: t.matmul(
                        sps[:, :], sb_onesr[:, :], sb_rowb[0:1, h * 512:(h + 1) * 512],
                        start=True, stop=False),
                        reads=[f"rb{h}", "onesr"], writes=[spk])
                    S.op("te", lambda t, h=h, j=j, sps=sps, kcol=kcol, qoff=qoff: t.matmul(
                        sps[:, :], sb_wk[:, kcol + h * 512 + j * 128: kcol + h * 512 + j * 128 + 128],
                        sb_b[:, h * TL + qoff: h * TL + qoff + 512],
                        start=False, stop=True),
                        reads=wkK(kcol + h * 512 + j * 128, 128) + bK(h * TL + qoff, 512),
                        writes=[spk])

                def emit_exp(idx):
                    it = its[idx]
                    spk = "PA" if idx % 2 == 0 else "PB"
                    sps = PS[spk]
                    pcol = AP0 + (idx % 8) * 512
                    if it["diag"]:
                        tcol = 1024 + (idx % 4) * 512
                        j = it["j"]
                        S.op("ve", lambda v, sps=sps, tcol=tcol, j=j: v.tensor_add(
                            fbv(tcol, 512), sps[:, :], sb_mext[:, 384 - j * 128:896 - j * 128]),
                            reads=[spk, "mext"], writes=fK(tcol, 512))
                        S.op("se", lambda s, tcol=tcol, pcol=pcol, jc=it["jc"]: s.activation(
                            wk(pcol, 512), fbv(tcol, 512), AF.Exp, bias=sb_jtab[:, jc:jc + 1]),
                            reads=fK(tcol, 512) + ["jtab"], writes=wkK(pcol, 512))
                    else:
                        S.op("se", lambda s, sps=sps, pcol=pcol, jc=it["jc"]: s.activation(
                            wk(pcol, 512), sps[:, :], AF.Exp, bias=sb_jtab[:, jc:jc + 1]),
                            reads=[spk, "jtab"], writes=wkK(pcol, 512))

                def emit_lav(idx):
                    it = its[idx]
                    h, j = it["h"], it["j"]
                    pcol = AP0 + (idx % 8) * 512
                    lbank = "PE2" if h == 0 else "PE1"
                    lps = PS[lbank]
                    obank = "PC" if h == 0 else "PD"
                    ops_ = PS[obank]
                    st, sp = it["st"], it["sp"]
                    S.op("te", lambda t, pcol=pcol, lps=lps, st=st, sp=sp: t.matmul(
                        lps[0:1, :], sb_ones[:, :], wk(pcol, 512), start=st, stop=sp),
                        reads=wkK(pcol, 512) + ["ones"],
                        writes=[lbank] if (st or sp) else [])
                    vcol = it["vcol"]
                    S.op("te", lambda t, pcol=pcol, vcol=vcol, h=h, j=j, ops_=ops_, st=st, sp=sp: t.matmul(
                        ops_[:, :], sb_wk[:, vcol + h * 512 + j * 128: vcol + h * 512 + j * 128 + 128],
                        wk(pcol, 512), start=st, stop=sp),
                        reads=wkK(vcol + h * 512 + j * 128, 128) + wkK(pcol, 512),
                        writes=[obank] if (st or sp) else [])

                def phase1(u):
                    if u + 1 < NU:
                        unit_dma(u + 1)
                    b0 = u * 8
                    emit_RS(b0)
                    emit_RS(b0 + 1)
                    for i in range(2, 8):
                        emit_exp(b0 + i - 2)
                        emit_RS(b0 + i)
                    emit_exp(b0 + 6)
                    emit_exp(b0 + 7)

                def phase2(u):
                    b0 = u * 8
                    for i in range(8):
                        emit_lav(b0 + i)

                unit_dma(0)
                phase1(0)
                for u in range(1, NU):
                    phase2(u - 1)
                    phase1(u)
                phase2(NU - 1)
                for h in range(2):
                    obank = "PC" if h == 0 else "PD"
                    ops_ = PS[obank]
                    lbank = "PE2" if h == 0 else "PE1"
                    lps = PS[lbank]
                    S.op("ve", lambda v, lps=lps: v.reciprocal(
                        sb_f[0:1, 1024:1536], lps[0:1, :]),
                        reads=[lbank], writes=["linv"])
                    S.op("te", lambda t: t.matmul(
                        PA[:, :], sb_onesrf[0:1, :], sb_f[0:1, 1024:1536],
                        start=True, stop=True),
                        reads=["onesrf", "linv"], writes=["PA"])
                    S.op("ve", lambda v: v.tensor_copy(fbv(1536, 512), PA[:, :]),
                         reads=["PA"], writes=fK(1536, 512))
                    S.op("ve", lambda v, h=h, h0=h0, qoff=qoff, ops_=ops_: v.tensor_mul(
                        wk(AO0 + (h0 + h) * TL + qoff, 512), fbv(1536, 512), ops_[:, :]),
                        reads=fK(1536, 512) + [obank],
                        writes=wkK(AO0 + (h0 + h) * TL + qoff, 512))

        def evac_wo(e, tile, ps, pk):
            t0 = tile * 512
            S.op("ve", lambda v, e=e, t0=t0, ps=ps: v.scalar_tensor_tensor(
                xch(e, t0), ps[:, :], vc("bo", e), xch(e, t0), ALU.add, ALU.add),
                reads=[pk, "vcn", xk(e)], writes=[xk(e)])

        proj2048(woT, evac_wo,
                 rhs=lambda ch, c0: wk(AO0 + ch * TL + c0, 512),
                 rhs_keys=lambda ch: wkK(AO0 + ch * TL, TL))

        if "attn" in tap:
            dump_tap("attn", lambda ch: wk(AO0 + ch * TL, TL),
                     lambda ch: wkK(AO0 + ch * TL, TL))

        outks = []
        for ch in range(NCH):
            c0 = 0 if ch % 2 == 0 else 1024
            S.op("ve", lambda v, ch=ch, c0=c0: v.tensor_copy(fbv(c0, 1024), xch(ch, 0, TL)),
                 reads=[xk(ch)], writes=fK(c0, 1024))
            S.op("sy", lambda s, ch=ch, c0=c0: s.dma_start(
                out=outT[ch * 128:(ch + 1) * 128, :], in_=fbv(c0, 1024)),
                reads=fK(c0, 1024), writes=[f"out{ch}"], sem="sd", amt=16)
            outks.append(f"out{ch}")
        S.wait_keys("sy", outks)
        S.wait_keys("ge", outks)

        @block.tensor
        def _(h):
            for f in S.fns["te"]:
                f(h)

        @block.vector
        def _(h):
            for f in S.fns["ve"]:
                f(h)

        @block.scalar
        def _(h):
            for f in S.fns["se"]:
                f(h)

        @block.gpsimd
        def _(h):
            for f in S.fns["ge"]:
                f(h)

        @block.sync
        def _(h):
            for f in S.fns["sy"]:
                f(h)

    return nc


def _host_prep(inputs):
    f32 = np.float32
    bf16 = ml_dtypes.bfloat16
    A = {k: np.asarray(v) for k, v in inputs.items()}
    x = np.ascontiguousarray(A["L0_attn_u"][0].T.astype(f32))

    def tr(a):
        return np.ascontiguousarray(np.asarray(a).T.astype(f32))

    shared = dict(
        gwT=tr(A["L0_attn_gate_w"]), w1T=tr(A["L0_ffn_W1"]), w3T=tr(A["L0_ffn_W3"]),
        w2T=tr(A["L0_ffn_W2"]), wqT=tr(A["L1_attn_Wq"]), wkT=tr(A["L1_attn_Wk"]),
        wvT=tr(A["L1_attn_Wv"]), woT=tr(A["L1_attn_Wo"]),
        wrt=np.ascontiguousarray(A["L1_ffn_w_router"].astype(f32)),
        wga=np.ascontiguousarray(A["L1_ffn_w_gate"].astype(f32)),
        wua=np.ascontiguousarray(A["L1_ffn_w_up"].astype(f32)),
        wda=np.ascontiguousarray(A["L1_ffn_w_down"].astype(f32)),
        swg=np.ascontiguousarray(A["L1_ffn_shared_w_gate"].astype(f32)),
        swu=np.ascontiguousarray(A["L1_ffn_shared_w_up"].astype(f32)),
        swd=np.ascontiguousarray(A["L1_ffn_shared_w_down"].astype(f32)),
        dlru=np.ascontiguousarray(A["L0_attn_d_lr_u"].astype(f32)),
        dlrv=np.ascontiguousarray(A["L0_attn_d_lr_v"].astype(f32)),
    )
    vcons = np.zeros((128, 164), f32)

    def pack(col, vec):
        vcons[:, col:col + 16] = np.asarray(vec, f32).reshape(16, 128).T

    pack(0, A["L0_attn_a"]); pack(16, A["L0_attn_b"]); pack(32, A["L0_attn_c"])
    pack(48, A["L0_attn_d"]); pack(64, A["L0_attn_gate_b"])
    pack(80, A["L1_attn_bq"]); pack(96, A["L1_attn_bk"]); pack(112, A["L1_attn_bv"])
    pack(128, A["L1_attn_bo"])
    vcons[:, 144] = np.asarray(A["L1_ffn_expert_scale"], f32)
    vcons[:, 146:162] = np.asarray(A["L1_ffn_amplitudes"], f32).reshape(128, 16)
    shared["vcons"] = vcons

    shared["ident"] = np.eye(128, dtype=bf16)
    shared["h128"] = _hadamard(128).astype(bf16)
    mext = np.full((128, 896), NEG, f32)
    for p in range(128):
        mext[p, 384 + p:] = 0.0
    shared["mext"] = mext.astype(bf16)
    selrow = np.zeros((8, 1024), f32)
    for k in range(8):
        selrow[k, k * 128:(k + 1) * 128] = 1.0
    shared["selrow"] = selrow.astype(bf16)

    in_maps = []
    for c in range(NCORES):
        blkA, blkB = c, 15 - c
        cols = np.r_[blkA * 512:(blkA + 1) * 512, blkB * 512:(blkB + 1) * 512]
        m = dict(shared)
        m["xT"] = np.ascontiguousarray(x[:, cols])
        m["iotag"] = cols.astype(bf16).reshape(1, TL)
        sel = np.zeros((128, 32), f32)
        sel[:, blkA] = 1.0
        sel[:, 16 + blkB] = 1.0
        m["selAB"] = sel
        jt = np.full((128, HEADS * JCOLS), NEG, f32)
        p = np.arange(128, dtype=f32)
        for h in range(HEADS):
            s = SLOPES[h]
            base = h * JCOLS
            for kvb in range(QB_UNITS):
                if kvb <= 14 - c:
                    for j in range(4):
                        jt[:, base + kvb * 4 + j] = s * (512 * kvb + 128 * j + p)
            for kvb in range(QA_UNITS):
                if kvb <= c - 1:
                    for j in range(4):
                        jt[:, base + (QB_UNITS + kvb) * 4 + j] = s * (512 * kvb + 128 * j + p)
            for j in range(4):
                jt[:, base + (QB_UNITS + QA_UNITS) * 4 + j] = s * (512 * blkB + 128 * j + p)
                jt[:, base + (QB_UNITS + QA_UNITS + 1) * 4 + j] = s * (512 * blkA + 128 * j + p)
        m["jtab"] = jt
        in_maps.append(m)
    return in_maps


def kernel(_tap=(), _strict_dma=True, **inputs):
    in_maps = _host_prep(inputs)
    nc = build(tap=_tap, strict_dma=_strict_dma)
    res = run_bass_kernel_spmd(nc, in_maps, core_ids=list(range(NCORES)))
    out = np.zeros((D, T), np.float32)
    for c in range(NCORES):
        r = res.results[c]
        blkA, blkB = c, 15 - c
        out[:, blkA * 512:(blkA + 1) * 512] = r["outT"][:, 0:512]
        out[:, blkB * 512:(blkB + 1) * 512] = r["outT"][:, 512:1024]
    final = np.ascontiguousarray(out.T)[None, :, :]
    if _tap:
        return final, [res.results[c] for c in range(NCORES)]
    return final


# version 09
# speedup vs baseline: 1.2700x; 1.0236x over previous
import numpy as np
import ml_dtypes
from contextlib import ExitStack

import concourse.bass as bass
import concourse.mybir as mybir
from concourse.bass_utils import run_bass_kernel_spmd

F32 = mybir.dt.float32
BF16 = mybir.dt.bfloat16
AF = mybir.ActivationFunctionType
ALU = mybir.AluOpType
AX = mybir.AxisListType

NCORES = 8
T = 8192
D = 2048
NCH = 16
TL = 1024
NBLK = 16
HEADS = 16
HMID = 8192
INV_SQRT_DH = float(1.0 / np.sqrt(128.0))
FWHT_SCALE = float(1.0 / np.sqrt(8192.0))
SLOPES = [2.0 ** (-0.5 * (h + 1)) for h in range(HEADS)]
NEG = -1e30
AGROWS = 2056
QB_UNITS = 15
QA_UNITS = 7
JCOLS = (QB_UNITS + QA_UNITS + 2) * 4

ENG2SEM = {"te": "te", "ve": "ve", "se": "se", "ge": "gc", "sy": "syc"}
NDMASEM = 8
DMASEMS = [f"gd{i}" for i in range(NDMASEM)] + [f"sd{i}" for i in range(NDMASEM)]
SEMNAMES = ["te", "ve", "se", "gc", "syc", "cc"] + DMASEMS

UG0 = 8192
WR0 = 0
MX0 = 8192
SH0 = 0
AK0 = 16384
AV0 = 18432
AP0 = 20480
AO0 = 0


class Sched:
    def __init__(self, strict_dma=False):
        self.strict_dma = strict_dma
        self.fns = {e: [] for e in ENG2SEM}
        self.cnt = {s: 0 for s in SEMNAMES}
        self.rr = {"gd": 0, "sd": 0}
        self.reg = {}
        self.lastw = {e: {} for e in ENG2SEM}
        self.sems = None

    def _need(self, reads, writes):
        need = {}
        for k in reads:
            r = self.reg.get(k)
            if r:
                for s, v in r["w"].items():
                    need[s] = max(need.get(s, 0), v)
        for k in writes:
            r = self.reg.get(k)
            if r:
                for s, v in r["w"].items():
                    need[s] = max(need.get(s, 0), v)
                for s, v in r["r"].items():
                    need[s] = max(need.get(s, 0), v)
        return need

    def op(self, eng, fn, reads=(), writes=(), sem=None, amt=1):
        if sem in ("gd", "sd"):
            q = sem
            sem = f"{q}{self.rr[q]}"
            self.rr[q] = (self.rr[q] + 1) % NDMASEM
        else:
            sem = sem or ENG2SEM[eng]
        need = self._need(reads, writes)
        if amt == 16:
            need[sem] = max(need.get(sem, 0), self.cnt[sem])
        lw = self.lastw[eng]
        waits = []
        for s, v in sorted(need.items()):
            if eng == "te" and s == "te":
                continue
            if lw.get(s, 0) < v:
                waits.append((s, v))
                lw[s] = v
        self.cnt[sem] += amt
        val = self.cnt[sem]
        for k in writes:
            self.reg[k] = {"w": {sem: val}, "r": {}}
        for k in reads:
            r = self.reg.setdefault(k, {"w": {}, "r": {}})
            r["r"][sem] = max(r["r"].get(sem, 0), val)

        def closure(h, _w=tuple(waits), _fn=fn, _sem=sem, _amt=amt):
            for s, v in _w:
                h.wait_ge(self.sems[s], v)
            _fn(h).then_inc(self.sems[_sem], _amt)

        self.fns[eng].append(closure)
        return (sem, val)

    def wait_keys(self, eng, keys):
        need = self._need(keys, ())
        lw = self.lastw[eng]
        waits = []
        for s, v in sorted(need.items()):
            if lw.get(s, 0) < v:
                waits.append((s, v))
                lw[s] = v

        def closure(h, _w=tuple(waits)):
            for s, v in _w:
                h.wait_ge(self.sems[s], v)

        self.fns[eng].append(closure)


def _hadamard(n):
    h = np.array([[1]], dtype=np.int64)
    while h.shape[0] < n:
        h = np.block([[h, h], [h, -h]])
    return h


def _slots(buf, c0, n):
    return [f"{buf}{s}" for s in range(c0 // 512, (c0 + n + 511) // 512)]


def build(tap=(), strict_dma=False):
    tap = set(tap)
    nc = bass.Bass("TRN2", target_bir_lowering=False)
    dt = nc.dram_tensor

    xT = dt("xT", [D, TL], F32, kind="ExternalInput").ap()
    iotag_i = dt("iotag", [1, TL], BF16, kind="ExternalInput").ap()
    selAB_i = dt("selAB", [128, 32], F32, kind="ExternalInput").ap()
    jtab_i = dt("jtab", [128, HEADS * JCOLS], F32, kind="ExternalInput").ap()
    outT = dt("outT", [D, TL], F32, kind="ExternalOutput").ap()

    gwT = dt("gwT", [D, D], F32, kind="ExternalInput").ap()
    w1T = dt("w1T", [D, HMID], F32, kind="ExternalInput").ap()
    w3T = dt("w3T", [D, HMID], F32, kind="ExternalInput").ap()
    w2T = dt("w2T", [HMID, D], F32, kind="ExternalInput").ap()
    wqT = dt("wqT", [D, D], F32, kind="ExternalInput").ap()
    wkT = dt("wkT", [D, D], F32, kind="ExternalInput").ap()
    wvT = dt("wvT", [D, D], F32, kind="ExternalInput").ap()
    woT = dt("woT", [D, D], F32, kind="ExternalInput").ap()
    wrt = dt("wrt", [D, 128], F32, kind="ExternalInput").ap()
    wga = dt("wga", [D, HMID], F32, kind="ExternalInput").ap()
    wua = dt("wua", [D, HMID], F32, kind="ExternalInput").ap()
    wda = dt("wda", [1024, D], F32, kind="ExternalInput").ap()
    swg = dt("swg", [D, D], F32, kind="ExternalInput").ap()
    swu = dt("swu", [D, D], F32, kind="ExternalInput").ap()
    swd = dt("swd", [D, D], F32, kind="ExternalInput").ap()
    dlru_i = dt("dlru", [D, 4], F32, kind="ExternalInput").ap()
    dlrv_i = dt("dlrv", [4, D], F32, kind="ExternalInput").ap()
    vcons_i = dt("vcons", [128, 164], F32, kind="ExternalInput").ap()
    ident_i = dt("ident", [128, 128], BF16, kind="ExternalInput").ap()
    h128_i = dt("h128", [128, 128], BF16, kind="ExternalInput").ap()
    mext_i = dt("mext", [128, 896], BF16, kind="ExternalInput").ap()
    selr_i = dt("selrow", [8, 1024], BF16, kind="ExternalInput").ap()

    ag1_in = dt("ag1_in", [3, D], F32)
    ag1_out = dt("ag1_out", [3 * NCORES, D], F32)
    ag2_in = dt("ag2_in", [AGROWS, D], BF16)
    ag2_out = dt("ag2_out", [AGROWS * NCORES, D], BF16)
    q_scr = dt("q_scr", [D, TL], BF16)

    taps = {}
    for name in ["xn0", "xl0", "xn1", "moe", "attn", "xa0"]:
        if name in tap:
            taps[name] = dt("t_" + name, [D, TL], F32, kind="ExternalOutput").ap()
    if "adbg" in tap:
        taps["adbg"] = dt("t_adbg", [128, 6144], F32, kind="ExternalOutput").ap()

    S = Sched(strict_dma=strict_dma)

    with ExitStack() as ctx:
        en = ctx.enter_context
        sb_x = en(nc.sbuf_tensor("sb_x", [128, NCH * TL], BF16))
        sb_xn = en(nc.sbuf_tensor("sb_xn", [128, NCH * TL], BF16))
        sb_wk = en(nc.sbuf_tensor("sb_wk", [128, 24576], BF16))
        sb_wsl = en(nc.sbuf_tensor("sb_wsl", [128, 16384], BF16))
        sb_f = en(nc.sbuf_tensor("sb_f", [128, 3072], F32))
        sb_b = en(nc.sbuf_tensor("sb_b", [128, 4096], BF16))
        sb_v = en(nc.sbuf_tensor("sb_v", [128, 256], F32))
        sb_s = en(nc.sbuf_tensor("sb_s", [8, 512], F32))
        sb_row = en(nc.sbuf_tensor("sb_row", [1, 1024], F32))
        sb_rowb = en(nc.sbuf_tensor("sb_rowb", [1, 1280], BF16))
        sb_cr = en(nc.sbuf_tensor("sb_cr", [128, 464], F32))
        sb_km = en(nc.sbuf_tensor("sb_km", [1, 144], F32))
        sb_id = en(nc.sbuf_tensor("sb_id", [128, 128], BF16))
        sb_h128 = en(nc.sbuf_tensor("sb_h128", [128, 128], BF16))
        sb_mext = en(nc.sbuf_tensor("sb_mext", [128, 896], BF16))
        sb_jtab = en(nc.sbuf_tensor("sb_jtab", [128, HEADS * JCOLS], F32))
        sb_iotag = en(nc.sbuf_tensor("sb_iotag", [1, TL], BF16))
        sb_vc = en(nc.sbuf_tensor("sb_vc", [128, 164], F32))
        sb_sel = en(nc.sbuf_tensor("sb_sel", [128, 32], F32))
        sb_selr = en(nc.sbuf_tensor("sb_selr", [8, 1024], BF16))
        sb_onesrf = en(nc.sbuf_tensor("sb_onesrf", [33, 128], F32))
        sb_ones = en(nc.sbuf_tensor("sb_ones", [128, 1], BF16))
        sb_onesr = en(nc.sbuf_tensor("sb_onesr", [1, 128], BF16))
        sb_dlru = en(nc.sbuf_tensor("sb_dlru", [128, 64], BF16))
        sb_d4 = en(nc.sbuf_tensor("sb_d4", [4, 1], BF16))
        sb_umb = en(nc.sbuf_tensor("sb_umb", [128, 16], BF16))

        PA = en(nc.psum_tensor("PA", [128, 512], F32))
        PB = en(nc.psum_tensor("PB", [128, 512], F32))
        PC = en(nc.psum_tensor("PC", [128, 512], F32))
        PD = en(nc.psum_tensor("PD", [128, 512], F32))
        PE1 = en(nc.psum_tensor("PE1", [128, 512], F32))
        PE2 = en(nc.psum_tensor("PE2", [128, 512], F32))
        PG = en(nc.psum_tensor("PG", [128, 1024], BF16))
        PH = en(nc.psum_tensor("PH", [8, 512], F32))
        PS = {"PA": PA, "PB": PB, "PC": PC, "PD": PD, "PE1": PE1, "PE2": PE2}

        sems = {}
        for s in SEMNAMES:
            sems[s] = en(nc.semaphore(s))
        S.sems = sems
        block = en(nc.Block())

        VCOFF = {"a": 0, "b": 16, "c": 32, "d": 48, "gb": 64, "bq": 80,
                 "bk": 96, "bv": 112, "bo": 128, "esc": 144, "amps": 146}

        def vc(name, ch):
            o = VCOFF[name]
            return sb_vc[:, o + ch: o + ch + 1]

        VOFF = {"aa": 0, "lnaa": 16, "aa512": 32, "bqs": 48, "dlr": 64,
                "carryA": 80, "carryB": 96, "cA": 112, "cB": 128,
                "umean": 144, "chain": 160, "mixp": 176, "tmp": 192}

        def vsl(name, n=16):
            o = VOFF[name]
            return sb_v[:, o: o + n]

        def vch(name, ch):
            o = VOFF[name]
            return sb_v[:, o + ch: o + ch + 1]

        def xch(ch, c0=0, n=512):
            return sb_x[:, ch * TL + c0: ch * TL + c0 + n]

        def xk(ch):
            return f"x{ch}"

        def xnch(ch, c0=0, n=512):
            return sb_xn[:, ch * TL + c0: ch * TL + c0 + n]

        def xnk(ch):
            return f"xn{ch}"

        def wk(c0, n):
            return sb_wk[:, c0:c0 + n]

        def wkK(c0, n):
            return _slots("wk", c0, n)

        def fbv(c0, n):
            return sb_f[:, c0:c0 + n]

        def fK(c0, n):
            return _slots("f", c0, n)

        def bbv(c0, n):
            return sb_b[:, c0:c0 + n]

        def bK(c0, n):
            return _slots("b", c0, n)

        def wslK(c0, n):
            return _slots("wsl", c0, n)

        def ug(ch, c0=0, n=512):
            return wk(UG0 + ch * TL + c0, n)

        def ugK(ch, c0=0, n=512):
            return wkK(UG0 + ch * TL + c0, n)

        S.op("ge", lambda g: g.dma_start(
            out=sb_x[:, :].rearrange("p (c f) -> p c f", f=TL),
            in_=xT.rearrange("(c p) f -> p c f", p=128)),
            writes=[xk(c) for c in range(NCH)], sem="gd", amt=16)
        for ap_, dst, wr in [(ident_i, sb_id, "ident"), (h128_i, sb_h128, "h128"),
                             (mext_i, sb_mext, "mext"), (jtab_i, sb_jtab, "jtab"),
                             (iotag_i, sb_iotag, "iotag"),
                             (vcons_i, sb_vc, "vcn"), (selAB_i, sb_sel, "sel"),
                             (selr_i, sb_selr, "selr")]:
            S.op("ge", lambda g, a=ap_, d=dst: g.dma_start(out=d[:, :], in_=a),
                 writes=[wr], sem="gd", amt=16)
        S.op("ge", lambda g: g.dma_start(
            out=sb_dlru[:, :].rearrange("p (c f) -> p c f", f=4),
            in_=dlru_i.rearrange("(c p) f -> p c f", p=128)),
            writes=["dlru"], sem="gd", amt=16)
        S.op("ge", lambda g: g.memset(sb_b[0:8, 0:2048], 0.0), writes=bK(0, 2048))
        S.op("sy", lambda s: s.dma_start(out=ag2_in[2048:AGROWS, :], in_=sb_b[0:8, 0:2048]),
             reads=bK(0, 2048), writes=["ag2m"], sem="sd", amt=16)
        S.op("ge", lambda g: g.memset(sb_ones[:, :], 1.0), writes=["ones"])
        S.op("ge", lambda g: g.memset(sb_onesr[:, :], 1.0), writes=["onesr"])
        S.op("ge", lambda g: g.memset(sb_onesrf[:, :], 1.0), writes=["onesrf"])
        S.op("se", lambda s: s.activation(vsl("aa"), sb_vc[:, 0:16], AF.Tanh),
             reads=["vcn"], writes=["aa"])
        S.op("se", lambda s: s.activation(vsl("lnaa"), vsl("aa"), AF.Ln),
             reads=["aa"], writes=["lnaa"])
        S.op("se", lambda s: s.activation(vsl("aa512"), vsl("lnaa"), AF.Exp, scale=512.0),
             reads=["lnaa"], writes=["aa512"])
        S.op("ve", lambda v: v.tensor_scalar_mul(vsl("bqs"), sb_vc[:, 80:96], INV_SQRT_DH),
             reads=["vcn"], writes=["bqs"])
        S.op("ve", lambda v: v.tensor_mul(fbv(0, 16), sb_vc[:, 146:162], sb_vc[:, 146:162]),
             reads=["vcn"], writes=fK(0, 16))
        S.op("ve", lambda v: v.tensor_add(vsl("mixp", 8), sb_f[:, 0:16:2], sb_f[:, 1:17:2]),
             reads=fK(0, 16), writes=["mixp"])
        S.op("ve", lambda v: v.reduce_sum(vch("tmp", 0), vsl("mixp", 8), axis=AX.X),
             reads=["mixp"], writes=["tmp0"])
        S.op("ve", lambda v: v.tensor_scalar_add(vch("tmp", 0), vch("tmp", 0), 1e-8),
             reads=["tmp0"], writes=["tmp0"])
        S.op("ve", lambda v: v.reciprocal(vch("tmp", 1), vch("tmp", 0)),
             reads=["tmp0"], writes=["tmp1"])
        S.op("ve", lambda v: v.tensor_scalar_mul(vsl("mixp", 8), vsl("mixp", 8), vch("tmp", 1)),
             reads=["mixp", "tmp1"], writes=["mixp"])
        S.op("ve", lambda v: v.tensor_scalar_mul(vsl("mixp", 8), vsl("mixp", 8), vc("esc", 0)),
             reads=["mixp", "vcn"], writes=["mixp"])
        S.op("ve", lambda v: v.tensor_copy(sb_b[:, 4088:4096], vsl("mixp", 8)),
             reads=["mixp"], writes=["mixpb"])

        def stream_w(src_ap, col0, ncols, buf, nchunks=NCH, rows=D):
            S.op("ge", lambda g: g.dma_start(
                out=sb_wsl[:, buf:buf + nchunks * ncols].rearrange("p (c f) -> p c f", f=ncols),
                in_=src_ap[0:rows, col0:col0 + ncols].rearrange("(c p) f -> p c f", p=128)),
                writes=wslK(buf, nchunks * ncols), sem="gd", amt=16)

        def accgrp(pk, n, fn_i, reads_i):
            for i in range(n):
                S.op("te", fn_i(i), reads=reads_i(i),
                     writes=[pk] if (i == 0 or i == n - 1) else [])

        def ln_phase(layer):
            for tile in range(2):
                t0 = tile * 512
                accgrp("PH", NCH,
                       lambda i, t0=t0: (lambda t, ch=i: t.matmul(
                           PH[0:1, :], sb_ones[:, :], xch(ch, t0),
                           start=(ch == 0), stop=(ch == NCH - 1))),
                       lambda i: [xk(i), "ones"])
                for ch in range(NCH):
                    dst = bbv(0, 512) if ch % 2 == 0 else bbv(512, 512)
                    dk = bK(0, 512) if ch % 2 == 0 else bK(512, 512)
                    S.op("ve", lambda v, ch=ch, t0=t0, dst=dst: v.tensor_mul(
                        dst, xch(ch, t0), xch(ch, t0)),
                        reads=[xk(ch)], writes=dk)
                    S.op("te", lambda t, ch=ch, dst=dst: t.matmul(
                        PA[0:1, :], sb_ones[:, :], dst,
                        start=(ch == 0), stop=(ch == NCH - 1)),
                        reads=dk + ["ones"],
                        writes=["PA"] if (ch == 0 or ch == NCH - 1) else [])
                S.op("ve", lambda v: v.tensor_scalar_mul(sb_row[0:1, 0:512], PH[0:1, :], 1.0 / D),
                     reads=["PH"], writes=["rw0"])
                S.op("ve", lambda v: v.tensor_scalar_mul(sb_row[0:1, 512:1024], PA[0:1, :], 1.0 / D),
                     reads=["PA"], writes=["rw1"])
                S.op("ve", lambda v: v.tensor_mul(sb_rowb[0:1, 0:512], sb_row[0:1, 0:512],
                                                  sb_row[0:1, 0:512]),
                     reads=["rw0"], writes=["rwb0"])
                S.op("ve", lambda v: v.tensor_sub(sb_row[0:1, 512:1024], sb_row[0:1, 512:1024],
                                                  sb_rowb[0:1, 0:512]),
                     reads=["rw1", "rwb0"], writes=["rw1"])
                S.op("ve", lambda v: v.tensor_scalar_add(sb_row[0:1, 512:1024],
                                                         sb_row[0:1, 512:1024], 1e-5),
                     reads=["rw1"], writes=["rw1"])
                S.op("ve", lambda v: v.reciprocal(sb_row[0:1, 512:1024], sb_row[0:1, 512:1024]),
                     reads=["rw1"], writes=["rw1"])
                S.op("se", lambda s: s.activation(sb_row[0:1, 512:1024], sb_row[0:1, 512:1024],
                                                  AF.Sqrt),
                     reads=["rw1"], writes=["rw1"])
                S.op("ve", lambda v: v.tensor_mul(sb_row[0:1, 0:512], sb_row[0:1, 0:512],
                                                  sb_row[0:1, 512:1024]),
                     reads=["rw0", "rw1"], writes=["rw0"])
                S.op("ve", lambda v: v.tensor_copy(sb_rowb[0:1, 0:512], sb_row[0:1, 512:1024]),
                     reads=["rw1"], writes=["rwb0"])
                S.op("ve", lambda v: v.tensor_copy(sb_rowb[0:1, 512:1024], sb_row[0:1, 0:512]),
                     reads=["rw0"], writes=["rwb1"])
                S.op("te", lambda t: t.matmul(PE1[:, :], sb_onesr[:, :], sb_rowb[0:1, 0:512],
                                              start=True, stop=True),
                     reads=["rwb0", "onesr"], writes=["PE1"])
                S.op("te", lambda t: t.matmul(PE2[:, :], sb_onesr[:, :], sb_rowb[0:1, 512:1024],
                                              start=True, stop=True),
                     reads=["rwb1", "onesr"], writes=["PE2"])
                for ch in range(NCH):
                    dst = fbv(0, 512) if ch % 2 == 0 else fbv(512, 512)
                    dk = fK(0, 512) if ch % 2 == 0 else fK(512, 512)
                    S.op("ve", lambda v, ch=ch, t0=t0, dst=dst: v.tensor_mul(
                        dst, xch(ch, t0), PE1[:, :]),
                        reads=[xk(ch), "PE1"], writes=dk)
                    S.op("ve", lambda v, ch=ch, t0=t0, dst=dst: v.tensor_sub(
                        xnch(ch, t0), dst, PE2[:, :]),
                        reads=dk + ["PE2"], writes=[xnk(ch)])

        def proj2048(wap, evac, rhs=None, rhs_keys=None):
            rhs = rhs or (lambda ch, c0: xnch(ch, c0))
            rhs_keys = rhs_keys or (lambda ch: [xnk(ch)])
            bufs = [0, 8192]
            stream_w(wap, 0, 512, bufs[0])
            for sl in range(4):
                if sl < 3:
                    stream_w(wap, (sl + 1) * 512, 512, bufs[(sl + 1) % 2])
                buf = bufs[sl % 2]
                for esub in range(4):
                    e = sl * 4 + esub
                    for tile in range(2):
                        pk = "PA" if tile == 0 else "PB"
                        ps = PS[pk]
                        accgrp(pk, NCH,
                               lambda i, ps=ps, buf=buf, esub=esub, tile=tile: (
                                   lambda t, ch=i: t.matmul(
                                       ps[:, :],
                                       sb_wsl[:, buf + ch * 512 + esub * 128:
                                              buf + ch * 512 + esub * 128 + 128],
                                       rhs(ch, tile * 512),
                                       start=(ch == 0), stop=(ch == NCH - 1))),
                               lambda i, buf=buf, esub=esub: (
                                   _slots("wsl", buf + i * 512 + esub * 128, 128) + rhs_keys(i)))
                        evac(e, tile, ps, pk)

        ln_phase(0)

        def dump_tap(name, src, src_keys):
            for ch in range(NCH):
                S.op("ve", lambda v, ch=ch: v.tensor_copy(fbv(0, 1024), src(ch)),
                     reads=src_keys(ch), writes=fK(0, 1024))
                S.op("sy", lambda s, ch=ch: s.dma_start(
                    out=taps[name][ch * 128:(ch + 1) * 128, :], in_=fbv(0, 1024)),
                    reads=fK(0, 1024), writes=["dmp"], sem="sd", amt=16)

        if "xn0" in tap:
            dump_tap("xn0", lambda ch: xnch(ch, 0, TL), lambda ch: [xnk(ch)])

        def evac_gate(e, tile, ps, pk):
            t0 = tile * 512
            dst = bbv(0, 512) if tile == 0 else bbv(512, 512)
            dk = bK(0, 512) if tile == 0 else bK(512, 512)
            S.op("se", lambda s, e=e, ps=ps, dst=dst: s.activation(
                dst, ps[:, :], AF.Sigmoid, bias=vc("gb", e)),
                reads=[pk, "vcn"], writes=dk)
            S.op("ve", lambda v, e=e, t0=t0, dst=dst, tile=tile: v.scalar_tensor_tensor(
                ug(e, t0), dst, 1.0, xnch(e, t0), ALU.mult, ALU.mult,
                accum_out=sb_cr[:, 32 + e * 2 + tile: 33 + e * 2 + tile]),
                reads=dk + [xnk(e)], writes=ugK(e, t0) + [f"ugs{e}_{tile}"])

        proj2048(gwT, evac_gate)

        for ch in range(NCH):
            for tile in range(2):
                t0 = tile * 512
                buf_ = fbv(1024, 512) if tile == 0 else fbv(1536, 512)
                buK = fK(1024, 512) if tile == 0 else fK(1536, 512)
                ssf = fbv(2048, 512) if tile == 0 else fbv(2560, 512)
                ssK = fK(2048, 512) if tile == 0 else fK(2560, 512)
                S.op("ve", lambda v, ch=ch, t0=t0, buf_=buf_: v.tensor_scalar_mul(
                    buf_, ug(ch, t0), vc("b", ch)),
                    reads=ugK(ch, t0) + ["vcn"], writes=buK)
                S.op("ve", lambda v, ch=ch, buf_=buf_, ssf=ssf: v.tensor_tensor_scan(
                    ssf, vch("aa", ch).broadcast_to((128, 512)), buf_, 0.0, ALU.mult, ALU.add),
                    reads=buK + ["aa"], writes=ssK)
                S.op("ve", lambda v, ch=ch, tile=tile, ssf=ssf: v.tensor_copy(
                    sb_cr[:, tile * 16 + ch: tile * 16 + ch + 1], ssf[:, 511:512]),
                    reads=ssK, writes=[f"carr{tile}_{ch}"])
                S.op("ve", lambda v, ch=ch, t0=t0, ssf=ssf: v.scalar_tensor_tensor(
                    xch(ch, t0), ssf, vc("c", ch), xch(ch, t0), ALU.mult, ALU.add),
                    reads=ssK + ["vcn", xk(ch)], writes=[xk(ch)])
                S.op("ve", lambda v, ch=ch, t0=t0: v.scalar_tensor_tensor(
                    xch(ch, t0), ug(ch, t0), vc("d", ch), xch(ch, t0), ALU.mult, ALU.add),
                    reads=ugK(ch, t0) + ["vcn", xk(ch)], writes=[xk(ch)])
        for ch in range(NCH):
            S.op("ve", lambda v, ch=ch: v.tensor_add(
                sb_cr[:, 64 + ch: 65 + ch], sb_cr[:, 32 + ch * 2: 33 + ch * 2],
                sb_cr[:, 33 + ch * 2: 34 + ch * 2]),
                reads=[f"ugs{ch}_0", f"ugs{ch}_1"], writes=[f"ups{ch}"])

        S.op("sy", lambda s: s.dma_start(
            out=ag1_in[0, :].rearrange("(p c) -> p c", c=16), in_=sb_cr[:, 0:16]),
            reads=[f"carr0_{ch}" for ch in range(NCH)], writes=["ag1a"], sem="sd", amt=16)
        S.op("sy", lambda s: s.dma_start(
            out=ag1_in[1, :].rearrange("(p c) -> p c", c=16), in_=sb_cr[:, 16:32]),
            reads=[f"carr1_{ch}" for ch in range(NCH)], writes=["ag1b"], sem="sd", amt=16)
        S.op("sy", lambda s: s.dma_start(
            out=ag1_in[2, :].rearrange("(p c) -> p c", c=16), in_=sb_cr[:, 64:80]),
            reads=[f"ups{ch}" for ch in range(NCH)], writes=["ag1c"], sem="sd", amt=16)
        S.op("ge", lambda g: g.collective_compute(
            "AllGather", ALU.bypass, replica_groups=[list(range(NCORES))],
            ins=[ag1_in.ap()], outs=[ag1_out.ap()]),
            reads=["ag1a", "ag1b", "ag1c"], writes=["ag1o"], sem="cc", amt=1)
        S.op("ge", lambda g: g.dma_start(
            out=sb_cr[:, 80:464].rearrange("p (r c) -> p r c", c=16),
            in_=ag1_out.ap().rearrange("r (p c) -> p r c", c=16)),
            reads=["ag1o"], writes=["agm"], sem="gd", amt=16)

        def crow(j):
            r = j if j < 8 else 15 - j
            return 80 + (3 * r + (0 if j < 8 else 1)) * 16

        S.op("ge", lambda g: g.memset(vsl("chain"), 0.0), writes=["chain"])
        S.op("ge", lambda g: g.memset(vsl("carryA"), 0.0), writes=["carryA"])
        S.op("ge", lambda g: g.memset(vsl("carryB"), 0.0), writes=["carryB"])
        for j in range(NBLK):
            S.op("ve", lambda v, j=j: v.scalar_tensor_tensor(
                vsl("carryA"), vsl("chain"), sb_sel[:, j:j + 1], vsl("carryA"),
                ALU.mult, ALU.add),
                reads=["chain", "sel", "carryA"], writes=["carryA"])
            S.op("ve", lambda v, j=j: v.scalar_tensor_tensor(
                vsl("carryB"), vsl("chain"), sb_sel[:, 16 + j:17 + j], vsl("carryB"),
                ALU.mult, ALU.add),
                reads=["chain", "sel", "carryB"], writes=["carryB"])
            if j < NBLK - 1:
                S.op("ve", lambda v: v.tensor_mul(vsl("chain"), vsl("aa512"), vsl("chain")),
                     reads=["chain", "aa512"], writes=["chain"])
                S.op("ve", lambda v, j=j: v.tensor_add(
                    vsl("chain"), vsl("chain"), sb_cr[:, crow(j):crow(j) + 16]),
                    reads=["chain", "agm"], writes=["chain"])
        S.op("ve", lambda v: v.tensor_mul(vsl("cA"), vsl("carryA"), sb_vc[:, 32:48]),
             reads=["carryA", "vcn"], writes=["cA"])
        S.op("ve", lambda v: v.tensor_mul(vsl("cB"), vsl("carryB"), sb_vc[:, 32:48]),
             reads=["carryB", "vcn"], writes=["cB"])
        S.op("ve", lambda v: v.tensor_add(vsl("umean"), sb_cr[:, 80 + 2 * 16:80 + 3 * 16],
                                          sb_cr[:, 80 + 5 * 16:80 + 6 * 16]),
             reads=["agm"], writes=["umean"])
        for r in range(2, 8):
            S.op("ve", lambda v, r=r: v.tensor_add(
                vsl("umean"), vsl("umean"),
                sb_cr[:, 80 + (3 * r + 2) * 16:80 + (3 * r + 2) * 16 + 16]),
                reads=["agm", "umean"], writes=["umean"])
        S.op("ve", lambda v: v.tensor_scalar_mul(vsl("umean"), vsl("umean"), 1.0 / T),
             reads=["umean"], writes=["umean"])
        S.op("ve", lambda v: v.tensor_copy(sb_umb[:, :], vsl("umean")),
             reads=["umean"], writes=["umb"])
        accgrp("PH", NCH,
               lambda i: (lambda t, ch=i: t.matmul(
                   PH[0:4, 0:1], sb_dlru[:, ch * 4: ch * 4 + 4],
                   sb_umb[:, ch:ch + 1], start=(ch == 0), stop=(ch == NCH - 1))),
               lambda i: ["dlru", "umb"])
        S.op("se", lambda s: s.activation(sb_d4[:, :], PH[0:4, 0:1], AF.Identity, scale=0.25),
             reads=["PH"], writes=["d4"])
        S.op("ge", lambda g: g.dma_start(out=sb_b[0:4, 0:2048], in_=dlrv_i),
             writes=bK(0, 2048), sem="gd", amt=16)
        for e in range(NCH):
            S.op("te", lambda t, e=e: t.matmul(
                PB[:, e:e + 1], sb_b[0:4, e * 128:(e + 1) * 128],
                sb_d4[:, :], start=True, stop=True),
                reads=bK(e * 128, 128) + ["d4"], writes=["PB"] if (e == 0 or e == NCH - 1) else [])
        S.op("ve", lambda v: v.tensor_copy(vsl("dlr"), PB[:, 0:16]),
             reads=["PB"], writes=["dlr"])
        S.op("ge", lambda g: g.iota(fbv(1024, 512), [[1, 512]], channel_multiplier=0, allow_small_or_imprecise_dtypes=True),
             writes=fK(1024, 512))
        for ch in range(NCH):
            apd = fbv(0, 512) if ch % 2 == 0 else fbv(512, 512)
            apK = fK(0, 512) if ch % 2 == 0 else fK(512, 512)
            S.op("se", lambda s, ch=ch, apd=apd: s.activation(
                apd, fbv(1024, 512), AF.Exp, scale=vch("lnaa", ch), bias=vch("lnaa", ch)),
                reads=["lnaa"] + fK(1024, 512), writes=apK)
            for tile in range(2):
                t0 = tile * 512
                cXn = "cA" if tile == 0 else "cB"
                S.op("ve", lambda v, ch=ch, t0=t0, cXn=cXn, apd=apd: v.scalar_tensor_tensor(
                    xch(ch, t0), apd, vch(cXn, ch), xch(ch, t0), ALU.mult, ALU.add),
                    reads=apK + [cXn, xk(ch)], writes=[xk(ch)])
                S.op("ve", lambda v, ch=ch, t0=t0: v.scalar_tensor_tensor(
                    xch(ch, t0), ug(ch, t0), vch("dlr", ch), xch(ch, t0), ALU.mult, ALU.add),
                    reads=ugK(ch, t0) + ["dlr", xk(ch)], writes=[xk(ch)])

        if "xa0" in tap:
            dump_tap("xa0", lambda ch: xch(ch, 0, TL), lambda ch: [xk(ch)])

        def hid(alpha, n=512):
            if alpha < 48:
                return wk(alpha * 512, n)
            return sb_wsl[:, 8192 + (alpha - 48) * 512: 8192 + (alpha - 48) * 512 + n]

        def hidK(alpha):
            if alpha < 48:
                return wkK(alpha * 512, 512)
            return wslK(8192 + (alpha - 48) * 512, 512)

        for tile in range(2):
            t0 = tile * 512
            stream_w(w1T, 0, 128, 0)
            stream_w(w3T, 0, 128, 4096)
            for alpha in range(64):
                if alpha < 63:
                    stream_w(w1T, (alpha + 1) * 128, 128, ((alpha + 1) % 2) * 2048)
                    stream_w(w3T, (alpha + 1) * 128, 128, 4096 + ((alpha + 1) % 2) * 2048)
                buf = (alpha % 2) * 2048
                pga = "PA" if alpha % 2 == 0 else "PC"
                pgu = "PB" if alpha % 2 == 0 else "PD"
                psa, psu = PS[pga], PS[pgu]
                accgrp(pga, NCH,
                       lambda i, psa=psa, buf=buf, t0=t0: (
                           lambda t, ch=i: t.matmul(
                               psa[:, :], sb_wsl[:, buf + ch * 128: buf + ch * 128 + 128],
                               xnch(ch, t0), start=(ch == 0), stop=(ch == NCH - 1))),
                       lambda i, buf=buf: _slots("wsl", buf + i * 128, 128) + [xnk(i)])
                accgrp(pgu, NCH,
                       lambda i, psu=psu, buf=buf, t0=t0: (
                           lambda t, ch=i: t.matmul(
                               psu[:, :], sb_wsl[:, 4096 + buf + ch * 128: 4096 + buf + ch * 128 + 128],
                               xnch(ch, t0), start=(ch == 0), stop=(ch == NCH - 1))),
                       lambda i, buf=buf: _slots("wsl", 4096 + buf + i * 128, 128) + [xnk(i)])
                dst = bbv(0, 512) if alpha % 2 == 0 else bbv(512, 512)
                dk = bK(0, 512) if alpha % 2 == 0 else bK(512, 512)
                S.op("se", lambda s, psa=psa, dst=dst: s.activation(dst, psa[:, :], AF.Silu),
                     reads=[pga], writes=dk)
                S.op("ve", lambda v, alpha=alpha, psu=psu, dst=dst: v.tensor_mul(
                    hid(alpha), dst, psu[:, :]),
                    reads=dk + [pgu], writes=hidK(alpha))
            for alpha in range(64):
                pk = "PE1" if alpha % 2 == 0 else "PE2"
                ps = PS[pk]
                S.op("te", lambda t, alpha=alpha, ps=ps: t.matmul(
                    ps[:, :], sb_h128[:, :], hid(alpha), start=True, stop=True),
                    reads=["h128"] + hidK(alpha), writes=[pk])
                S.op("ve", lambda v, alpha=alpha, ps=ps: v.tensor_copy(hid(alpha), ps[:, :]),
                    reads=[pk], writes=hidK(alpha))
            for k in range(6):
                step = 1 << k
                for base in range(0, 64, step * 2):
                    for off in range(step):
                        a, b2 = base + off, base + off + step
                        S.op("ve", lambda v, a=a, b2=b2: v.tensor_add(
                            hid(a), hid(a), hid(b2)),
                            reads=hidK(a) + hidK(b2), writes=hidK(a))
                        S.op("ve", lambda v, a=a, b2=b2: v.scalar_tensor_tensor(
                            hid(b2), hid(b2), -2.0, hid(a), ALU.mult, ALU.add),
                            reads=hidK(a) + hidK(b2), writes=hidK(b2))
            def w2slab(do, hq, buf):
                S.op("ge", lambda g, do=do, hq=hq, buf=buf: g.dma_start(
                    out=sb_wsl[:, buf:buf + 2048].rearrange("p (c f) -> p c f", f=128),
                    in_=w2T[hq * 2048:(hq + 1) * 2048, do * 128:(do + 1) * 128]
                    .rearrange("(c p) f -> p c f", p=128)),
                    writes=wslK(buf, 2048), sem="gd", amt=16)
            w2slab(0, 0, 0)
            for qi in range(64):
                do, hq = qi // 4, qi % 4
                if qi < 63:
                    do2, hq2 = (qi + 1) // 4, (qi + 1) % 4
                    w2slab(do2, hq2, ((qi + 1) % 2) * 2048)
                buf = (qi % 2) * 2048
                pk = "PA" if do % 2 == 0 else "PB"
                ps = PS[pk]
                accgrp(pk, 16,
                       lambda i, buf=buf, ps=ps, hq=hq: (
                           lambda t, hc=i: t.matmul(
                               ps[:, :], sb_wsl[:, buf + hc * 128: buf + hc * 128 + 128],
                               hid(hq * 16 + hc), start=(hq == 0 and hc == 0),
                               stop=(hq == 3 and hc == 15))),
                       lambda i, buf=buf, hq=hq: _slots("wsl", buf + i * 128, 128) + hidK(hq * 16 + i))
                if hq == 3:
                    S.op("ve", lambda v, do=do, t0=t0, ps=ps: v.scalar_tensor_tensor(
                        xch(do, t0), ps[:, :], FWHT_SCALE, xch(do, t0), ALU.mult, ALU.add),
                        reads=[pk, xk(do)], writes=[xk(do)])

        if "xl0" in tap:
            dump_tap("xl0", lambda ch: xch(ch, 0, TL), lambda ch: [xk(ch)])

        ln_phase(1)
        if "xn1" in tap:
            dump_tap("xn1", lambda ch: xnch(ch, 0, TL), lambda ch: [xnk(ch)])

        def evac_q(e, tile, ps, pk):
            t0 = tile * 512
            dst = bbv(0, 512) if tile == 0 else bbv(512, 512)
            dk = bK(0, 512) if tile == 0 else bK(512, 512)
            S.op("se", lambda s, e=e, ps=ps, dst=dst: s.activation(
                dst, ps[:, :], AF.Identity, bias=vch("bqs", e), scale=INV_SQRT_DH),
                reads=[pk, "bqs"], writes=dk)
            S.op("sy", lambda s, e=e, t0=t0, dst=dst: s.dma_start(
                out=bass.AP(q_scr, e * 128 * TL + t0, [[TL, 128], [1, 512]]), in_=dst),
                reads=dk, writes=["qscr"], sem="sd", amt=16)

        proj2048(wqT, evac_q)

        def evac_k(e, tile, ps, pk):
            t0 = tile * 512
            dst = bbv(0, 512) if tile == 0 else bbv(512, 512)
            dk = bK(0, 512) if tile == 0 else bK(512, 512)
            S.op("se", lambda s, e=e, ps=ps, dst=dst: s.activation(
                dst, ps[:, :], AF.Identity, bias=vc("bk", e)),
                reads=[pk, "vcn"], writes=dk)
            S.op("sy", lambda s, e=e, t0=t0, dst=dst: s.dma_start(
                out=bass.AP(ag2_in, e * 128 * TL + t0, [[TL, 128], [1, 512]]), in_=dst),
                reads=dk, writes=["ag2k"], sem="sd", amt=16)
            S.op("ve", lambda v, dst=dst: v.tensor_mul(bbv(1024, 512), dst, dst),
                 reads=dk, writes=bK(1024, 512))
            S.op("te", lambda t: t.matmul(PH[0:1, :], sb_ones[:, :], bbv(1024, 512),
                                          start=True, stop=True),
                 reads=bK(1024, 512) + ["ones"], writes=["PH"])
            S.op("ve", lambda v, e=e, tile=tile: v.reduce_max(
                sb_km[0:1, 16 + e * 2 + tile:17 + e * 2 + tile], PH[0:1, :], axis=AX.X),
                reads=["PH"], writes=[f"kn{e}_{tile}"])

        proj2048(wkT, evac_k)
        for e in range(NCH):
            S.op("ve", lambda v, e=e: v.tensor_max(
                sb_km[0:1, 48 + e:49 + e], sb_km[0:1, 16 + e * 2:17 + e * 2],
                sb_km[0:1, 17 + e * 2:18 + e * 2]),
                reads=[f"kn{e}_0", f"kn{e}_1"], writes=["knm"])
        S.op("ve", lambda v: v.tensor_scalar_mul(sb_km[0:1, 48:64], sb_km[0:1, 48:64], 1.02),
             reads=["knm"], writes=["knm"])
        S.op("ve", lambda v: v.tensor_copy(sb_rowb[0:1, 1024:1040], sb_km[0:1, 48:64]),
             reads=["knm"], writes=["knmb"])
        S.op("sy", lambda s: s.dma_start(
            out=bass.AP(ag2_in, 2048 * D, [[16, 1], [1, 16]]), in_=sb_rowb[0:1, 1024:1040]),
            reads=["knmb"], writes=["ag2m"], sem="sd", amt=16)

        def evac_v(e, tile, ps, pk):
            t0 = tile * 512
            dst = bbv(0, 512) if tile == 0 else bbv(512, 512)
            dk = bK(0, 512) if tile == 0 else bK(512, 512)
            S.op("se", lambda s, e=e, ps=ps, dst=dst: s.activation(
                dst, ps[:, :], AF.Identity, bias=vc("bv", e)),
                reads=[pk, "vcn"], writes=dk)
            for tc in range(4):
                S.op("te", lambda t, tc=tc, dst=dst: t.transpose(
                    PG[:, tc * 128:(tc + 1) * 128], dst[:, tc * 128:(tc + 1) * 128],
                    sb_id[:, :]),
                    reads=dk + ["ident"], writes=["PG"])
                vcol = tile * 8192 + tc * 2048 + e * 128
                S.op("ve", lambda v, tc=tc, vcol=vcol: v.tensor_copy(
                    wk(vcol, 128), PG[:, tc * 128:(tc + 1) * 128]),
                    reads=["PG"], writes=wkK(vcol, 128))
            if e == NCH - 1:
                for tc in range(4):
                    vcol = tile * 8192 + tc * 2048
                    S.op("sy", lambda s, tc=tc, t0=t0, vcol=vcol: s.dma_start(
                        out=bass.AP(ag2_in, (1024 + t0 + tc * 128) * D, [[D, 128], [1, D]]),
                        in_=wk(vcol, 2048)),
                        reads=wkK(vcol, 2048), writes=["ag2v"], sem="sd", amt=16)

        proj2048(wvT, evac_v)

        S.op("ge", lambda g: g.collective_compute(
            "AllGather", ALU.bypass, replica_groups=[list(range(NCORES))],
            ins=[ag2_in.ap()], outs=[ag2_out.ap()]),
            reads=["ag2k", "ag2v", "ag2m"], writes=["ag2o"], sem="cc", amt=1)

        S.op("ge", lambda g: g.dma_start(
            out=sb_wsl[:, 0:2048].rearrange("p (c f) -> p c f", f=128),
            in_=wrt.rearrange("(c p) f -> p c f", p=128)),
            writes=wslK(0, 2048), sem="gd", amt=16)
        for tile in range(2):
            t0 = tile * 512
            accgrp("PA", NCH,
                   lambda i, t0=t0: (lambda t, ch=i: t.matmul(
                       PA[:, :], sb_wsl[:, ch * 128:(ch + 1) * 128], xnch(ch, t0),
                       start=(ch == 0), stop=(ch == NCH - 1))),
                   lambda i: _slots("wsl", i * 128, 128) + [xnk(i)])
            S.op("ve", lambda v: v.tensor_copy(bbv(0, 512), PA[:, :]),
                 reads=["PA"], writes=bK(0, 512))
            for g4 in range(4):
                S.op("te", lambda t, g4=g4: t.transpose(
                    PG[:, g4 * 128:(g4 + 1) * 128], sb_b[:, g4 * 128:(g4 + 1) * 128],
                    sb_id[:, :]),
                    reads=bK(0, 512) + ["ident"], writes=["PG"])
            S.op("ve", lambda v: v.tensor_copy(bbv(512, 512), PG[:, 0:512]),
                 reads=["PG"], writes=bK(512, 512))
            for g4 in range(4):
                rt = sb_b[:, 512 + g4 * 128: 512 + (g4 + 1) * 128]
                S.op("ve", lambda v, rt=rt: v.reduce_max(fbv(1024, 1), rt, axis=AX.X, negate=True),
                     reads=bK(512, 512), writes=["mr0"])
                S.op("se", lambda s, rt=rt: s.activation(
                    fbv(1536, 128), rt, AF.Exp, bias=sb_f[:, 1024:1025],
                    accum_out=sb_f[:, 1025:1026]),
                    reads=["mr0"] + bK(512, 512), writes=fK(1536, 128) + ["den"])
                S.op("ve", lambda v: v.reciprocal(sb_f[:, 1026:1027], sb_f[:, 1025:1026]),
                     reads=["den"], writes=["deni"])
                S.op("ve", lambda v: v.tensor_scalar_mul(
                    fbv(1536, 128), fbv(1536, 128), sb_f[:, 1026:1027]),
                    reads=fK(1536, 128) + ["deni"], writes=fK(1536, 128))
                S.op("ve", lambda v: v.tensor_copy(fbv(1664, 128), fbv(1536, 128)),
                     reads=fK(1536, 128), writes=fK(1664, 128))
                for rnd in range(3):
                    S.op("ve", lambda v: v.reduce_max(sb_f[:, 1027:1028], fbv(1664, 128), axis=AX.X),
                         reads=fK(1664, 128), writes=["mr"])
                    S.op("ve", lambda v: v.tensor_scalar(
                        fbv(1792, 128), fbv(1664, 128), sb_f[:, 1027:1028], None, ALU.is_equal),
                        reads=fK(1664, 128) + ["mr"], writes=fK(1792, 128))
                    S.op("ve", lambda v: v.scalar_tensor_tensor(
                        fbv(1664, 128), fbv(1792, 128), NEG, fbv(1664, 128), ALU.mult, ALU.add),
                        reads=fK(1792, 128) + fK(1664, 128), writes=fK(1664, 128))
                S.op("ve", lambda v: v.reduce_max(sb_f[:, 1028:1029], fbv(1664, 128), axis=AX.X),
                     reads=fK(1664, 128), writes=["m4"])
                S.op("ve", lambda v: v.scalar_tensor_tensor(
                    fbv(1792, 128), fbv(1536, 128), sb_f[:, 1028:1029], fbv(1536, 128),
                    ALU.is_ge, ALU.mult, accum_out=sb_f[:, 1029:1030]),
                    reads=fK(1536, 128) + ["m4"], writes=fK(1792, 128) + ["ps4"])
                S.op("ve", lambda v: v.tensor_scalar_add(sb_f[:, 1029:1030], sb_f[:, 1029:1030], 1e-8),
                     reads=["ps4"], writes=["ps4"])
                S.op("ve", lambda v: v.reciprocal(sb_f[:, 1030:1031], sb_f[:, 1029:1030]),
                     reads=["ps4"], writes=["ps4i"])
                S.op("ve", lambda v, g4=g4: v.tensor_scalar_mul(
                    sb_b[:, 1024 + g4 * 128: 1024 + (g4 + 1) * 128],
                    fbv(1792, 128), sb_f[:, 1030:1031]),
                    reads=fK(1792, 128) + ["ps4i"], writes=bK(1024 + g4 * 128, 128))
            for g4 in range(4):
                S.op("te", lambda t, g4=g4: t.transpose(
                    PG[:, 512 + g4 * 128: 512 + (g4 + 1) * 128],
                    sb_b[:, 1024 + g4 * 128: 1024 + (g4 + 1) * 128], sb_id[:, :]),
                    reads=bK(1024 + g4 * 128, 128) + ["ident"], writes=["PG"])
            S.op("ve", lambda v: v.tensor_copy(bbv(1536, 512), PG[:, 512:1024]),
                 reads=["PG"], writes=bK(1536, 512))
            S.op("te", lambda t: t.matmul(PH[0:8, :], sb_b[:, 4088:4096], bbv(1536, 512),
                                          start=True, stop=True),
                 reads=["mixpb"] + bK(1536, 512), writes=["PH"])
            S.op("ve", lambda v: v.tensor_copy(sb_b[0:8, 3072:3584], PH[0:8, :]),
                 reads=["PH"], writes=bK(3072, 512))
            for kk in range(8):
                S.op("te", lambda t, kk=kk: t.matmul(
                    PE1[:, :], sb_selr[0:8, kk * 128:(kk + 1) * 128], sb_b[0:8, 3072:3584],
                    start=True, stop=True),
                    reads=["selr"] + bK(3072, 512), writes=["PE1"])
                wcol = WR0 + (kk * 2 + tile) * 512
                S.op("ve", lambda v, wcol=wcol: v.tensor_copy(wk(wcol, 512), PE1[:, :]),
                     reads=["PE1"], writes=wkK(wcol, 512))
        S.op("ge", lambda g: g.memset(wk(MX0, 8192), 0.0), writes=wkK(MX0, 8192))
        bufs = [0, 8192]
        stream_w(wga, 0, 256, 0)
        stream_w(wua, 0, 256, 4096)
        for hb in range(32):
            if hb < 31:
                nb = bufs[(hb + 1) % 2]
                stream_w(wga, (hb + 1) * 256, 256, nb)
                stream_w(wua, (hb + 1) * 256, 256, nb + 4096)
            buf = bufs[hb % 2]
            for hc in range(2):
                alpha = hb * 2 + hc
                kk, ic = alpha >> 3, alpha & 7
                for tile in range(2):
                    t0 = tile * 512
                    pga = "PA" if tile == 0 else "PC"
                    pgu = "PB" if tile == 0 else "PD"
                    psa, psu = PS[pga], PS[pgu]
                    accgrp(pga, NCH,
                           lambda i, psa=psa, buf=buf, hc=hc, t0=t0: (
                               lambda t, ch=i: t.matmul(
                                   psa[:, :],
                                   sb_wsl[:, buf + ch * 256 + hc * 128: buf + ch * 256 + hc * 128 + 128],
                                   xnch(ch, t0), start=(ch == 0), stop=(ch == NCH - 1))),
                           lambda i, buf=buf, hc=hc: _slots("wsl", buf + i * 256 + hc * 128, 128) + [xnk(i)])
                    accgrp(pgu, NCH,
                           lambda i, psu=psu, buf=buf, hc=hc, t0=t0: (
                               lambda t, ch=i: t.matmul(
                                   psu[:, :],
                                   sb_wsl[:, buf + 4096 + ch * 256 + hc * 128: buf + 4096 + ch * 256 + hc * 128 + 128],
                                   xnch(ch, t0), start=(ch == 0), stop=(ch == NCH - 1))),
                           lambda i, buf=buf, hc=hc: _slots("wsl", buf + 4096 + i * 256 + hc * 128, 128) + [xnk(i)])
                    dst = bbv(0, 512) if tile == 0 else bbv(512, 512)
                    dk = bK(0, 512) if tile == 0 else bK(512, 512)
                    S.op("se", lambda s, psa=psa, dst=dst: s.activation(dst, psa[:, :], AF.Silu),
                         reads=[pga], writes=dk)
                    bdst = bbv(1024, 512) if tile == 0 else bbv(2048, 512)
                    bkK = bK(1024, 512) if tile == 0 else bK(2048, 512)
                    S.op("ve", lambda v, psu=psu, dst=dst, bdst=bdst: v.tensor_mul(
                        bdst, dst, psu[:, :]),
                        reads=dk + [pgu], writes=bkK)
                    fdst = fbv(0, 512) if tile == 0 else fbv(512, 512)
                    fkK = fK(0, 512) if tile == 0 else fK(512, 512)
                    wcol = WR0 + (kk * 2 + tile) * 512
                    S.op("ve", lambda v, bdst=bdst, fdst=fdst, wcol=wcol: v.tensor_mul(
                        fdst, bdst, wk(wcol, 512)),
                        reads=bkK + wkK(wcol, 512), writes=fkK)
                    mcol = MX0 + (ic * 2 + tile) * 512
                    S.op("ve", lambda v, mcol=mcol, fdst=fdst: v.tensor_add(
                        wk(mcol, 512), wk(mcol, 512), fdst),
                        reads=fkK + wkK(mcol, 512), writes=wkK(mcol, 512))
        S.op("ge", lambda g: g.dma_start(
            out=sb_wsl[:, :].rearrange("p (c f) -> p c f", f=D),
            in_=wda.rearrange("(c p) f -> p c f", p=128)),
            writes=wslK(0, 16384), sem="gd", amt=16)
        for tile in range(2):
            t0 = tile * 512
            for do in range(16):
                pk = "PA" if do % 2 == 0 else "PB"
                ps = PS[pk]
                accgrp(pk, 8,
                       lambda i, do=do, ps=ps, tile=tile: (
                           lambda t, ic=i: t.matmul(
                               ps[:, :], sb_wsl[:, ic * D + do * 128: ic * D + do * 128 + 128],
                               wk(MX0 + (ic * 2 + tile) * 512, 512),
                               start=(ic == 0), stop=(ic == 7))),
                       lambda i, do=do, tile=tile: (_slots("wsl", i * D + do * 128, 128) +
                                                    wkK(MX0 + (i * 2 + tile) * 512, 512)))
                S.op("ve", lambda v, do=do, t0=t0, ps=ps: v.scalar_tensor_tensor(
                    xch(do, t0), ps[:, :], 1.0, xch(do, t0), ALU.mult, ALU.add),
                    reads=[pk, xk(do)], writes=[xk(do)])
        bufs = [0, 8192]
        stream_w(swg, 0, 256, 0)
        stream_w(swu, 0, 256, 4096)
        for i2b in range(8):
            if i2b < 7:
                nb = bufs[(i2b + 1) % 2]
                stream_w(swg, (i2b + 1) * 256, 256, nb)
                stream_w(swu, (i2b + 1) * 256, 256, nb + 4096)
            buf = bufs[i2b % 2]
            for hc in range(2):
                i2 = i2b * 2 + hc
                for tile in range(2):
                    t0 = tile * 512
                    pga = "PA" if tile == 0 else "PC"
                    pgu = "PB" if tile == 0 else "PD"
                    psa, psu = PS[pga], PS[pgu]
                    accgrp(pga, NCH,
                           lambda i, psa=psa, buf=buf, hc=hc, t0=t0: (
                               lambda t, ch=i: t.matmul(
                                   psa[:, :],
                                   sb_wsl[:, buf + ch * 256 + hc * 128: buf + ch * 256 + hc * 128 + 128],
                                   xnch(ch, t0), start=(ch == 0), stop=(ch == NCH - 1))),
                           lambda i, buf=buf, hc=hc: _slots("wsl", buf + i * 256 + hc * 128, 128) + [xnk(i)])
                    accgrp(pgu, NCH,
                           lambda i, psu=psu, buf=buf, hc=hc, t0=t0: (
                               lambda t, ch=i: t.matmul(
                                   psu[:, :],
                                   sb_wsl[:, buf + 4096 + ch * 256 + hc * 128: buf + 4096 + ch * 256 + hc * 128 + 128],
                                   xnch(ch, t0), start=(ch == 0), stop=(ch == NCH - 1))),
                           lambda i, buf=buf, hc=hc: _slots("wsl", buf + 4096 + i * 256 + hc * 128, 128) + [xnk(i)])
                    dst = bbv(0, 512) if tile == 0 else bbv(512, 512)
                    dk = bK(0, 512) if tile == 0 else bK(512, 512)
                    S.op("se", lambda s, psa=psa, dst=dst: s.activation(dst, psa[:, :], AF.Silu),
                         reads=[pga], writes=dk)
                    scol = SH0 + (i2 * 2 + tile) * 512
                    S.op("ve", lambda v, scol=scol, psu=psu, dst=dst: v.tensor_mul(
                        wk(scol, 512), dst, psu[:, :]),
                        reads=dk + [pgu], writes=wkK(scol, 512))
        bufs = [0, 8192]
        stream_w(swd, 0, 512, 0)
        for sl4 in range(4):
            if sl4 < 3:
                stream_w(swd, (sl4 + 1) * 512, 512, bufs[(sl4 + 1) % 2])
            buf = bufs[sl4 % 2]
            for esub in range(4):
                do = sl4 * 4 + esub
                for tile in range(2):
                    t0 = tile * 512
                    pk = "PA" if tile == 0 else "PB"
                    ps = PS[pk]
                    accgrp(pk, NCH,
                           lambda i, buf=buf, esub=esub, ps=ps, tile=tile: (
                               lambda t, i2=i: t.matmul(
                                   ps[:, :],
                                   sb_wsl[:, buf + i2 * 512 + esub * 128: buf + i2 * 512 + esub * 128 + 128],
                                   wk(SH0 + (i2 * 2 + tile) * 512, 512),
                                   start=(i2 == 0), stop=(i2 == NCH - 1))),
                           lambda i, buf=buf, esub=esub, tile=tile: (
                               _slots("wsl", buf + i * 512 + esub * 128, 128) +
                               wkK(SH0 + (i * 2 + tile) * 512, 512)))
                    S.op("ve", lambda v, do=do, t0=t0, ps=ps: v.scalar_tensor_tensor(
                        xch(do, t0), ps[:, :], 1.0, xch(do, t0), ALU.mult, ALU.add),
                        reads=[pk, xk(do)], writes=[xk(do)])

        if "moe" in tap:
            dump_tap("moe", lambda ch: xch(ch, 0, TL), lambda ch: [xk(ch)])

        S.op("ge", lambda g: g.dma_start(
            out=sb_rowb[0:1, 1040:1168],
            in_=bass.AP(ag2_out, 2048 * D, [[AGROWS * D, 8], [1, 16]])),
            reads=["ag2o"], writes=["kmg"], sem="gd", amt=16)
        S.op("ve", lambda v: v.tensor_copy(sb_km[0:1, 64:80], sb_rowb[0:1, 1040:1056]),
             reads=["kmg"], writes=["km2"])
        for r in range(1, 8):
            S.op("ve", lambda v, r=r: v.tensor_max(
                sb_km[0:1, 64:80], sb_km[0:1, 64:80], sb_rowb[0:1, 1040 + r * 16:1056 + r * 16]),
                reads=["kmg", "km2"], writes=["km2"])
        S.op("se", lambda s: s.activation(sb_km[0:1, 80:96], sb_km[0:1, 64:80], AF.Sqrt),
             reads=["km2"], writes=["km"])

        for hg in range(8):
            h0 = hg * 2
            S.op("sy", lambda s, h0=h0: s.dma_start(
                out=sb_b[:, 0:2048].rearrange("p (h f) -> p h f", f=TL),
                in_=bass.AP(q_scr, h0 * 128 * TL, [[TL, 128], [128 * TL, 2], [1, TL]])),
                reads=["qscr"], writes=bK(0, 2048), sem="sd", amt=16)
            for h in range(2):
                S.op("ve", lambda v, h=h: v.tensor_mul(
                    bbv(2048, 1024), sb_b[:, h * TL:(h + 1) * TL], sb_b[:, h * TL:(h + 1) * TL]),
                    reads=bK(h * TL, TL), writes=bK(2048, 1024))
                for half in range(2):
                    S.op("te", lambda t, half=half: t.matmul(
                        PH[0:1, :], sb_ones[:, :], bbv(2048 + half * 512, 512),
                        start=True, stop=True),
                        reads=bK(2048 + half * 512, 512) + ["ones"], writes=["PH"])
                    S.op("ve", lambda v, h=h, half=half: v.reduce_max(
                        sb_km[0:1, 96 + h * 2 + half:97 + h * 2 + half], PH[0:1, :], axis=AX.X),
                        reads=["PH"], writes=[f"qn{h}_{half}"])
                S.op("ve", lambda v, h=h: v.tensor_max(
                    sb_km[0:1, 104 + h:105 + h], sb_km[0:1, 96 + h * 2:97 + h * 2],
                    sb_km[0:1, 97 + h * 2:98 + h * 2]),
                    reads=[f"qn{h}_0", f"qn{h}_1"], writes=[f"qnm{h}"])
            S.op("se", lambda s: s.activation(sb_km[0:1, 108:110], sb_km[0:1, 104:106], AF.Sqrt),
                 reads=["qnm0", "qnm1"], writes=["qm"])
            S.op("ve", lambda v, h0=h0: v.tensor_mul(
                sb_km[0:1, 112:114], sb_km[0:1, 108:110], sb_km[0:1, 80 + h0:82 + h0]),
                reads=["qm", "km"], writes=["Bh"])
            S.op("ve", lambda v: v.tensor_scalar_mul(sb_km[0:1, 116:118], sb_km[0:1, 112:114], -1.0),
                 reads=["Bh"], writes=["nB"])

            for qt in ("B", "A"):
                nun = QB_UNITS if qt == "B" else QA_UNITS
                qoff = 0 if qt == "A" else 512
                for h in range(2):
                    S.op("se", lambda s, h=h, h0=h0, qoff=qoff: s.activation(
                        sb_row[0:1, 0:512], sb_iotag[0:1, qoff:qoff + 512], AF.Identity,
                        scale=-SLOPES[h0 + h], bias=sb_km[0:1, 116 + h:117 + h]),
                        reads=["iotag", "nB"], writes=["rw0"])
                    S.op("ve", lambda v, h=h: v.tensor_copy(
                        sb_rowb[0:1, h * 512:(h + 1) * 512], sb_row[0:1, 0:512]),
                        reads=["rw0"], writes=[f"rb{h}"])
                units = [(kvb, False) for kvb in range(nun)] + [(None, True)]
                NU = len(units)
                its = []
                for ui, (kvb, diag) in enumerate(units):
                    kcol = AK0 + (ui % 2) * 1024
                    vcol = AV0 + (ui % 2) * 1024
                    for h in range(2):
                        for j in range(4):
                            if diag:
                                jc = (h0 + h) * JCOLS + (QB_UNITS + QA_UNITS + (0 if qt == "B" else 1)) * 4 + j
                            else:
                                u = kvb if qt == "B" else QB_UNITS + kvb
                                jc = (h0 + h) * JCOLS + u * 4 + j
                            its.append(dict(ui=ui, kvb=kvb, diag=diag, h=h, j=j,
                                            kcol=kcol, vcol=vcol, jc=jc,
                                            st=(ui == 0 and j == 0),
                                            sp=(diag and j == 3)))

                def unit_dma(ui):
                    kvb, diag = units[ui]
                    kcol = AK0 + (ui % 2) * 1024
                    vcol = AV0 + (ui % 2) * 1024
                    if diag:
                        tloc = 0 if qt == "A" else 512
                        src_, skey, base = ag2_in, ["ag2k", "ag2v"], 0
                    else:
                        rr = kvb if kvb < 8 else 15 - kvb
                        tloc = 0 if kvb < 8 else 512
                        src_, skey, base = ag2_out, ["ag2o"], rr * AGROWS * D
                    S.op("sy", lambda s, kcol=kcol, base=base, tloc=tloc, h0=h0, src_=src_: s.dma_start(
                        out=wk(kcol, 1024).rearrange("p (h f) -> p h f", f=512),
                        in_=bass.AP(src_, base + h0 * 128 * TL + tloc,
                                    [[TL, 128], [128 * TL, 2], [1, 512]])),
                        reads=skey, writes=wkK(kcol, 1024), sem="sd", amt=16)
                    for hh in range(2):
                        S.op("sy", lambda s, vcol=vcol, base=base, tloc=tloc, h0=h0, src_=src_, hh=hh: s.dma_start(
                            out=wk(vcol + hh * 512, 512).rearrange("p (j f) -> p j f", f=128),
                            in_=bass.AP(src_, base + (1024 + tloc) * D + (h0 + hh) * 128,
                                        [[D, 128], [128 * D, 4], [1, 128]])),
                            reads=skey, writes=wkK(vcol + hh * 512, 512), sem="sd", amt=16)

                def emit_RS(idx):
                    it = its[idx]
                    spk = "PA" if idx % 2 == 0 else "PB"
                    sps = PS[spk]
                    h, j, kcol = it["h"], it["j"], it["kcol"]
                    S.op("te", lambda t, h=h, sps=sps: t.matmul(
                        sps[:, :], sb_onesr[:, :], sb_rowb[0:1, h * 512:(h + 1) * 512],
                        start=True, stop=False),
                        reads=[f"rb{h}", "onesr"], writes=[spk])
                    S.op("te", lambda t, h=h, j=j, sps=sps, kcol=kcol, qoff=qoff: t.matmul(
                        sps[:, :], sb_wk[:, kcol + h * 512 + j * 128: kcol + h * 512 + j * 128 + 128],
                        sb_b[:, h * TL + qoff: h * TL + qoff + 512],
                        start=False, stop=True),
                        reads=wkK(kcol + h * 512 + j * 128, 128) + bK(h * TL + qoff, 512),
                        writes=[spk])

                def emit_exp(idx):
                    it = its[idx]
                    spk = "PA" if idx % 2 == 0 else "PB"
                    sps = PS[spk]
                    pcol = AP0 + (idx % 8) * 512
                    if it["diag"]:
                        tcol = 1024 + (idx % 4) * 512
                        j = it["j"]
                        S.op("ve", lambda v, sps=sps, tcol=tcol, j=j: v.tensor_add(
                            fbv(tcol, 512), sps[:, :], sb_mext[:, 384 - j * 128:896 - j * 128]),
                            reads=[spk, "mext"], writes=fK(tcol, 512))
                        S.op("se", lambda s, tcol=tcol, pcol=pcol, jc=it["jc"]: s.activation(
                            wk(pcol, 512), fbv(tcol, 512), AF.Exp, bias=sb_jtab[:, jc:jc + 1]),
                            reads=fK(tcol, 512) + ["jtab"], writes=wkK(pcol, 512))
                    else:
                        S.op("se", lambda s, sps=sps, pcol=pcol, jc=it["jc"]: s.activation(
                            wk(pcol, 512), sps[:, :], AF.Exp, bias=sb_jtab[:, jc:jc + 1]),
                            reads=[spk, "jtab"], writes=wkK(pcol, 512))

                def emit_lav(idx):
                    it = its[idx]
                    h, j = it["h"], it["j"]
                    pcol = AP0 + (idx % 8) * 512
                    lbank = "PE2" if h == 0 else "PE1"
                    lps = PS[lbank]
                    obank = "PC" if h == 0 else "PD"
                    ops_ = PS[obank]
                    st, sp = it["st"], it["sp"]
                    S.op("te", lambda t, pcol=pcol, lps=lps, st=st, sp=sp: t.matmul(
                        lps[0:1, :], sb_ones[:, :], wk(pcol, 512), start=st, stop=sp),
                        reads=wkK(pcol, 512) + ["ones"],
                        writes=[lbank] if (st or sp) else [])
                    vcol = it["vcol"]
                    S.op("te", lambda t, pcol=pcol, vcol=vcol, h=h, j=j, ops_=ops_, st=st, sp=sp: t.matmul(
                        ops_[:, :], sb_wk[:, vcol + h * 512 + j * 128: vcol + h * 512 + j * 128 + 128],
                        wk(pcol, 512), start=st, stop=sp),
                        reads=wkK(vcol + h * 512 + j * 128, 128) + wkK(pcol, 512),
                        writes=[obank] if (st or sp) else [])

                def phase1(u):
                    if u + 1 < NU:
                        unit_dma(u + 1)
                    b0 = u * 8
                    emit_RS(b0)
                    emit_RS(b0 + 1)
                    for i in range(2, 8):
                        emit_exp(b0 + i - 2)
                        emit_RS(b0 + i)
                    emit_exp(b0 + 6)
                    emit_exp(b0 + 7)

                def phase2(u):
                    b0 = u * 8
                    for i in range(8):
                        emit_lav(b0 + i)

                unit_dma(0)
                phase1(0)
                for u in range(1, NU):
                    phase2(u - 1)
                    phase1(u)
                phase2(NU - 1)
                for h in range(2):
                    obank = "PC" if h == 0 else "PD"
                    ops_ = PS[obank]
                    lbank = "PE2" if h == 0 else "PE1"
                    lps = PS[lbank]
                    S.op("ve", lambda v, lps=lps: v.reciprocal(
                        sb_f[0:1, 1024:1536], lps[0:1, :]),
                        reads=[lbank], writes=["linv"] + fK(1024, 512))
                    S.op("te", lambda t: t.matmul(
                        PA[:, :], sb_onesrf[0:1, :], sb_f[0:1, 1024:1536],
                        start=True, stop=True),
                        reads=["onesrf", "linv"], writes=["PA"])
                    S.op("ve", lambda v: v.tensor_copy(fbv(1536, 512), PA[:, :]),
                         reads=["PA"], writes=fK(1536, 512))
                    S.op("ve", lambda v, h=h, h0=h0, qoff=qoff, ops_=ops_: v.tensor_mul(
                        wk(AO0 + (h0 + h) * TL + qoff, 512), fbv(1536, 512), ops_[:, :]),
                        reads=fK(1536, 512) + [obank],
                        writes=wkK(AO0 + (h0 + h) * TL + qoff, 512))

        def evac_wo(e, tile, ps, pk):
            t0 = tile * 512
            S.op("ve", lambda v, e=e, t0=t0, ps=ps: v.scalar_tensor_tensor(
                xch(e, t0), ps[:, :], vc("bo", e), xch(e, t0), ALU.add, ALU.add),
                reads=[pk, "vcn", xk(e)], writes=[xk(e)])

        proj2048(woT, evac_wo,
                 rhs=lambda ch, c0: wk(AO0 + ch * TL + c0, 512),
                 rhs_keys=lambda ch: wkK(AO0 + ch * TL, TL))

        if "attn" in tap:
            dump_tap("attn", lambda ch: wk(AO0 + ch * TL, TL),
                     lambda ch: wkK(AO0 + ch * TL, TL))

        outks = []
        for ch in range(NCH):
            c0 = 0 if ch % 2 == 0 else 1024
            S.op("ve", lambda v, ch=ch, c0=c0: v.tensor_copy(fbv(c0, 1024), xch(ch, 0, TL)),
                 reads=[xk(ch)], writes=fK(c0, 1024))
            S.op("sy", lambda s, ch=ch, c0=c0: s.dma_start(
                out=outT[ch * 128:(ch + 1) * 128, :], in_=fbv(c0, 1024)),
                reads=fK(c0, 1024), writes=[f"out{ch}"], sem="sd", amt=16)
            outks.append(f"out{ch}")
        S.wait_keys("sy", outks)
        S.wait_keys("ge", outks)

        @block.tensor
        def _(h):
            for f in S.fns["te"]:
                f(h)

        @block.vector
        def _(h):
            for f in S.fns["ve"]:
                f(h)

        @block.scalar
        def _(h):
            for f in S.fns["se"]:
                f(h)

        @block.gpsimd
        def _(h):
            for f in S.fns["ge"]:
                f(h)

        @block.sync
        def _(h):
            for f in S.fns["sy"]:
                f(h)

    return nc


def _host_prep(inputs):
    f32 = np.float32
    bf16 = ml_dtypes.bfloat16
    A = {k: np.asarray(v) for k, v in inputs.items()}
    x = np.ascontiguousarray(A["L0_attn_u"][0].T.astype(f32))

    def tr(a):
        return np.ascontiguousarray(np.asarray(a).T.astype(f32))

    shared = dict(
        gwT=tr(A["L0_attn_gate_w"]), w1T=tr(A["L0_ffn_W1"]), w3T=tr(A["L0_ffn_W3"]),
        w2T=tr(A["L0_ffn_W2"]), wqT=tr(A["L1_attn_Wq"]), wkT=tr(A["L1_attn_Wk"]),
        wvT=tr(A["L1_attn_Wv"]), woT=tr(A["L1_attn_Wo"]),
        wrt=np.ascontiguousarray(A["L1_ffn_w_router"].astype(f32)),
        wga=np.ascontiguousarray(A["L1_ffn_w_gate"].astype(f32)),
        wua=np.ascontiguousarray(A["L1_ffn_w_up"].astype(f32)),
        wda=np.ascontiguousarray(A["L1_ffn_w_down"].astype(f32)),
        swg=np.ascontiguousarray(A["L1_ffn_shared_w_gate"].astype(f32)),
        swu=np.ascontiguousarray(A["L1_ffn_shared_w_up"].astype(f32)),
        swd=np.ascontiguousarray(A["L1_ffn_shared_w_down"].astype(f32)),
        dlru=np.ascontiguousarray(A["L0_attn_d_lr_u"].astype(f32)),
        dlrv=np.ascontiguousarray(A["L0_attn_d_lr_v"].astype(f32)),
    )
    vcons = np.zeros((128, 164), f32)

    def pack(col, vec):
        vcons[:, col:col + 16] = np.asarray(vec, f32).reshape(16, 128).T

    pack(0, A["L0_attn_a"]); pack(16, A["L0_attn_b"]); pack(32, A["L0_attn_c"])
    pack(48, A["L0_attn_d"]); pack(64, A["L0_attn_gate_b"])
    pack(80, A["L1_attn_bq"]); pack(96, A["L1_attn_bk"]); pack(112, A["L1_attn_bv"])
    pack(128, A["L1_attn_bo"])
    vcons[:, 144] = np.asarray(A["L1_ffn_expert_scale"], f32)
    vcons[:, 146:162] = np.asarray(A["L1_ffn_amplitudes"], f32).reshape(128, 16)
    shared["vcons"] = vcons

    shared["ident"] = np.eye(128, dtype=bf16)
    shared["h128"] = _hadamard(128).astype(bf16)
    mext = np.full((128, 896), NEG, f32)
    for p in range(128):
        mext[p, 384 + p:] = 0.0
    shared["mext"] = mext.astype(bf16)
    selrow = np.zeros((8, 1024), f32)
    for k in range(8):
        selrow[k, k * 128:(k + 1) * 128] = 1.0
    shared["selrow"] = selrow.astype(bf16)

    in_maps = []
    for c in range(NCORES):
        blkA, blkB = c, 15 - c
        cols = np.r_[blkA * 512:(blkA + 1) * 512, blkB * 512:(blkB + 1) * 512]
        m = dict(shared)
        m["xT"] = np.ascontiguousarray(x[:, cols])
        m["iotag"] = cols.astype(bf16).reshape(1, TL)
        sel = np.zeros((128, 32), f32)
        sel[:, blkA] = 1.0
        sel[:, 16 + blkB] = 1.0
        m["selAB"] = sel
        jt = np.full((128, HEADS * JCOLS), NEG, f32)
        p = np.arange(128, dtype=f32)
        for h in range(HEADS):
            s = SLOPES[h]
            base = h * JCOLS
            for kvb in range(QB_UNITS):
                if kvb <= 14 - c:
                    for j in range(4):
                        jt[:, base + kvb * 4 + j] = s * (512 * kvb + 128 * j + p)
            for kvb in range(QA_UNITS):
                if kvb <= c - 1:
                    for j in range(4):
                        jt[:, base + (QB_UNITS + kvb) * 4 + j] = s * (512 * kvb + 128 * j + p)
            for j in range(4):
                jt[:, base + (QB_UNITS + QA_UNITS) * 4 + j] = s * (512 * blkB + 128 * j + p)
                jt[:, base + (QB_UNITS + QA_UNITS + 1) * 4 + j] = s * (512 * blkA + 128 * j + p)
        m["jtab"] = jt
        in_maps.append(m)
    return in_maps


def kernel(_tap=(), _strict_dma=True, **inputs):
    in_maps = _host_prep(inputs)
    nc = build(tap=_tap, strict_dma=_strict_dma)
    res = run_bass_kernel_spmd(nc, in_maps, core_ids=list(range(NCORES)))
    out = np.zeros((D, T), np.float32)
    for c in range(NCORES):
        r = res.results[c]
        blkA, blkB = c, 15 - c
        out[:, blkA * 512:(blkA + 1) * 512] = r["outT"][:, 0:512]
        out[:, blkB * 512:(blkB + 1) * 512] = r["outT"][:, 512:1024]
    final = np.ascontiguousarray(out.T)[None, :, :]
    if _tap:
        return final, [res.results[c] for c in range(NCORES)]
    return final
